# version 03
# baseline (speedup 1.0000x reference)
import numpy as np
import ml_dtypes
from contextlib import ExitStack, nullcontext

import concourse.bass as bass
import concourse.mybir as mybir
from concourse.bass_utils import run_bass_kernel_spmd
from concourse.aot_env import allow_ici_all_to_all

F32 = mybir.dt.float32
BF16 = mybir.dt.bfloat16
AF = mybir.ActivationFunctionType
ALU = mybir.AluOpType
AX = mybir.AxisListType
BF16NP = ml_dtypes.bfloat16

NCORE = 8
B, S, D = 4, 4096, 2048
SC = S // NCORE
TOK = B * SC
NT = 4
NCH = D // 128
EPS_LN = 1e-5
EPS_RMS = 1e-6
CAP = 15.0
H = 8
DQK = 128
DV = 256
FF_IN = 8192
DFF = 5632
NEXP = 8
RLORA = 16
MOE_SCALING = 2.0
FF1 = 8192


class Ctr:
    def __init__(self, handle):
        self.h = handle
        self.n = 0

    def inc(self, inst, k=1):
        inst.then_inc(self.h, k)
        self.n += k
        return self.n


def fmload(dram_ap, cols):
    return dram_ap.rearrange("(c p) m -> p c m", p=128)


def sb3(sb_ap, cols):
    return sb_ap.rearrange("p (c m) -> p c m", m=cols)


def mk_steps(block, serialize_ctrs=()):
    def step(deco, ctr, waits, emit):
        @deco
        def _(eng):
            for sem, val in waits:
                if val:
                    eng.wait_ge(sem, val)
            inst = emit(eng)
            ctr.inc(inst)
        return ctr.n

    def stepseq(deco, ctr, waits, fns):
        @deco
        def _(eng):
            for sem, val in waits:
                if val:
                    eng.wait_ge(sem, val)
            inst = None
            for i, fn in enumerate(fns):
                if i > 0:
                    ctr.inc(inst)
                    eng.wait_ge(ctr.h, ctr.n)
                inst = fn(eng)
            ctr.inc(inst)
        return ctr.n

    return step, stepseq


def build_l1(fz=None):
    pre = "" if fz is None else "z1_"
    if fz is None:
        nc = bass.Bass("TRN2", target_bir_lowering=False,
                       detect_race_conditions=False)
        x = nc.dram_tensor("x_fm", [D, TOK], F32, kind="ExternalInput").ap()
        o_xn = nc.dram_tensor("xn_bf", [D, TOK], BF16, kind="ExternalOutput").ap()
        o_hin = nc.dram_tensor("hin_bf", [D, TOK], BF16, kind="ExternalOutput").ap()
        o_q = nc.dram_tensor("q_fm", [1024, TOK], BF16, kind="ExternalOutput").ap()
        o_k = nc.dram_tensor("k_fm", [1024, TOK], BF16, kind="ExternalOutput").ap()
        o_v = nc.dram_tensor("v_tm", [TOK, D], BF16, kind="ExternalOutput").ap()
        o_g = nc.dram_tensor("gates", [TOK, 16], F32, kind="ExternalOutput").ap()
    else:
        nc = fz.nc
        x = fz.dram["x_fm"]
        o_xn = fz.dram["xn_bf"]
        o_hin = fz.dram["hin_bf"]
        o_q = o_k = o_v = o_g = None
    wq = nc.dram_tensor("wq", [D, 1024], F32, kind="ExternalInput").ap()
    wk = nc.dram_tensor("wk", [D, 1024], F32, kind="ExternalInput").ap()
    wv = nc.dram_tensor("wv", [D, D], F32, kind="ExternalInput").ap()
    wg16 = nc.dram_tensor("wg16", [D, 16], F32, kind="ExternalInput").ap()
    n1w = nc.dram_tensor("n1w", [128, NCH], F32, kind="ExternalInput").ap()

    c_ones_col = nc.inline_tensor(np.ones((128, 1), np.float32),
                                  pre + "c_ones_col").ap()
    c_ones_row = nc.inline_tensor(np.ones((1, 128), np.float32),
                                  pre + "c_ones_row").ap()

    with ExitStack() as ctx:
        def sb(name, shape, dtype=F32):
            return ctx.enter_context(nc.sbuf_tensor(pre + name, shape, dtype))

        def ps(name, shape, dtype=F32):
            return ctx.enter_context(nc.psum_tensor(pre + name, shape, dtype))

        s_x = sb("s_x", [128, NCH * 512])
        s_xsq = sb("s_xsq", [128, 8 * 512])
        s_xn = sb("s_xn", [128, NCH * 512], BF16)
        s_hin = sb("s_hin", [128, NCH * 512], BF16)
        s_n1 = sb("s_n1", [128, NCH])
        s_wv = sb("s_wv", [128, NCH * D], BF16)
        s_wqk = sb("s_wqk", [128, NCH * 1024], BF16)
        s_wg = sb("s_wg", [128, NCH * 16], BF16)
        s_row = sb("s_row", [1, 6 * 512])
        s_ab = sb("s_ab", [128, 2 * 512])
        s_qk = sb("s_qk", [128, 8 * 512], BF16)
        s_v = sb("s_v", [128, 4 * 512], BF16)
        s_g = sb("s_g", [128, 4 * 16])
        s_oc = sb("s_oc", [128, 1])
        s_or = sb("s_or", [1, 128])

        ps_s = ps("ps_s", [1, 512])
        ps_q = ps("ps_q", [1, 512])
        ps_a = ps("ps_a", [128, 512])
        ps_b = ps("ps_b", [128, 512])
        ps_p = [ps("ps_p0", [128, 512]), ps("ps_p1", [128, 512])]
        ps_g = ps("ps_g", [128, 16])
        ps_g2 = ps("ps_g2", [128, 512])

        if fz is None:
            sW = Ctr(ctx.enter_context(nc.semaphore("sW")))
            sX = Ctr(ctx.enter_context(nc.semaphore("sX")))
            sPE = Ctr(ctx.enter_context(nc.semaphore("sPE")))
            sDV = Ctr(ctx.enter_context(nc.semaphore("sDV")))
            sAC = Ctr(ctx.enter_context(nc.semaphore("sAC")))
            sST = Ctr(ctx.enter_context(nc.semaphore("sST")))
        else:
            sW, sX, sPE, sDV, sAC, sST = (fz.sW, fz.sX, fz.sPE, fz.sDV,
                                          fz.sAC, fz.sST)

        with (nc.Block() if fz is None else nullcontext(fz.block)) as block:
            @block.gpsimd
            def _(g):
                g.dma_start(out=sb3(s_wv[:], D), in_=fmload(wv, D)).then_inc(sW.h, 16)
                g.dma_start(out=sb3(s_wg[:], 16), in_=fmload(wg16, 16)).then_inc(sW.h, 16)
                sW.n += 32

            @block.sync
            def _(s):
                s.dma_start(out=s_oc[:], in_=c_ones_col).then_inc(sX.h, 16)
                s.dma_start(out=s_or[:], in_=c_ones_row).then_inc(sX.h, 16)
                s.dma_start(out=s_n1[:], in_=n1w).then_inc(sX.h, 16)
                sX.n += 48

            evac_done = {}
            stats_done = [None] * NT
            store_done = {}
            xn_done = [0] * NT
            sums_hist = [0] * NT
            proj_hist = [None]

            for t in range(NT):
                @block.sync
                def _(s, t=t):
                    if t > 0:
                        s.wait_ge(sDV.h, xn_done[t - 1])
                    s.dma_start(
                        out=sb3(s_x[:], 512), in_=fmload(x[:, t * 512:(t + 1) * 512], 512)
                    ).then_inc(sX.h, 16)
                    sX.n += 16

                xload = sX.n

                for half in range(2):
                    @block.scalar
                    def _(sc, t=t, xload=xload, half=half):
                        if half == 0:
                            sc.wait_ge(sX.h, xload)
                            if t > 0:
                                sc.wait_ge(sPE.h, sums_hist[t - 1])
                        else:
                            sc.wait_ge(sPE.h, half_done[0])
                        for c in range(8):
                            inst = sc.activation(
                                s_xsq[:, c * 512:(c + 1) * 512],
                                s_x[:, (half * 8 + c) * 512:(half * 8 + c + 1) * 512],
                                AF.Square)
                        sAC.inc(inst)

                    sq_done = sAC.n

                    @block.tensor
                    def _(te, t=t, sq_done=sq_done, half=half):
                        te.wait_ge(sAC.h, sq_done)
                        if half == 0 and stats_done[t - 1] is not None:
                            te.wait_ge(sDV.h, stats_done[t - 1])
                        for c in range(8):
                            cc = half * 8 + c
                            te.matmul(ps_s[:], s_oc[:],
                                      s_x[:, cc * 512:(cc + 1) * 512],
                                      start=(cc == 0), stop=(cc == NCH - 1))
                            inst = te.matmul(
                                ps_q[:], s_oc[:], s_xsq[:, c * 512:(c + 1) * 512],
                                start=(cc == 0), stop=(cc == NCH - 1))
                        sPE.inc(inst)

                    if half == 0:
                        half_done = [sPE.n]

                sums_done = sPE.n
                sums_hist[t] = sPE.n

                @block.vector
                def _(v, t=t, sums_done=sums_done):
                    v.wait_ge(sPE.h, sums_done)
                    v.tensor_scalar_mul(s_row[0:1, 0:512], ps_s[:], 1.0 / D)
                    inst = v.tensor_scalar_mul(s_row[0:1, 512:1024], ps_q[:],
                                               1.0 / D)
                    sDV.inc(inst)
                    v.wait_ge(sDV.h, sDV.n)
                    inst = v.tensor_mul(s_row[0:1, 1024:1536], s_row[0:1, 0:512],
                                        s_row[0:1, 0:512])
                    sDV.inc(inst)
                    v.wait_ge(sDV.h, sDV.n)
                    inst = v.tensor_sub(s_row[0:1, 2048:2560],
                                        s_row[0:1, 512:1024],
                                        s_row[0:1, 1024:1536])
                    sDV.inc(inst)
                    v.wait_ge(sDV.h, sDV.n)
                    inst = v.tensor_scalar_add(s_row[0:1, 512:1024],
                                               s_row[0:1, 2048:2560], EPS_LN)
                    sDV.inc(inst)

                stats_done[t] = sDV.n
                var_done = sDV.n

                @block.scalar
                def _(sc, var_done=var_done):
                    sc.wait_ge(sDV.h, var_done)
                    inst = sc.activation(s_row[0:1, 512:1024],
                                         s_row[0:1, 512:1024], AF.Sqrt)
                    sAC.inc(inst)

                sqrt_done = sAC.n

                @block.vector
                def _(v, sqrt_done=sqrt_done):
                    v.wait_ge(sAC.h, sqrt_done)
                    inst = v.reciprocal(s_row[0:1, 1536:2048],
                                        s_row[0:1, 512:1024])
                    sDV.inc(inst)
                    v.wait_ge(sDV.h, sDV.n)
                    inst = v.scalar_tensor_tensor(
                        s_row[0:1, 0:512], s_row[0:1, 0:512], -1.0,
                        s_row[0:1, 1536:2048], ALU.mult, ALU.mult)
                    sDV.inc(inst)
                    v.wait_ge(sDV.h, sDV.n)
                    inst = v.tensor_mul(s_row[0:1, 2560:3072],
                                        s_row[0:1, 1536:2048],
                                        s_row[0:1, 1536:2048])
                    sDV.inc(inst)
                    v.wait_ge(sDV.h, sDV.n)
                    inst = v.scalar_tensor_tensor(
                        s_row[0:1, 2560:3072], s_row[0:1, 2560:3072], 1.0,
                        s_row[0:1, 2048:2560], ALU.mult, ALU.mult)
                    sDV.inc(inst)
                    v.wait_ge(sDV.h, sDV.n)
                    inst = v.tensor_scalar_add(s_row[0:1, 2560:3072],
                                               s_row[0:1, 2560:3072], EPS_RMS)
                    sDV.inc(inst)

                ab_row = sDV.n

                @block.scalar
                def _(sc, n=ab_row):
                    sc.wait_ge(sDV.h, n)
                    inst = sc.activation(s_row[0:1, 2560:3072],
                                         s_row[0:1, 2560:3072], AF.Sqrt)
                    sAC.inc(inst)

                r2s = sAC.n

                @block.vector
                def _(v, n=r2s):
                    v.wait_ge(sAC.h, n)
                    inst = v.reciprocal(s_row[0:1, 2560:3072],
                                        s_row[0:1, 2560:3072])
                    sDV.inc(inst)

                ab_row = sDV.n

                @block.tensor
                def _(te, ab_row=ab_row, t=t):
                    te.wait_ge(sDV.h, ab_row)
                    if t > 0:
                        te.wait_ge(sDV.h, xn_done[t - 1])
                    te.matmul(ps_a[:], s_or[:], s_row[0:1, 1536:2048],
                              start=True, stop=True)
                    inst = te.matmul(ps_b[:], s_or[:], s_row[0:1, 0:512],
                                     start=True, stop=True)
                    sPE.inc(inst)

                bcast_done = sPE.n

                @block.tensor
                def _(te, t=t):
                    if t > 0:
                        te.wait_ge(sDV.h, xn_done[t - 1])
                    inst = te.matmul(ps_g2[:], s_or[:], s_row[0:1, 2560:3072],
                                     start=True, stop=True)
                    sPE.inc(inst)

                bc2_done = sPE.n

                @block.vector
                def _(v, bcast_done=bcast_done, bc2_done=bc2_done, t=t):
                    v.wait_ge(sPE.h, bc2_done)
                    v.wait_ge(sPE.h, bcast_done)
                    if "xn" in store_done:
                        v.wait_ge(sST.h, store_done["xn"])
                    if "hin" in store_done:
                        v.wait_ge(sST.h, store_done["hin"])
                    v.tensor_copy(s_ab[:, 0:512], ps_a[:])
                    inst = v.tensor_copy(s_ab[:, 512:1024], ps_b[:])
                    sDV.inc(inst)
                    v.wait_ge(sDV.h, sDV.n)
                    for c in range(NCH):
                        inst = v.tensor_mul(s_x[:, c * 512:(c + 1) * 512],
                                            s_x[:, c * 512:(c + 1) * 512],
                                            s_ab[:, 0:512])
                        sDV.inc(inst)
                        v.wait_ge(sDV.h, sDV.n)
                        inst = v.tensor_add(s_x[:, c * 512:(c + 1) * 512],
                                            s_x[:, c * 512:(c + 1) * 512],
                                            s_ab[:, 512:1024])
                        sDV.inc(inst)
                        v.wait_ge(sDV.h, sDV.n)
                        inst = v.tensor_copy(s_xn[:, c * 512:(c + 1) * 512],
                                             s_x[:, c * 512:(c + 1) * 512])
                        sDV.inc(inst)
                        v.wait_ge(sDV.h, sDV.n)
                        inst = v.scalar_tensor_tensor(
                            s_hin[:, c * 512:(c + 1) * 512],
                            s_x[:, c * 512:(c + 1) * 512],
                            s_n1[:, c:c + 1], ps_g2[:], ALU.mult, ALU.mult)
                        sDV.inc(inst)

                xn_done[t] = sDV.n

                @block.sync
                def _(s, t=t, n=sDV.n):
                    s.wait_ge(sDV.h, n)
                    s.dma_start(
                        out=fmload(o_xn[:, t * 512:(t + 1) * 512], 512),
                        in_=sb3(s_xn[:], 512)).then_inc(sST.h, 16)
                    sST.n += 16

                store_done["xn"] = sST.n

                @block.sync
                def _(s, t=t, n=xn_done[t]):
                    s.wait_ge(sDV.h, n)
                    s.dma_start(
                        out=fmload(o_hin[:, t * 512:(t + 1) * 512], 512),
                        in_=sb3(s_hin[:], 512)).then_inc(sST.h, 16)
                    sST.n += 16

                store_done["hin"] = sST.n

                def mm_proj(te, wsb, wcols, m, pidx):
                    pd = ps_p[pidx % 2]
                    if pidx % 2 in evac_done:
                        sem, cnt = evac_done[pidx % 2]
                        te.wait_ge(sem, cnt)
                    inst = None
                    for c in range(NCH):
                        inst = te.matmul(
                            pd[:],
                            wsb[:, c * wcols + m * 128: c * wcols + (m + 1) * 128],
                            s_hin[:, c * 512:(c + 1) * 512],
                            start=(c == 0), stop=(c == NCH - 1))
                    return inst

                pidx = [0]
                for which, wdram, odram in (("q", wq, o_q), ("k", wk, o_k)):
                    @block.gpsimd
                    def _(g, wdram=wdram):
                        if proj_hist[0] is not None:
                            g.wait_ge(sPE.h, proj_hist[0])
                        g.dma_start(out=sb3(s_wqk[:], 1024), in_=fmload(wdram, 1024)
                                    ).then_inc(sW.h, 16)
                        sW.n += 16

                    wload = sW.n
                    for m in range(8):
                        @block.tensor
                        def _(te, m=m, wload=wload):
                            if m == 0:
                                te.wait_ge(sW.h, wload)
                                te.wait_ge(sDV.h, xn_done[t])
                            inst = mm_proj(te, s_wqk, 1024, m, pidx[0])
                            sPE.inc(inst)

                        mm_done = sPE.n

                        @block.scalar
                        def _(sc, m=m, mm_done=mm_done):
                            sc.wait_ge(sPE.h, mm_done)
                            if m == 0 and "qk" in store_done:
                                sc.wait_ge(sST.h, store_done["qk"])
                            inst = sc.copy(s_qk[:, m * 512:(m + 1) * 512],
                                           ps_p[pidx[0] % 2][:])
                            sAC.inc(inst)

                        evac_done[pidx[0] % 2] = (sAC.h, sAC.n)
                        pidx[0] += 1

                    proj_hist[0] = sPE.n

                    @block.sync
                    def _(s, odram=odram, n=sAC.n, t=t, which=which):
                        s.wait_ge(sAC.h, n)
                        if fz is None:
                            s.dma_start(
                                out=fmload(odram[:, t * 512:(t + 1) * 512], 512),
                                in_=sb3(s_qk[:], 512)).then_inc(sST.h, 16)
                            sST.n += 16
                        else:
                            off = 0 if which == "q" else 512
                            for hg in range(2):
                                dd = 2 * t + hg
                                s.dma_start(
                                    out=fmload(fz.a2a1_in[
                                        dd * 1536 + off:dd * 1536 + off + 512,
                                        0:512], 512),
                                    in_=sb3(s_qk[:, hg * 2048:(hg + 1) * 2048],
                                            512)).then_inc(sST.h, 16)
                                sST.n += 16

                    store_done["qk"] = sST.n

                for tc in range(4):
                    for gi in range(4):
                        @block.tensor
                        def _(te, tc=tc, gi=gi):
                            pd = ps_p[pidx[0] % 2]
                            sem, cnt = evac_done[pidx[0] % 2]
                            te.wait_ge(sem, cnt)
                            for c in range(NCH):
                                inst = te.matmul(
                                    pd[:],
                                    s_hin[:, c * 512 + tc * 128: c * 512 + (tc + 1) * 128],
                                    s_wv[:, c * D + gi * 512: c * D + (gi + 1) * 512],
                                    start=(c == 0), stop=(c == NCH - 1))
                            sPE.inc(inst)

                        mm_done = sPE.n

                        @block.vector
                        def _(v, gi=gi, mm_done=mm_done):
                            v.wait_ge(sPE.h, mm_done)
                            if gi == 0 and "v" in store_done:
                                v.wait_ge(sST.h, store_done["v"])
                            inst = v.tensor_copy(s_v[:, gi * 512:(gi + 1) * 512],
                                                 ps_p[pidx[0] % 2][:])
                            sDV.inc(inst)

                        evac_done[pidx[0] % 2] = (sDV.h, sDV.n)
                        pidx[0] += 1

                    @block.sync
                    def _(s, tc=tc, n=sDV.n, t=t):
                        s.wait_ge(sDV.h, n)
                        if fz is None:
                            s.dma_start(
                                out=o_v[t * 512 + tc * 128:
                                        t * 512 + (tc + 1) * 128, :],
                                in_=s_v[:]).then_inc(sST.h, 16)
                            sST.n += 16
                        else:
                            for hg in range(2):
                                dd = 2 * t + hg
                                r0 = dd * 1536 + 1024 + tc * 128
                                s.dma_start(
                                    out=fz.a2a1_in[r0:r0 + 128, 0:1024],
                                    in_=s_v[:, hg * 1024:(hg + 1) * 1024]
                                ).then_inc(sST.h, 16)
                                sST.n += 16

                    store_done["v"] = sST.n

                gate_ev = [None]
                for tc in range(4):
                    @block.tensor
                    def _(te, tc=tc, t=t):
                        if gate_ev[0] is not None:
                            te.wait_ge(sDV.h, gate_ev[0])
                        inst = None
                        for c in range(NCH):
                            inst = te.matmul(
                                ps_g[:],
                                s_hin[:, c * 512 + tc * 128: c * 512 + (tc + 1) * 128],
                                s_wg[:, c * 16:(c + 1) * 16],
                                start=(c == 0), stop=(c == NCH - 1))
                        sPE.inc(inst)

                    mm_done = sPE.n

                    @block.vector
                    def _(v, tc=tc, mm_done=mm_done):
                        v.wait_ge(sPE.h, mm_done)
                        if tc == 0 and "g" in store_done:
                            v.wait_ge(sST.h, store_done["g"])
                        inst = v.tensor_copy(s_g[:, tc * 16:(tc + 1) * 16], ps_g[:])
                        sDV.inc(inst)

                    gate_ev[0] = sDV.n

                @block.sync
                def _(s, t=t, n=sDV.n):
                    s.wait_ge(sDV.h, n)
                    if fz is None:
                        s.dma_start(
                            out=o_g[t * 512:(t + 1) * 512, :].rearrange(
                                "(tc p) g -> p tc g", p=128),
                            in_=sb3(s_g[:], 16)).then_inc(sST.h, 16)
                        sST.n += 16
                    else:
                        g3 = s_g[:].rearrange("p (tc g) -> p tc g", g=16)
                        for hg in range(2):
                            dd = 2 * t + hg
                            h0 = 4 * hg
                            s.dma_start(
                                out=fz.g1a_in[dd * 512:(dd + 1) * 512,
                                              0:4].rearrange(
                                    "(tc p) g -> p tc g", p=128),
                                in_=g3[:, :, h0:h0 + 4]).then_inc(sST.h, 16)
                            sST.n += 16
                            s.dma_start(
                                out=fz.g1a_in[dd * 512:(dd + 1) * 512,
                                              4:8].rearrange(
                                    "(tc p) g -> p tc g", p=128),
                                in_=g3[:, :, 8 + h0:8 + h0 + 4]
                            ).then_inc(sST.h, 16)
                            sST.n += 16

                store_done["g"] = sST.n

            @block.sync
            def _(s):
                s.wait_ge(sST.h, sST.n)

    return nc



LNSQ = float(np.log(np.sqrt(128.0)))
L2_SERIALIZE = False
L2_DEBUG = False
L2_NPAIR = 4
NKC = 32
NQT = 8


def build_l2(fz=None):
    pre = "" if fz is None else "z2_"
    if fz is None:
        nc = bass.Bass("TRN2", target_bir_lowering=False,
                       detect_race_conditions=False)
        q_in = nc.dram_tensor("q_bh", [512, S], BF16, kind="ExternalInput").ap()
        k_in = nc.dram_tensor("k_bh", [512, S], BF16, kind="ExternalInput").ap()
        v_in = nc.dram_tensor("v_bh", [S, 1024], BF16, kind="ExternalInput").ap()
        g_in = nc.dram_tensor("gates_px", [S, 8], F32, kind="ExternalInput").ap()
        o_h = nc.dram_tensor("h_fm", [1024, S], BF16, kind="ExternalOutput").ap()
    else:
        nc = fz.nc
        q_in = k_in = v_in = o_h = None
        g_in = fz.g1a_out
    gb_in = nc.dram_tensor("gbias", [128, 8], F32, kind="ExternalInput").ap()
    mh_in = nc.dram_tensor("mh", [128, 8], F32, kind="ExternalInput").ap()
    if L2_DEBUG:
        o_dbg = nc.dram_tensor("dbg", [128, 8 * 128], F32, kind="ExternalOutput").ap()
        o_dbg2 = nc.dram_tensor("dbg2", [128, 4 * 257], F32, kind="ExternalOutput").ap()

    _c = np.arange(128)
    ident_np = np.eye(128, dtype=np.float32)
    consts = {
        "ident": ident_np,
        "identb": ident_np.astype(BF16NP),
        "onesq": np.ones((128, 128), np.float32),
        "utri": np.triu(np.ones((128, 128), np.float32)),
        "trilm": np.triu(np.ones((128, 128), np.float32)),
        "trilneg": ((np.triu(np.ones((128, 128))) - 1.0) * 1e30).astype(np.float32),
        "strictu": np.triu(np.ones((128, 128), np.float32), 1),
        "strictuneg": ((np.triu(np.ones((128, 128)), 1) - 1.0) * 1e30).astype(np.float32),
        "ones_row": np.ones((1, 128), np.float32),
        "negones_row": -np.ones((1, 128), np.float32),
    }
    mb = np.zeros((128, 4, 512), np.float32)
    for dj in range(4):
        mb[:, dj, :] = np.where(
            (_c[:, None] + dj * 128) <= np.arange(512)[None, :], 0.0, -1e30)
    consts["maskb"] = np.ascontiguousarray(mb.reshape(128, 4 * 512))
    caps = {k: nc.inline_tensor(v, pre + "c_" + k).ap()
            for k, v in consts.items()}

    with ExitStack() as ctx:
        def sb(name, shape, dtype=F32):
            return ctx.enter_context(nc.sbuf_tensor(pre + name, shape, dtype))

        def ps(name, shape, dtype=F32):
            return ctx.enter_context(nc.psum_tensor(pre + name, shape, dtype))

        s_q = sb("s_q", [128, 4 * S], BF16)
        s_k = sb("s_k", [128, 4 * S], BF16)
        s_v = sb("s_v", [128, NKC * 257], BF16)
        s_g = sb("s_g", [128, NKC * 8])
        s_mh = sb("s_mh", [128, 8])
        s_gb = sb("s_gb", [128, 8])
        cst = {k: sb("k_" + k, list(v.shape), BF16 if k == "identb" else F32)
               for k, v in consts.items()}
        pa_names = ["iraw", "fraw", "ti", "fcap", "labs", "lexp", "lln",
                    "logfg", "fcum", "u", "uadj", "mintra", "mall", "mT",
                    "enm", "psel", "tmpa", "tmpb"]
        pa = {n: sb("pa_" + n, [128, 128]) for n in pa_names}
        s_f = sb("s_f", [128, 256])
        s_prow = sb("s_prow", [1, 128])
        s_mrow = sb("s_mrow", [1, 512])
        s_cp = [sb(f"s_cp{i}", [128, 512], BF16) for i in range(3)]
        s_E = sb("s_E", [128, 512])
        s_acc = sb("s_acc", [128, 4 * 257])
        s_ep = sb("s_ep", [128, 24])
        s_eps = sb("s_eps", [128, 1])
        s_h = sb("s_h", [128, 256])
        s_hb = sb("s_hb", [128, 256], BF16)
        s_ho = sb("s_ho", [128, 2 * 512], BF16)

        ps_qk = [ps("ps_qk0", [128, 512]), ps("ps_qk1", [128, 512])]
        ps_av = [ps(f"ps_av{i}", [128, 257]) for i in range(4)]
        ps_nm = ps("ps_nm", [128, 512])
        ps_aux = ps("ps_aux", [128, 128])

        if fz is None:
            sX = Ctr(ctx.enter_context(nc.semaphore("sX")))
            sPE = Ctr(ctx.enter_context(nc.semaphore("sPE")))
            sDV = Ctr(ctx.enter_context(nc.semaphore("sDV")))
            sAC = Ctr(ctx.enter_context(nc.semaphore("sAC")))
            sST = Ctr(ctx.enter_context(nc.semaphore("sST")))
        else:
            sX, sPE, sDV, sAC, sST = fz.sX, fz.sPE, fz.sDV, fz.sAC, fz.sST

        with (nc.Block() if fz is None else nullcontext(fz.block)) as block:
            def step(deco, ctr, waits, emit):
                @deco
                def _(eng):
                    if L2_SERIALIZE:
                        for c in (sPE, sDV, sAC):
                            if c is not ctr and c.n:
                                eng.wait_ge(c.h, c.n)
                    for sem, val in waits:
                        if val:
                            eng.wait_ge(sem, val)
                    inst = emit(eng)
                    ctr.inc(inst)
                return ctr.n

            def stepseq(deco, ctr, waits, fns):
                @deco
                def _(eng):
                    for sem, val in waits:
                        if val:
                            eng.wait_ge(sem, val)
                    inst = None
                    for i, fn in enumerate(fns):
                        if i > 0:
                            ctr.inc(inst)
                            eng.wait_ge(ctr.h, ctr.n)
                        inst = fn(eng)
                    ctr.inc(inst)
                return ctr.n

            @block.sync
            def _(s):
                if fz is None:
                    s.dma_start(out=sb3(s_q[:], S),
                                in_=fmload(q_in, S)).then_inc(sX.h, 16)
                    s.dma_start(out=sb3(s_k[:], S),
                                in_=fmload(k_in, S)).then_inc(sX.h, 16)
                    sX.n += 32
                else:
                    for dd in range(8):
                        for c in range(4):
                            r0 = dd * 1536 + c * 128
                            s.dma_start(
                                out=s_q[:, c * S + dd * 512:
                                        c * S + dd * 512 + 512],
                                in_=fz.a2a1_out[r0:r0 + 128, 0:512]
                            ).then_inc(sX.h, 16)
                            s.dma_start(
                                out=s_k[:, c * S + dd * 512:
                                        c * S + dd * 512 + 512],
                                in_=fz.a2a1_out[r0 + 512:r0 + 640, 0:512]
                            ).then_inc(sX.h, 16)
                            sX.n += 32
                s.dma_start(out=sb3(s_g[:], 8), in_=fmload(g_in, 8)).then_inc(sX.h, 16)
                s.dma_start(out=s_mh[:], in_=mh_in).then_inc(sX.h, 16)
                s.dma_start(out=s_gb[:], in_=gb_in).then_inc(sX.h, 16)
                for k in cst:
                    s.dma_start(out=cst[k][:], in_=caps[k]).then_inc(sX.h, 16)
                sX.n += 16 * (3 + len(cst))

            base_loads = sX.n

            def emit_init(v):
                v.tensor_scalar_mul(s_gb[:], s_gb[:], 1.0 / CAP)
                v.memset(s_eps[:], EPS_RMS)
                for t in pa.values():
                    v.memset(t[:], 0.0)
                v.memset(s_f[:], 0.0)
                v.memset(s_prow[:], 0.0)
                return v.memset(
                    s_v[:].rearrange("p (c k) -> p c k", k=257)[:, :, 256:257], 1.0)
            vset = step(block.vector, sDV, [(sX.h, base_loads)], emit_init)

            pair_end = [0]
            vfree = [0]
            auxg = [0]

            for pr in range(L2_NPAIR):
                @block.sync
                def _(s, pr=pr):
                    s.wait_ge(sDV.h, vset)
                    if vfree[0]:
                        s.wait_ge(sPE.h, vfree[0])
                    if fz is None:
                        s.dma_start(
                            out=s_v[:].rearrange("p (c k) -> p c k",
                                                 k=257)[:, :, 0:256],
                            in_=v_in[:, pr * 256:(pr + 1) * 256].rearrange(
                                "(c p) d -> p c d", p=128)).then_inc(sX.h, 16)
                        sX.n += 16
                    else:
                        for dd in range(8):
                            r0 = dd * 1536 + 1024
                            s.dma_start(
                                out=s_v[:].rearrange(
                                    "p (c k) -> p c k",
                                    k=257)[:, dd * 4:(dd + 1) * 4, 0:256],
                                in_=fz.a2a1_out[r0:r0 + 512,
                                                pr * 256:(pr + 1) * 256
                                                ].rearrange(
                                    "(c p) d -> p c d", p=128)
                            ).then_inc(sX.h, 16)
                            sX.n += 16

                vld = sX.n

                g3 = s_g[:].rearrange("p (c g) -> p c g", g=8)
                raw = step(block.vector, sDV, [(sX.h, base_loads)], lambda v: (
                    v.tensor_copy(sb3(pa["iraw"][:, 0:32], 1), g3[:, :, pr:pr + 1]),
                    v.tensor_copy(sb3(pa["fraw"][:, 0:32], 1),
                                  g3[:, :, 4 + pr:5 + pr]))[-1])
                tanh = step(block.scalar, sAC,
                            [(sDV.h, raw), (sDV.h, vset)], lambda sc, pr=pr: (
                    sc.activation(pa["ti"][:, 0:32], pa["iraw"][:, 0:32], AF.Tanh,
                                  scale=1.0 / CAP, bias=s_gb[:, pr:pr + 1]),
                    sc.activation(pa["fcap"][:, 0:32], pa["fraw"][:, 0:32], AF.Tanh,
                                  scale=1.0 / CAP, bias=s_gb[:, 4 + pr:5 + pr]))[-1])
                fc = step(block.vector, sDV, [(sAC.h, tanh)], lambda v:
                          v.tensor_scalar_mul(pa["fcap"][:, 0:32],
                                              pa["fcap"][:, 0:32], CAP))
                lns = step(block.scalar, sAC, [(sDV.h, fc)], lambda sc: (
                    sc.activation(pa["labs"][:, 0:32], pa["fcap"][:, 0:32], AF.Abs),
                    sc.activation(pa["lexp"][:, 0:32], pa["labs"][:, 0:32], AF.Exp,
                                  scale=-1.0),
                    sc.activation(pa["lln"][:, 0:32], pa["lexp"][:, 0:32], AF.Ln,
                                  bias=1.0))[-1])
                lfg = step(block.vector, sDV, [(sAC.h, lns)], lambda v:
                    v.scalar_tensor_tensor(pa["logfg"][:, 0:32],
                                           pa["fcap"][:, 0:32], 0.0,
                                           pa["lln"][:, 0:32],
                                           ALU.min, ALU.subtract))

                def emit_fcum(te):
                    inst = None
                    for j in range(NKC):
                        first = True
                        for jp in range(j):
                            inst = te.matmul(ps_aux[:, j:j + 1], cst["onesq"][:],
                                             pa["logfg"][:, jp:jp + 1],
                                             start=first, stop=False)
                            first = False
                        inst = te.matmul(ps_aux[:, j:j + 1], cst["utri"][:],
                                         pa["logfg"][:, j:j + 1],
                                         start=first, stop=True)
                    return inst

                fcum_mm = step(block.tensor, sPE, [(sDV.h, lfg), (sDV.h, auxg[0])],
                               emit_fcum)
                ucols = stepseq(block.vector, sDV, [(sPE.h, fcum_mm)], [
                    lambda v: v.tensor_copy(pa["fcum"][:, 0:32], ps_aux[:, 0:32]),
                    lambda v: v.scalar_tensor_tensor(
                        pa["u"][:, 0:32], pa["ti"][:, 0:32], CAP,
                        pa["fcum"][:, 0:32], ALU.mult, ALU.subtract),
                    lambda v: v.tensor_scalar_add(pa["uadj"][:, 0:32],
                                                  pa["u"][:, 0:32], -LNSQ)])
                auxg[0] = ucols
                ut_mm = step(block.tensor, sPE, [(sDV.h, ucols)], lambda te:
                             te.matmul(ps_aux[:], pa["u"][:], cst["ident"][:],
                                       start=True, stop=True))
                ucol = step(block.vector, sDV, [(sPE.h, ut_mm)], lambda v:
                            v.reduce_max(pa["tmpa"][:, 0:1], ps_aux[:], axis=AX.X))
                auxg[0] = ucol
                pf1 = step(block.vector, sDV, [(sDV.h, ucol)], lambda v:
                    v.scalar_tensor_tensor(pa["tmpb"][:], cst["strictu"][:],
                                           pa["tmpa"][:, 0:1],
                                           cst["strictuneg"][:],
                                           ALU.mult, ALU.add))
                pf_mm = step(block.tensor, sPE, [(sDV.h, pf1)], lambda te:
                             te.matmul(ps_aux[:], pa["tmpb"][:], cst["ident"][:],
                                       start=True, stop=True))
                pcol = step(block.vector, sDV, [(sPE.h, pf_mm)], lambda v:
                            v.reduce_max(pa["tmpa"][:, 1:2], ps_aux[:], axis=AX.X))
                auxg[0] = pcol
                pr_mm = step(block.tensor, sPE, [(sDV.h, pcol)], lambda te:
                             te.matmul(ps_aux[0:1, :], pa["tmpa"][:, 1:2],
                                       cst["ident"][:], start=True, stop=True))
                prow = step(block.vector, sDV, [(sPE.h, pr_mm)], lambda v:
                            v.tensor_copy(s_prow[:], ps_aux[0:1, :]))
                auxg[0] = prow

                mi = prow
                for j in range(NKC):
                    tmask = step(block.vector, sDV, [(sDV.h, mi)], lambda v, j=j:
                        v.scalar_tensor_tensor(pa["tmpb"][:], cst["trilm"][:],
                                               pa["u"][:, j:j + 1],
                                               cst["trilneg"][:],
                                               ALU.mult, ALU.add))
                    tt_mm = step(block.tensor, sPE,
                                 [(sDV.h, tmask), (sDV.h, auxg[0])], lambda te:
                                 te.matmul(ps_aux[:], pa["tmpb"][:],
                                           cst["ident"][:], start=True, stop=True))
                    mi = step(block.vector, sDV, [(sPE.h, tt_mm)], lambda v, j=j:
                              v.reduce_max(pa["mintra"][:, j:j + 1], ps_aux[:],
                                           axis=AX.X))
                    auxg[0] = mi

                pbc_mm = step(block.tensor, sPE, [(sDV.h, mi), (sDV.h, auxg[0])],
                              lambda te: te.matmul(ps_aux[:], cst["ones_row"][:],
                                                   s_prow[:], start=True, stop=True))
                mall = stepseq(block.vector, sDV, [(sPE.h, pbc_mm)], [
                    lambda v: v.tensor_max(pa["mall"][:, 0:32],
                                           pa["mintra"][:, 0:32],
                                           ps_aux[:, 0:32]),
                    lambda v: v.tensor_add(pa["tmpb"][:, 0:32],
                                           pa["fcum"][:, 0:32],
                                           pa["mall"][:, 0:32])])
                auxg[0] = mall
                enm = step(block.scalar, sAC, [(sDV.h, mall)], lambda sc:
                           sc.activation(pa["enm"][:, 0:32], pa["tmpb"][:, 0:32],
                                         AF.Exp, scale=-1.0))
                psel_mm = step(block.tensor, sPE, [(sAC.h, enm)], lambda te:
                              te.matmul(ps_aux[:, 0:8], cst["ones_row"][:],
                                        s_prow[0:1, 0:32].rearrange(
                                            "o (i f) -> o i f", f=4)[:, :, 0:1],
                                        start=True, stop=True))
                psel = step(block.vector, sDV, [(sPE.h, psel_mm)], lambda v:
                            v.tensor_copy(pa["psel"][:, 0:8], ps_aux[:, 0:8]))
                auxg[0] = psel
                prev_exp = 0
                for i in range(1, NQT):
                    f1 = step(block.vector, sDV,
                              [(sDV.h, psel), (sAC.h, prev_exp)],
                              lambda v, i=i:
                              v.tensor_scalar(s_E[:, i * 32:(i + 1) * 32],
                                              pa["uadj"][:, 0:32],
                                              pa["psel"][:, i:i + 1], 0.0,
                                              ALU.subtract, ALU.min))
                    prev_exp = step(block.scalar, sAC, [(sDV.h, f1)],
                                    lambda sc, i=i:
                                    sc.activation(s_f[:, i * 32:(i + 1) * 32],
                                                  s_E[:, i * 32:(i + 1) * 32],
                                                  AF.Exp))
                fdone_ac = sAC.n
                mt_mm = step(block.tensor, sPE, [(sDV.h, auxg[0])], lambda te:
                             te.matmul(ps_aux[:], pa["mall"][:], cst["ident"][:],
                                       start=True, stop=True))
                mt = step(block.vector, sDV, [(sPE.h, mt_mm), (sAC.h, fdone_ac)],
                          lambda v: v.tensor_copy(pa["mT"][:], ps_aux[:]))
                auxg[0] = mt

                if L2_DEBUG and pr == 0:
                    @block.sync
                    def _(s, n=sDV.n):
                        s.wait_ge(sDV.h, n)
                        for ii, nm in enumerate(["fcap", "fcum", "lexp", "lln",
                                                 "iraw", "fraw", "ti", "logfg"]):
                            s.dma_start(out=o_dbg[:, ii * 128:(ii + 1) * 128],
                                        in_=pa[nm][:]).then_inc(sST.h, 16)
                            sST.n += 16

                cp_free = [0, 0, 0]
                qk_free = [None, None]
                av_free = [0]
                nm_free = [0]
                ho_store = [0]
                hb_guard = [0]

                for i in range(NQT):
                    noff = 4 * i
                    qtile = s_q[:, (pr * NQT + i) * 512:(pr * NQT + i + 1) * 512]
                    def emit_mrow(te, i=i):
                        inst = None
                        for jj in range(4):
                            c = 4 * i + jj
                            inst = te.matmul(ps_nm[0:1, jj * 128:(jj + 1) * 128],
                                             cst["ident"][:, c:c + 1], pa["mT"][:],
                                             start=True, stop=True)
                        return inst
                    mr_mm = step(block.tensor, sPE,
                                 [(sDV.h, mt), (sDV.h, nm_free[0])], emit_mrow)
                    mrow = step(block.vector, sDV, [(sPE.h, mr_mm)], lambda v:
                                v.tensor_copy(s_mrow[:], ps_nm[0:1, :]))
                    nm_mm = step(block.tensor, sPE, [(sDV.h, mrow)], lambda te:
                                 te.matmul(ps_nm[:], cst["negones_row"][:],
                                           s_mrow[:], start=True, stop=True))
                    if i > 0:
                        b1 = step(block.vector, sDV, [], lambda v, i=i:
                                  v.tensor_scalar(s_ep[:, 0:4],
                                                  pa["mall"][:, 4 * i:4 * i + 4],
                                                  pa["psel"][:, i:i + 1], None,
                                                  ALU.subtract))
                        bcols = step(block.scalar, sAC, [(sDV.h, b1)], lambda sc:
                                     sc.activation(s_ep[:, 4:8], s_ep[:, 0:4],
                                                   AF.Exp, scale=-1.0))
                    for jk in range(noff):
                        slot = jk % 3
                        qb = jk % 2
                        qk_mm = step(block.tensor, sPE,
                                     [(sX.h, vld),
                                      qk_free[qb] if qk_free[qb] else (sX.h, 0)],
                                     lambda te, jk=jk, qb=qb, qtile=qtile:
                                     te.matmul(ps_qk[qb][:],
                                               s_k[:, pr * S + jk * 128:
                                                   pr * S + (jk + 1) * 128],
                                               qtile, start=True, stop=True))
                        if jk % 2 == 0:
                            ev = step(block.vector, sDV,
                                      [(sPE.h, qk_mm), (sPE.h, cp_free[slot])],
                                      lambda v, slot=slot, qb=qb, i=i, jk=jk:
                                      v.tensor_scalar(
                                          s_cp[slot][:], ps_qk[qb][:],
                                          s_f[:, i * 32 + jk:i * 32 + jk + 1],
                                          None, ALU.mult))
                            evw = (sDV.h, ev)
                        else:
                            ev = step(block.scalar, sAC,
                                      [(sPE.h, qk_mm), (sPE.h, cp_free[slot])],
                                      lambda sc, slot=slot, qb=qb, i=i, jk=jk:
                                      sc.activation(
                                          s_cp[slot][:], ps_qk[qb][:], AF.Copy,
                                          scale=s_f[:, i * 32 + jk:i * 32 + jk + 1]))
                            evw = (sAC.h, ev)
                        qk_free[qb] = evw
                        def emit_av(te, slot=slot, jk=jk, noff=noff):
                            inst = None
                            for sc_i in range(4):
                                inst = te.matmul(
                                    ps_av[sc_i][:],
                                    s_cp[slot][:, sc_i * 128:(sc_i + 1) * 128],
                                    s_v[:, jk * 257:(jk + 1) * 257],
                                    start=(jk == 0), stop=(jk == noff - 1))
                            return inst
                        av_mm = step(block.tensor, sPE,
                                     [evw, (sDV.h, av_free[0]) if jk == 0
                                      else evw], emit_av)
                        cp_free[slot] = av_mm
                    if noff > 0:
                        offev = step(block.vector, sDV,
                                     [(sPE.h, sPE.n), (sAC.h, bcols)],
                                     lambda v: [
                                         v.tensor_scalar(
                                             s_acc[:, sc_i * 257:(sc_i + 1) * 257],
                                             ps_av[sc_i][:],
                                             s_ep[:, 4 + sc_i:5 + sc_i], None,
                                             ALU.mult)
                                         for sc_i in range(4)][-1])
                        av_free[0] = offev
                    e_guard = 0
                    for dj in range(4):
                        jd = 4 * i + dj
                        slot = dj % 3
                        qb = dj % 2
                        qk_mm = step(block.tensor, sPE,
                                     [qk_free[qb] if qk_free[qb] else (sX.h, vld)],
                                     lambda te, jd=jd, qb=qb, qtile=qtile:
                                     te.matmul(ps_qk[qb][:],
                                               s_k[:, pr * S + jd * 128:
                                                   pr * S + (jd + 1) * 128],
                                               qtile, start=True, stop=True))
                        et = step(block.vector, sDV, [(sPE.h, nm_mm)],
                                  lambda v, dj=dj: v.tensor_tensor(
                                      s_E[:], ps_nm[:],
                                      cst["maskb"][:, dj * 512:(dj + 1) * 512],
                                      ALU.add))
                        e_guard = et
                        ea = step(block.scalar, sAC, [(sDV.h, et)],
                                  lambda sc, jd=jd: sc.activation(
                                      s_E[:], s_E[:], AF.Exp,
                                      bias=pa["uadj"][:, jd:jd + 1]))
                        cp = step(block.vector, sDV,
                                  [(sAC.h, ea), (sPE.h, qk_mm),
                                   (sPE.h, cp_free[slot])],
                                  lambda v, slot=slot, qb=qb: v.tensor_mul(
                                      s_cp[slot][:], ps_qk[qb][:], s_E[:]))
                        qk_free[qb] = (sDV.h, cp)
                        def emit_av2(te, slot=slot, dj=dj, jd=jd):
                            inst = None
                            for sc_i in range(4):
                                inst = te.matmul(
                                    ps_av[sc_i][:],
                                    s_cp[slot][:, sc_i * 128:(sc_i + 1) * 128],
                                    s_v[:, jd * 257:(jd + 1) * 257],
                                    start=(dj == 0), stop=(dj == 3))
                            return inst
                        av_mm = step(block.tensor, sPE,
                                     [(sDV.h, cp), (sDV.h, av_free[0]) if dj == 0 and noff == 0
                                      else (sDV.h, cp)], emit_av2)
                        cp_free[slot] = av_mm
                    nm_free[0] = e_guard
                    dev = stepseq(block.vector, sDV, [(sPE.h, sPE.n)], [
                        (lambda v, sc_i=sc_i, i=i:
                         (v.tensor_add(s_acc[:, sc_i * 257:(sc_i + 1) * 257],
                                       s_acc[:, sc_i * 257:(sc_i + 1) * 257],
                                       ps_av[sc_i][:]) if i > 0 else
                          v.tensor_copy(s_acc[:, sc_i * 257:(sc_i + 1) * 257],
                                        ps_av[sc_i][:])))
                        for sc_i in range(4)])
                    av_free[0] = dev
                    if L2_DEBUG and pr == 0 and i == 1:
                        @block.sync
                        def _(s, n=sDV.n):
                            s.wait_ge(sDV.h, n)
                            s.dma_start(out=o_dbg2[:], in_=s_acc[:]).then_inc(
                                sST.h, 16)
                            sST.n += 16
                    for sc_i in range(4):
                        ab = step(block.scalar, sAC, [(sDV.h, dev)],
                                  lambda sc, sc_i=sc_i: sc.activation(
                                      s_ep[:, 8 + sc_i:9 + sc_i],
                                      s_acc[:, sc_i * 257 + 256:sc_i * 257 + 257],
                                      AF.Abs))
                        nrm = stepseq(block.vector, sDV, [(sAC.h, ab)], [
                            lambda v, sc_i=sc_i, i=i: v.tensor_max(
                                s_ep[:, 12 + sc_i:13 + sc_i],
                                s_ep[:, 8 + sc_i:9 + sc_i],
                                pa["enm"][:, 4 * i + sc_i:4 * i + sc_i + 1]),
                            lambda v, sc_i=sc_i: v.reciprocal(
                                s_ep[:, 16 + sc_i:17 + sc_i],
                                s_ep[:, 12 + sc_i:13 + sc_i]),
                            lambda v, sc_i=sc_i: v.tensor_scalar(
                                s_h[:], s_acc[:, sc_i * 257:sc_i * 257 + 256],
                                s_ep[:, 16 + sc_i:17 + sc_i], None, ALU.mult)])
                        sq = step(block.scalar, sAC, [(sDV.h, nrm)],
                                  lambda sc, sc_i=sc_i: (
                                      sc.activation(s_E[:, 0:256], s_h[:],
                                                    AF.Square,
                                                    accum_out=s_ep[:, 20:21]),
                                      sc.activation(s_ep[:, 21:22],
                                                    s_ep[:, 20:21], AF.Sqrt,
                                                    scale=1.0 / 256.0,
                                                    bias=s_eps[:, 0:1]))[-1])
                        hb = stepseq(block.vector, sDV,
                                  [(sAC.h, sq), (sPE.h, hb_guard[0])], [
                                      lambda v: v.reciprocal(s_ep[:, 22:23],
                                                             s_ep[:, 21:22]),
                                      lambda v: v.tensor_scalar(
                                          s_hb[:], s_h[:], s_ep[:, 22:23],
                                          None, ALU.mult)])
                        for cc in range(2):
                            tr_mm = step(block.tensor, sPE,
                                         [(sDV.h, hb), (sDV.h, auxg[0])],
                                         lambda te, cc=cc: te.matmul(
                                             ps_aux[:],
                                             s_hb[:, cc * 128:(cc + 1) * 128],
                                             cst["identb"][:], start=True,
                                             stop=True))
                            hb_guard[0] = tr_mm
                            hv = step(block.vector, sDV,
                                      [(sPE.h, tr_mm), (sST.h, ho_store[0])],
                                      lambda v, cc=cc, sc_i=sc_i: v.tensor_scalar(
                                          s_ho[:, cc * 512 + sc_i * 128:
                                               cc * 512 + (sc_i + 1) * 128],
                                          ps_aux[:],
                                          s_mh[:, pr * 2 + cc:pr * 2 + cc + 1],
                                          None, ALU.mult))
                            auxg[0] = hv
                    @block.sync
                    def _(s, pr=pr, i=i, n=sDV.n):
                        s.wait_ge(sDV.h, n)
                        if fz is None:
                            s.dma_start(
                                out=fmload(o_h[pr * 256:(pr + 1) * 256,
                                               i * 512:(i + 1) * 512], 512),
                                in_=sb3(s_ho[:], 512)).then_inc(sST.h, 16)
                        else:
                            r0 = i * 1024 + pr * 256
                            s.dma_start(
                                out=fmload(fz.a2a2_in[r0:r0 + 256, 0:512], 512),
                                in_=sb3(s_ho[:], 512)).then_inc(sST.h, 16)
                        sST.n += 16
                    ho_store[0] = sST.n
                vfree[0] = sPE.n
                pair_end[0] = sST.n

            @block.sync
            def _(s):
                s.wait_ge(sST.h, sST.n)

    return nc



def build_l3(fz=None):
    pre = "" if fz is None else "z3_"
    if fz is None:
        nc = bass.Bass("TRN2", target_bir_lowering=False,
                       detect_race_conditions=False)
        xn_in = nc.dram_tensor("xn_bf", [D, TOK], BF16, kind="ExternalInput").ap()
        hin_in = nc.dram_tensor("hin_bf", [D, TOK], BF16,
                                kind="ExternalInput").ap()
        h_in = nc.dram_tensor("h_fm", [D, TOK], BF16, kind="ExternalInput").ap()
        o_att = nc.dram_tensor("attn_out", [D, TOK], F32,
                               kind="ExternalOutput").ap()
    else:
        nc = fz.nc
        xn_in = fz.dram["xn_bf"]
        hin_in = fz.dram["hin_bf"]
        h_in = None
        o_att = fz.dram["attn_out"]
    wog = nc.dram_tensor("w_og", [D, D], F32, kind="ExternalInput").ap()
    wout = nc.dram_tensor("w_out", [D, D], F32, kind="ExternalInput").ap()
    n2w = nc.dram_tensor("n2w", [128, NCH], F32, kind="ExternalInput").ap()
    wgate = nc.dram_tensor("w_gate", [D, FF_IN], F32, kind="ExternalInput").ap()
    wup = nc.dram_tensor("w_up", [D, FF_IN], F32, kind="ExternalInput").ap()
    wdown = nc.dram_tensor("w_down", [FF_IN, D], F32, kind="ExternalInput").ap()
    x1_dram = nc.dram_tensor("x1_scr", [D, TOK], F32, kind="Internal").ap()
    h3_dram = nc.dram_tensor("h3_scr", [FF_IN, TOK], BF16, kind="Internal").ap()

    ones_col = nc.inline_tensor(np.ones((128, 1), np.float32),
                                pre + "l3_oc").ap()
    ones_row = nc.inline_tensor(np.ones((1, 128), np.float32),
                                pre + "l3_or").ap()

    with ExitStack() as ctx:
        def sb(name, shape, dtype=F32, c=None):
            return (c or ctx).enter_context(nc.sbuf_tensor(pre + name, shape, dtype))

        def ps(name, shape, dtype=F32):
            return ctx.enter_context(nc.psum_tensor(pre + name, shape, dtype))

        s_w0 = sb("s_w0", [128, NCH * D], BF16)
        s_sg = sb("s_sg", [128, 512])
        s_g = sb("s_g", [128, 512], BF16)
        s_h3b = sb("s_h3b", [128, 512], BF16)
        s_sq = sb("s_sq", [128, 512])
        s_row = sb("s_row", [1, 512])
        s_nw = sb("s_nw", [128, NCH])
        ones_sb = sb("ones_sb", [128, 1])
        ones_rsb = sb("ones_rsb", [1, 128])

        ps_a = ps("ps_a", [128, 512])
        ps_b = ps("ps_b", [128, 512])
        ps_r = ps("ps_r", [1, 512])
        ps_c = ps("ps_c", [128, 512])

        if fz is None:
            sW = Ctr(ctx.enter_context(nc.semaphore("sW")))
            sX = Ctr(ctx.enter_context(nc.semaphore("sX")))
            sPE = Ctr(ctx.enter_context(nc.semaphore("sPE")))
            sDV = Ctr(ctx.enter_context(nc.semaphore("sDV")))
            sAC = Ctr(ctx.enter_context(nc.semaphore("sAC")))
            sST = Ctr(ctx.enter_context(nc.semaphore("sST")))
        else:
            sW, sX, sPE, sDV, sAC, sST = (fz.sW, fz.sX, fz.sPE, fz.sDV,
                                          fz.sAC, fz.sST)

        c_h2 = ExitStack()
        s_h2 = sb("s_h2", [128, 4 * NCH * 512], BF16, c_h2)

        with (nc.Block() if fz is None else nullcontext(fz.block)) as block:
            step, stepseq = mk_steps(block)

            @block.sync
            def _(s):
                s.dma_start(out=s_nw[:], in_=n2w).then_inc(sX.h, 16)
                s.dma_start(out=ones_sb[:], in_=ones_col).then_inc(sX.h, 16)
                s.dma_start(out=ones_rsb[:], in_=ones_row).then_inc(sX.h, 16)
                sX.n += 48

            nw_ld = sX.n
            evg = {}
            rst_read = [0]
            tile_consumed = [0]
            w_guard = [0]

            c1 = ExitStack()
            s_xn = sb("s_xn", [128, NCH * 512], BF16, c1)
            s_xf = sb("s_xf", [128, 2 * 512], BF16, c1)
            s_hf = sb("s_hf", [128, 2 * 512], BF16, c1)
            s_ho = sb("s_ho", [128, NCH * 512], BF16, c1)
            s_x1 = sb("s_x1", [128, NCH * 512], F32, c1)
            hf_guard = {}
            xf_guard = {}

            for t in range(NT):
                @block.sync
                def _(s, t=t):
                    if t > 0:
                        s.wait_ge(sDV.h, tile_consumed[0])
                    s.dma_start(out=sb3(s_xn[:], 512),
                                in_=fmload(hin_in[:, t * 512:(t + 1) * 512], 512)
                                ).then_inc(sX.h, 16)
                    sX.n += 16

                xld = sX.n
                hf_ld = {}

                @block.gpsimd
                def _(g, t=t):
                    g.wait_ge(sPE.h, w_guard[0])
                    g.dma_start(out=sb3(s_w0[:], D), in_=fmload(wog, D)
                                ).then_inc(sW.h, 16)
                    sW.n += 16

                wld = sW.n
                for c in range(NCH):
                    @block.sync
                    def _(s, t=t, c=c):
                        sem, val = hf_guard.get(c % 2, (None, 0))
                        if val:
                            s.wait_ge(sem, val)
                        if fz is None:
                            src = h_in[c * 128:(c + 1) * 128,
                                       t * 512:(t + 1) * 512]
                        else:
                            e = 2 * t + c // 8
                            r0 = e * 1024 + (c % 8) * 128
                            src = fz.a2a2_out[r0:r0 + 128, 0:512]
                        s.dma_start(
                            out=s_hf[:, (c % 2) * 512:(c % 2 + 1) * 512],
                            in_=src).then_inc(sX.h, 16)
                        sX.n += 16

                    hf_ld[c] = sX.n
                    pd, pdn = (ps_a, "a") if c % 2 == 0 else (ps_b, "b")
                    mm = step(block.tensor, sPE,
                              [(sW.h, wld), (sX.h, xld),
                               evg.get(pdn, (None, 0))],
                              lambda te, c=c, pd=pd: [
                                  te.matmul(pd[:],
                                            s_w0[:, k * D + c * 128:
                                                 k * D + (c + 1) * 128],
                                            s_xn[:, k * 512:(k + 1) * 512],
                                            start=(k == 0), stop=(k == NCH - 1))
                                  for k in range(NCH)][-1])
                    sg = step(block.scalar, sAC,
                              [(sPE.h, mm), evg.get("sg", (None, 0))],
                              lambda sc, pd=pd: sc.activation(s_sg[:], pd[:],
                                                              AF.Sigmoid))
                    evg[pdn] = (sAC.h, sg)
                    ho = step(block.vector, sDV,
                              [(sAC.h, sg), (sX.h, hf_ld[c])],
                              lambda v, c=c: v.tensor_mul(
                                  s_ho[:, c * 512:(c + 1) * 512],
                                  s_sg[:], s_hf[:, (c % 2) * 512:(c % 2 + 1) * 512]))
                    evg["sg"] = (sDV.h, ho)
                    hf_guard[c % 2] = (sDV.h, ho)
                hout_done = sDV.n
                w_guard[0] = sPE.n

                @block.gpsimd
                def _(g, t=t):
                    g.wait_ge(sPE.h, w_guard[0])
                    g.dma_start(out=sb3(s_w0[:], D), in_=fmload(wout, D)
                                ).then_inc(sW.h, 16)
                    sW.n += 16

                wld = sW.n
                for c in range(NCH):
                    @block.sync
                    def _(s, t=t, c=c):
                        sem, val = xf_guard.get(c % 2, (None, 0))
                        if val:
                            s.wait_ge(sem, val)
                        s.dma_start(
                            out=s_xf[:, (c % 2) * 512:(c % 2 + 1) * 512],
                            in_=xn_in[c * 128:(c + 1) * 128,
                                      t * 512:(t + 1) * 512]).then_inc(sX.h, 16)
                        sX.n += 16

                    xf_ld = sX.n
                    pd, pdn = (ps_a, "a") if c % 2 == 0 else (ps_b, "b")
                    mm = step(block.tensor, sPE,
                              [(sW.h, wld), (sDV.h, hout_done),
                               evg.get(pdn, (None, 0))],
                              lambda te, c=c, pd=pd: [
                                  te.matmul(pd[:],
                                            s_w0[:, k * D + c * 128:
                                                 k * D + (c + 1) * 128],
                                            s_ho[:, k * 512:(k + 1) * 512],
                                            start=(k == 0), stop=(k == NCH - 1))
                                  for k in range(NCH)][-1])
                    x1c = step(block.vector, sDV,
                               [(sPE.h, mm), (sX.h, xf_ld)],
                               lambda v, c=c, pd=pd: v.tensor_add(
                                   s_x1[:, c * 512:(c + 1) * 512], pd[:],
                                   s_xf[:, (c % 2) * 512:(c % 2 + 1) * 512]))
                    evg[pdn] = (sDV.h, x1c)
                    xf_guard[c % 2] = (sDV.h, x1c)
                    sq = step(block.scalar, sAC,
                              [(sDV.h, x1c), evg.get("sq", (None, 0))],
                              lambda sc, c=c: sc.activation(
                                  s_sq[:], s_x1[:, c * 512:(c + 1) * 512],
                                  AF.Square))
                    cs = step(block.tensor, sPE,
                              [(sAC.h, sq),
                               (sDV.h, rst_read[0]) if c == 0 else (None, 0)],
                              lambda te, c=c: te.matmul(
                                  ps_r[:], ones_sb[:], s_sq[:],
                                  start=(c == 0), stop=(c == NCH - 1)))
                    evg["sq"] = (sPE.h, cs)
                x1_done = sDV.n
                sq_done = sPE.n
                w_guard[0] = sPE.n

                @block.sync
                def _(s, t=t, n=x1_done):
                    s.wait_ge(sDV.h, n)
                    s.dma_start(
                        out=fmload(x1_dram[:, t * 512:(t + 1) * 512], 512),
                        in_=sb3(s_x1[:], 512)).then_inc(sST.h, 16)
                    sST.n += 16

                rrow = step(block.vector, sDV, [(sPE.h, sq_done)],
                            lambda v: v.tensor_scalar(
                                s_row[:], ps_r[:], 1.0 / D, EPS_RMS,
                                ALU.mult, ALU.add))
                rs = step(block.scalar, sAC, [(sDV.h, rrow)],
                          lambda sc: sc.activation(s_row[:], s_row[:], AF.Sqrt))
                rr = step(block.vector, sDV, [(sAC.h, rs)],
                          lambda v: v.reciprocal(s_row[:], s_row[:]))
                rst_read[0] = rr
                bc = step(block.tensor, sPE,
                          [(sDV.h, rr), evg.get("c", (None, 0))],
                          lambda te: te.matmul(ps_c[:], ones_rsb[:], s_row[:],
                                               start=True, stop=True))
                h2l = rr
                for c in range(NCH):
                    h2l = step(block.vector, sDV,
                               [(sPE.h, bc), (sX.h, nw_ld)],
                               lambda v, c=c, t=t: v.scalar_tensor_tensor(
                                   s_h2[:, (t * NCH + c) * 512:
                                        (t * NCH + c + 1) * 512],
                                   s_x1[:, c * 512:(c + 1) * 512],
                                   s_nw[:, c:c + 1], ps_c[:],
                                   ALU.mult, ALU.mult))
                evg["c"] = (sDV.h, h2l)
                tile_consumed[0] = h2l

            c1.close()

            h3_last = [0]
            for mg in range(16):
                @block.gpsimd
                def _(g, mg=mg):
                    g.wait_ge(sPE.h, w_guard[0])
                    g.dma_start(out=sb3(s_w0[:, 0:NCH * 512], 512),
                                in_=fmload(wgate[:, mg * 512:(mg + 1) * 512], 512)
                                ).then_inc(sW.h, 16)
                    g.dma_start(out=sb3(s_w0[:, NCH * 512:2 * NCH * 512], 512),
                                in_=fmload(wup[:, mg * 512:(mg + 1) * 512], 512)
                                ).then_inc(sW.h, 16)
                    sW.n += 32

                wld = sW.n
                UPOFF = NCH * 512
                for t in range(NT):
                    for mc in range(4):
                        pg, pgn = (ps_a, "a") if mc % 2 == 0 else (ps_b, "b")
                        mmg = step(block.tensor, sPE,
                                   [(sW.h, wld), evg.get(pgn, (None, 0))],
                                   lambda te, t=t, mc=mc, pg=pg: [
                                       te.matmul(pg[:],
                                                 s_w0[:, k * 512 + mc * 128:
                                                      k * 512 + (mc + 1) * 128],
                                                 s_h2[:, (t * NCH + k) * 512:
                                                      (t * NCH + k + 1) * 512],
                                                 start=(k == 0),
                                                 stop=(k == NCH - 1))
                                       for k in range(NCH)][-1])
                        gs = step(block.scalar, sAC,
                                  [(sPE.h, mmg), evg.get("g", (None, 0))],
                                  lambda sc, pg=pg: sc.activation(s_g[:], pg[:],
                                                                  AF.Silu))
                        evg[pgn] = (sAC.h, gs)
                        mmu = step(block.tensor, sPE,
                                   [evg.get("c", (None, 0))],
                                   lambda te, t=t, mc=mc: [
                                       te.matmul(ps_c[:],
                                                 s_w0[:, UPOFF + k * 512 + mc * 128:
                                                      UPOFF + k * 512 + (mc + 1) * 128],
                                                 s_h2[:, (t * NCH + k) * 512:
                                                      (t * NCH + k + 1) * 512],
                                                 start=(k == 0),
                                                 stop=(k == NCH - 1))
                                       for k in range(NCH)][-1])
                        h3c = step(block.vector, sDV,
                                   [(sPE.h, mmu), (sAC.h, gs),
                                    (sST.h, h3_last[0])],
                                   lambda v: v.tensor_mul(s_h3b[:], ps_c[:],
                                                          s_g[:]))
                        evg["c"] = (sDV.h, h3c)
                        evg["g"] = (sDV.h, h3c)

                        @block.sync
                        def _(s, t=t, mc=mc, mg=mg, n=h3c):
                            s.wait_ge(sDV.h, n)
                            s.dma_start(
                                out=h3_dram[mg * 512 + mc * 128:
                                            mg * 512 + (mc + 1) * 128,
                                            t * 512:(t + 1) * 512],
                                in_=s_h3b[:]).then_inc(sST.h, 16)
                            sST.n += 16

                        h3_last[0] = sST.n
                w_guard[0] = sPE.n

            c_h2.close()

            with ExitStack() as c2:
                s_h3 = sb("s_h3", [128, 64 * 512], BF16, c2)
                s_x1b = sb("s_x1b", [128, NCH * 512], F32, c2)
                s_acc = sb("s_acc", [128, NCH * 512], F32, c2)
                pb_guard = [0]
                for t in range(NT):
                    @block.sync
                    def _(s, t=t):
                        if t > 0:
                            s.wait_ge(sST.h, pb_guard[0])
                        s.dma_start(out=sb3(s_h3[:], 512),
                                    in_=fmload(h3_dram[:, t * 512:(t + 1) * 512],
                                               512)).then_inc(sX.h, 16)
                        s.dma_start(out=sb3(s_x1b[:], 512),
                                    in_=fmload(x1_dram[:, t * 512:(t + 1) * 512],
                                               512)).then_inc(sX.h, 16)
                        sX.n += 32

                    h3ld = sX.n
                    for kg in range(8):
                        half = (kg % 2) * NCH * 1024
                        @block.gpsimd
                        def _(g, kg=kg, half=half):
                            g.wait_ge(sPE.h, w_guard[0])
                            g.dma_start(
                                out=sb3(s_w0[:, half:half + 8 * D], D),
                                in_=fmload(wdown[kg * 1024:(kg + 1) * 1024, :], D)
                                ).then_inc(sW.h, 16)
                            sW.n += 16

                        wld = sW.n
                        for c in range(NCH):
                            pd, pdn = (ps_a, "a") if c % 2 == 0 else (ps_b, "b")
                            mm = step(block.tensor, sPE,
                                      [(sW.h, wld), (sX.h, h3ld),
                                       evg.get(pdn, (None, 0))],
                                      lambda te, c=c, pd=pd, kg=kg, half=half: [
                                          te.matmul(
                                              pd[:],
                                              s_w0[:, half + k * D + c * 128:
                                                   half + k * D + (c + 1) * 128],
                                              s_h3[:, (kg * 8 + k) * 512:
                                                   (kg * 8 + k + 1) * 512],
                                              start=(k == 0), stop=(k == 7))
                                          for k in range(8)][-1])
                            if kg == 0:
                                acc = step(block.vector, sDV, [(sPE.h, mm)],
                                           lambda v, c=c, pd=pd: v.tensor_copy(
                                               s_acc[:, c * 512:(c + 1) * 512],
                                               pd[:]))
                            elif kg < 7:
                                acc = step(block.vector, sDV, [(sPE.h, mm)],
                                           lambda v, c=c, pd=pd: v.tensor_add(
                                               s_acc[:, c * 512:(c + 1) * 512],
                                               s_acc[:, c * 512:(c + 1) * 512],
                                               pd[:]))
                            else:
                                acc = stepseq(block.vector, sDV, [(sPE.h, mm)], [
                                    lambda v, c=c, pd=pd: v.tensor_add(
                                        s_acc[:, c * 512:(c + 1) * 512],
                                        s_acc[:, c * 512:(c + 1) * 512], pd[:]),
                                    lambda v, c=c: v.tensor_add(
                                        s_acc[:, c * 512:(c + 1) * 512],
                                        s_acc[:, c * 512:(c + 1) * 512],
                                        s_x1b[:, c * 512:(c + 1) * 512])])
                            evg[pdn] = (sDV.h, acc)
                        w_guard[0] = sPE.n if kg > 0 else w_guard[0]

                    @block.sync
                    def _(s, t=t, n=sDV.n):
                        s.wait_ge(sDV.h, n)
                        s.dma_start(
                            out=fmload(o_att[:, t * 512:(t + 1) * 512], 512),
                            in_=sb3(s_acc[:], 512)).then_inc(sST.h, 16)
                        sST.n += 16

                    pb_guard[0] = sST.n

            @block.sync
            def _(s):
                s.wait_ge(sST.h, sST.n)

    return nc



NMG = DFF // 512


def build_l4(fz=None):
    pre = "" if fz is None else "z4_"
    if fz is None:
        nc = bass.Bass("TRN2", target_bir_lowering=False,
                       detect_race_conditions=False)
        xn_in = nc.dram_tensor("xn_bf", [D, TOK], BF16, kind="ExternalInput").ap()
        at_in = nc.dram_tensor("attn_in", [D, TOK], F32,
                               kind="ExternalInput").ap()
        x_in = nc.dram_tensor("x_fm", [D, TOK], F32, kind="ExternalInput").ap()
        o_x2 = nc.dram_tensor("x2", [D, TOK], F32, kind="ExternalOutput").ap()
        o_xn1 = nc.dram_tensor("xn1", [D, TOK], BF16, kind="ExternalOutput").ap()
    else:
        nc = fz.nc
        xn_in = fz.dram["xn_bf"]
        at_in = fz.dram["attn_out"]
        x_in = fz.dram["x_fm"]
        o_x2 = fz.dram["x2"]
        o_xn1 = fz.dram["xn1"]
    wgT = nc.dram_tensor("wgT", [D, DFF], F32, kind="ExternalInput").ap()
    wuT = nc.dram_tensor("wuT", [D, DFF], F32, kind="ExternalInput").ap()
    wdT = nc.dram_tensor("wdT", [DFF, D], F32, kind="ExternalInput").ap()
    wrT = nc.dram_tensor("wrT", [D, 8], F32, kind="ExternalInput").ap()
    agT = nc.dram_tensor("agT", [D, 128], F32, kind="ExternalInput").ap()
    auT = nc.dram_tensor("auT", [D, 128], F32, kind="ExternalInput").ap()
    bgT = nc.dram_tensor("bgT", [128, DFF], F32, kind="ExternalInput").ap()
    buT = nc.dram_tensor("buT", [128, DFF], F32, kind="ExternalInput").ap()
    adT = nc.dram_tensor("adT", [DFF, 128], F32, kind="ExternalInput").ap()
    bdT = nc.dram_tensor("bdT", [128, D], F32, kind="ExternalInput").ap()
    csel = nc.dram_tensor("csel", [16, 8 * 128], F32, kind="ExternalInput").ap()
    crep = nc.dram_tensor("crep", [8, 128], F32, kind="ExternalInput").ap()
    cmask = nc.dram_tensor("cmask", [128, 16], F32, kind="ExternalInput").ap()
    hb_dram = nc.dram_tensor("hb_scr", [DFF, TOK], BF16, kind="Internal").ap()

    ones_col = nc.inline_tensor(np.ones((128, 1), np.float32),
                                pre + "l4_oc").ap()
    ones_row = nc.inline_tensor(np.ones((1, 128), np.float32),
                                pre + "l4_or").ap()
    ident_c = nc.inline_tensor(np.eye(128, dtype=np.float32),
                               pre + "l4_id").ap()

    with ExitStack() as ctx:
        def sb(name, shape, dtype=F32, c=None):
            return (c or ctx).enter_context(nc.sbuf_tensor(pre + name, shape, dtype))

        def ps(name, shape, dtype=F32):
            return ctx.enter_context(nc.psum_tensor(pre + name, shape, dtype))

        s_wg = sb("s_wg", [128, NCH * 512], BF16)
        s_wu = sb("s_wu", [128, NCH * 512], BF16)
        s_wd = sb("s_wd", [128, 8 * D], BF16)
        s_xn = sb("s_xn", [128, NCH * 512], BF16)
        s_hin = sb("s_hin", [128, NCH * 512], BF16)
        s_n1 = sb("s_n1", [128, NCH])
        s_ag = sb("s_ag", [128, NCH * 128], BF16)
        s_au = sb("s_au", [128, NCH * 128], BF16)
        s_bg = sb("s_bg", [128, 512], BF16)
        s_bu = sb("s_bu", [128, 512], BF16)
        s_ad = sb("s_ad", [128, 512], BF16)
        s_bd = sb("s_bd", [128, D], BF16)
        s_wr = sb("s_wr", [128, NCH * 8], BF16)
        s_sel = sb("s_sel", [16, 8 * 128])
        s_rep = sb("s_rep", [8, 128])
        s_msk = sb("s_msk", [128, 16])
        s_id = sb("s_id", [128, 128])
        s_rg = sb("s_rg", [128, 512], BF16)
        s_ru = sb("s_ru", [128, 512], BF16)
        s_rgd = sb("s_rgd", [128, 8 * 512], BF16)
        s_rud = sb("s_rud", [128, 8 * 512], BF16)
        s_rt = sb("s_rt", [128, 8])
        s_rt2 = sb("s_rt2", [128, 8])
        s_ew = sb("s_ew", [128, 8])
        s_ewr = sb("s_ewr", [8, 512])
        s_rd = [sb(f"s_rd{e}", [16, 512]) for e in range(8)]
        s_rdb = sb("s_rdb", [128, 512], BF16)
        s_sg = sb("s_sg", [128, 512], BF16)
        s_hb = sb("s_hb", [128, 512], BF16)
        s_h3 = sb("s_h3", [128, 8 * 512], BF16)
        s_x2 = sb("s_x2", [128, NCH * 512])
        s_at = sb("s_at", [128, 512])
        s_xf = sb("s_xf", [128, 512])
        s_row = sb("s_row", [1, 2 * 512])
        s_sq = sb("s_sq", [128, 512])
        ones_sb = sb("ones_sb", [128, 1])
        ones_rsb = sb("ones_rsb", [1, 128])

        ps_g = ps("ps_g", [128, 512])
        ps_u = ps("ps_u", [128, 512])
        ps_r = [ps("ps_r0", [16, 512]), ps("ps_r1", [16, 512])]
        ps_m = ps("ps_m", [128, 512])
        ps_m2 = ps("ps_m2", [128, 512])
        ps_s1 = ps("ps_s1", [1, 512])
        ps_s2 = ps("ps_s2", [1, 512])

        if fz is None:
            sW = Ctr(ctx.enter_context(nc.semaphore("sW")))
            sX = Ctr(ctx.enter_context(nc.semaphore("sX")))
            sPE = Ctr(ctx.enter_context(nc.semaphore("sPE")))
            sDV = Ctr(ctx.enter_context(nc.semaphore("sDV")))
            sAC = Ctr(ctx.enter_context(nc.semaphore("sAC")))
            sST = Ctr(ctx.enter_context(nc.semaphore("sST")))
        else:
            sW, sX, sPE, sDV, sAC, sST = (fz.sW, fz.sX, fz.sPE, fz.sDV,
                                          fz.sAC, fz.sST)

        with (nc.Block() if fz is None else nullcontext(fz.block)) as block:
            step, stepseq = mk_steps(block)

            @block.gpsimd
            def _(g):
                for src_, dst, cols in ((agT, s_ag, 128), (auT, s_au, 128),
                                        (bdT, s_bd, None), (wrT, s_wr, 8)):
                    if cols is None:
                        g.dma_start(out=dst[:], in_=src_).then_inc(sW.h, 16)
                    else:
                        g.dma_start(out=sb3(dst[:], cols),
                                    in_=fmload(src_, cols)).then_inc(sW.h, 16)
                    sW.n += 16

            @block.sync
            def _(s):
                s.dma_start(out=s_sel[:], in_=csel).then_inc(sX.h, 16)
                s.dma_start(out=s_rep[:], in_=crep).then_inc(sX.h, 16)
                s.dma_start(out=s_msk[:], in_=cmask).then_inc(sX.h, 16)
                s.dma_start(out=s_id[:], in_=ident_c).then_inc(sX.h, 16)
                s.dma_start(out=ones_sb[:], in_=ones_col).then_inc(sX.h, 16)
                s.dma_start(out=ones_rsb[:], in_=ones_row).then_inc(sX.h, 16)
                sX.n += 96

            const_ld = sX.n
            wA_ld = sW.n
            evg = {}
            guard = {"xn_st": 0, "h3": 0, "stats": 0, "xn_pe": 0}
            wgu_guard = [0]
            wd_guard = [0]

            for t in range(NT):
                @block.sync
                def _(s, t=t):
                    if guard["xn_pe"]:
                        s.wait_ge(sPE.h, guard["xn_pe"])
                    if guard["xn_st"]:
                        s.wait_ge(sST.h, guard["xn_st"])
                    s.dma_start(out=sb3(s_xn[:], 512),
                                in_=fmload(xn_in[:, t * 512:(t + 1) * 512], 512)
                                ).then_inc(sX.h, 16)
                    sX.n += 16

                xld = sX.n

                for tc in range(4):
                    mm = step(block.tensor, sPE,
                              [(sW.h, wA_ld), (sX.h, xld), (sX.h, const_ld),
                               evg.get("m", (None, 0))],
                              lambda te, tc=tc: [
                                  te.matmul(ps_m[0:128, 0:8],
                                            s_xn[:, k * 512 + tc * 128:
                                                 k * 512 + (tc + 1) * 128],
                                            s_wr[:, k * 8:(k + 1) * 8],
                                            start=(k == 0), stop=(k == NCH - 1))
                                  for k in range(NCH)][-1])
                    sm = stepseq(block.vector, sDV, [(sPE.h, mm)], [
                        lambda v: v.tensor_copy(s_rt[:], ps_m[0:128, 0:8]),
                        lambda v: v.reduce_max(s_ew[:, 0:1], s_rt[:], axis=AX.X),
                        lambda v: v.tensor_scalar(s_rt[:], s_rt[:],
                                                  s_ew[:, 0:1], None,
                                                  ALU.subtract)])
                    evg["m"] = (sDV.h, sm)
                    ex = step(block.scalar, sAC,
                              [(sDV.h, sm), evg.get("rt_ac", (None, 0))],
                              lambda sc: sc.activation(s_rt[:], s_rt[:], AF.Exp,
                                                       accum_out=s_ew[:, 1:2]))
                    evg["rt_ac"] = (sAC.h, ex)
                    pr = stepseq(block.vector, sDV, [(sAC.h, ex)], [
                        lambda v: v.reciprocal(s_ew[:, 2:3], s_ew[:, 1:2]),
                        lambda v: v.tensor_scalar(s_rt[:], s_rt[:],
                                                  s_ew[:, 2:3], None, ALU.mult),
                        lambda v: v.reduce_max(s_ew[:, 3:4], s_rt[:], axis=AX.X),
                        lambda v: v.tensor_scalar(s_rt2[:], s_rt[:],
                                                  s_ew[:, 3:4], None, ALU.is_ge),
                        lambda v: v.scalar_tensor_tensor(s_rt2[:], s_rt2[:],
                                                         -1e30, s_rt[:],
                                                         ALU.mult, ALU.add),
                        lambda v: v.reduce_max(s_ew[:, 5:6], s_rt2[:], axis=AX.X),
                        lambda v: v.tensor_scalar(s_rt2[:], s_rt[:],
                                                  s_ew[:, 5:6], None, ALU.is_ge),
                        lambda v: v.tensor_mul(s_rt2[:], s_rt2[:], s_rt[:])])
                    tr = step(block.tensor, sPE,
                              [(sDV.h, pr), evg.get("m2", (None, 0))],
                              lambda te: te.matmul(ps_m2[0:8, 0:128], s_rt2[:],
                                                   s_id[:], start=True,
                                                   stop=True))
                    ev = step(block.vector, sDV, [(sPE.h, tr)],
                              lambda v, tc=tc: v.tensor_copy(
                                  s_ewr[:, tc * 128:(tc + 1) * 128],
                                  ps_m2[0:8, 0:128]))
                    evg["m2"] = (sDV.h, ev)
                ew_done = sDV.n

                rmm = step(block.tensor, sPE, [evg.get("m", (None, 0))],
                           lambda te: [
                               te.matmul(ps_m[:],
                                         s_ag[:, k * 128:(k + 1) * 128],
                                         s_xn[:, k * 512:(k + 1) * 512],
                                         start=(k == 0), stop=(k == NCH - 1))
                               for k in range(NCH)][-1])
                rg = step(block.vector, sDV, [(sPE.h, rmm)],
                          lambda v: v.tensor_scalar(s_rg[:], ps_m[:],
                                                    MOE_SCALING, None, ALU.mult))
                evg["m"] = (sDV.h, rg)
                rmm2 = step(block.tensor, sPE, [evg["m"]],
                            lambda te: [
                                te.matmul(ps_m[:],
                                          s_au[:, k * 128:(k + 1) * 128],
                                          s_xn[:, k * 512:(k + 1) * 512],
                                          start=(k == 0), stop=(k == NCH - 1))
                                for k in range(NCH)][-1])
                ru = step(block.vector, sDV, [(sPE.h, rmm2)],
                          lambda v: v.tensor_scalar(s_ru[:], ps_m[:],
                                                    MOE_SCALING, None, ALU.mult))
                evg["m"] = (sDV.h, ru)
                rdl = ru
                for e in range(8):
                    rdl = step(block.vector, sDV, [(sDV.h, rdl)],
                               lambda v, e=e: v.tensor_scalar(
                                   s_rgd[:, e * 512:(e + 1) * 512], s_rg[:],
                                   s_msk[:, 8 + e:9 + e], None, ALU.mult))
                    rdl = step(block.vector, sDV, [(sDV.h, rdl)],
                               lambda v, e=e: v.tensor_scalar(
                                   s_rud[:, e * 512:(e + 1) * 512], s_ru[:],
                                   s_msk[:, 8 + e:9 + e], None, ALU.mult))
                r_done = rdl

                rd_init = [False] * 8
                for mg in range(NMG):
                    @block.gpsimd
                    def _(g, mg=mg):
                        g.wait_ge(sPE.h, wgu_guard[0])
                        g.dma_start(out=sb3(s_wg[:], 512),
                                    in_=fmload(wgT[:, mg * 512:(mg + 1) * 512],
                                               512)).then_inc(sW.h, 16)
                        g.dma_start(out=sb3(s_wu[:], 512),
                                    in_=fmload(wuT[:, mg * 512:(mg + 1) * 512],
                                               512)).then_inc(sW.h, 16)
                        g.dma_start(out=s_bg[:],
                                    in_=bgT[:, mg * 512:(mg + 1) * 512]
                                    ).then_inc(sW.h, 16)
                        g.dma_start(out=s_bu[:],
                                    in_=buT[:, mg * 512:(mg + 1) * 512]
                                    ).then_inc(sW.h, 16)
                        g.dma_start(out=s_ad[:].rearrange("p (c m) -> p c m", m=128),
                                    in_=adT[mg * 512:(mg + 1) * 512, :].rearrange(
                                        "(c p) m -> p c m", p=128)
                                    ).then_inc(sW.h, 16)
                        sW.n += 80

                    wld = sW.n
                    for mc in range(4):
                        mgc = mg * 4 + mc
                        gmm = step(block.tensor, sPE,
                                   [(sW.h, wld), (sDV.h, r_done),
                                    evg.get("gps", (None, 0))],
                                   lambda te, mc=mc: [
                                       te.matmul(ps_g[:],
                                                 s_wg[:, k * 512 + mc * 128:
                                                      k * 512 + (mc + 1) * 128],
                                                 s_xn[:, k * 512:(k + 1) * 512],
                                                 start=(k == 0),
                                                 stop=(k == NCH - 1))
                                       for k in range(NCH)][-1])
                        umm = step(block.tensor, sPE,
                                   [evg.get("ups", (None, 0))],
                                   lambda te, mc=mc: [
                                       te.matmul(ps_u[:],
                                                 s_wu[:, k * 512 + mc * 128:
                                                      k * 512 + (mc + 1) * 128],
                                                 s_xn[:, k * 512:(k + 1) * 512],
                                                 start=(k == 0),
                                                 stop=(k == NCH - 1))
                                       for k in range(NCH)][-1])
                        sgb = step(block.scalar, sAC,
                                   [(sPE.h, gmm), evg.get("sg", (None, 0))],
                                   lambda sc: sc.activation(s_sg[:], ps_g[:],
                                                            AF.Silu))
                        hbb = step(block.vector, sDV,
                                   [(sAC.h, sgb), (sPE.h, umm),
                                    (sST.h, guard["h3"])],
                                   lambda v: v.tensor_mul(s_hb[:], ps_u[:],
                                                          s_sg[:]))
                        evg["sg"] = (sDV.h, hbb)

                        @block.sync
                        def _(s, t=t, mgc=mgc, n=hbb):
                            s.wait_ge(sDV.h, n)
                            s.dma_start(
                                out=hb_dram[mgc * 128:(mgc + 1) * 128,
                                            t * 512:(t + 1) * 512],
                                in_=s_hb[:]).then_inc(sST.h, 16)
                            sST.n += 16

                        guard["h3"] = sST.n

                        g_reader = (sAC.h, sgb)
                        u_reader = (sDV.h, hbb)
                        for e in range(8):
                            gdm = step(block.tensor, sPE, [g_reader],
                                       lambda te, e=e, mc=mc: te.matmul(
                                           ps_g[:],
                                           s_bg[:, mc * 128:(mc + 1) * 128],
                                           s_rgd[:, e * 512:(e + 1) * 512],
                                           start=False, stop=True,
                                           skip_group_check=True))
                            udm = step(block.tensor, sPE, [u_reader],
                                       lambda te, e=e, mc=mc: te.matmul(
                                           ps_u[:],
                                           s_bu[:, mc * 128:(mc + 1) * 128],
                                           s_rud[:, e * 512:(e + 1) * 512],
                                           start=False, stop=True,
                                           skip_group_check=True))
                            sge = step(block.scalar, sAC,
                                       [(sPE.h, gdm), evg.get("sg", (None, 0))],
                                       lambda sc: sc.activation(s_sg[:], ps_g[:],
                                                                AF.Silu))
                            hbe = step(block.vector, sDV,
                                       [(sAC.h, sge), (sPE.h, udm),
                                        (sST.h, guard["h3"]) if e == 0
                                        else (None, 0)],
                                       lambda v: v.tensor_mul(s_hb[:], ps_u[:],
                                                              s_sg[:]))
                            evg["sg"] = (sDV.h, hbe)
                            g_reader = (sAC.h, sge)
                            u_reader = (sDV.h, hbe)
                            par = (mgc * 8 + e) % 2
                            prd = ps_r[par]
                            amm = step(block.tensor, sPE,
                                       [(sDV.h, hbe),
                                        evg.get(("rd", par), (None, 0))],
                                       lambda te, e=e, mc=mc, prd=prd:
                                       te.matmul(
                                           prd[:],
                                           s_ad[:, mc * 128 + e * 16:
                                                mc * 128 + e * 16 + 16],
                                           s_hb[:], start=True, stop=True))
                            if not rd_init[e]:
                                rdev = step(block.vector, sDV, [(sPE.h, amm)],
                                            lambda v, e=e, prd=prd:
                                            v.tensor_copy(s_rd[e][:], prd[:]))
                                rd_init[e] = True
                            else:
                                rdev = step(block.vector, sDV, [(sPE.h, amm)],
                                            lambda v, e=e, prd=prd:
                                            v.tensor_add(s_rd[e][:],
                                                         s_rd[e][:], prd[:]))
                            evg[("rd", par)] = (sDV.h, rdev)
                        evg["gps"] = g_reader
                        evg["ups"] = u_reader
                    wgu_guard[0] = sPE.n

                stk = step(block.tensor, sPE,
                           [evg.get(("rd", 0), (None, 0)),
                            evg.get(("rd", 1), (None, 0)),
                            evg.get("m", (None, 0))],
                           lambda te: [
                               te.matmul(ps_m[:],
                                         s_sel[:, e * 128:(e + 1) * 128],
                                         s_rd[e][:], start=(e == 0),
                                         stop=(e == 7))
                               for e in range(8)][-1])
                evg[("rd", 0)] = (sPE.h, stk)
                evg[("rd", 1)] = (sPE.h, stk)
                rda = step(block.vector, sDV, [(sPE.h, stk)],
                           lambda v: v.tensor_copy(s_sq[:], ps_m[:]))
                evg["m"] = (sDV.h, rda)
                web = step(block.tensor, sPE,
                           [(sDV.h, ew_done), evg.get("m2", (None, 0))],
                           lambda te: te.matmul(ps_m2[:], s_rep[:], s_ewr[:],
                                                start=True, stop=True))
                rdb = step(block.vector, sDV, [(sPE.h, web), (sDV.h, rda)],
                           lambda v: v.tensor_mul(s_rdb[:], ps_m2[:], s_sq[:]))
                evg["m2"] = (sDV.h, rdb)

                at_guard = {}
                kg_first = [1]
                for kg in range(6):
                    lo = kg * 1024
                    nch_k = 8 if kg < 5 else 4

                    @block.gpsimd
                    def _(g, lo=lo, nch_k=nch_k):
                        g.wait_ge(sPE.h, wd_guard[0])
                        g.dma_start(
                            out=sb3(s_wd[:, 0:nch_k * D], D),
                            in_=fmload(wdT[lo:lo + nch_k * 128, :], D)
                            ).then_inc(sW.h, 16)
                        sW.n += 16

                    wdld = sW.n

                    @block.sync
                    def _(s, t=t, lo=lo, nch_k=nch_k):
                        s.wait_ge(sPE.h, wd_guard[0])
                        if kg_first[0]:
                            s.wait_ge(sST.h, guard["h3"])
                            kg_first[0] = 0
                        s.dma_start(
                            out=sb3(s_h3[:, 0:nch_k * 512], 512),
                            in_=fmload(hb_dram[lo:lo + nch_k * 128,
                                               t * 512:(t + 1) * 512], 512)
                            ).then_inc(sX.h, 16)
                        sX.n += 16

                    h3ld = sX.n
                    for c in range(NCH):
                        pd = ps_m if c % 2 == 0 else ps_m2
                        pdn = "m" if c % 2 == 0 else "m2"

                        def emit_down(te, c=c, pd=pd, kg=kg, nch_k=nch_k):
                            inst = None
                            for k in range(nch_k):
                                inst = te.matmul(
                                    pd[:],
                                    s_wd[:, k * D + c * 128:k * D + (c + 1) * 128],
                                    s_h3[:, k * 512:(k + 1) * 512],
                                    start=(k == 0),
                                    stop=(kg != 5 and k == nch_k - 1))
                            if kg == 5:
                                inst = te.matmul(pd[:],
                                                 s_bd[:, c * 128:(c + 1) * 128],
                                                 s_rdb[:], start=False, stop=True)
                            return inst

                        waits = [(sW.h, wdld), (sX.h, h3ld),
                                 evg.get(pdn, (None, 0))]
                        if kg == 5:
                            waits.append((sDV.h, rdb))
                        mm = step(block.tensor, sPE, waits, emit_down)
                        if kg == 0:
                            ac = step(block.vector, sDV, [(sPE.h, mm)],
                                      lambda v, c=c, pd=pd: v.tensor_copy(
                                          s_x2[:, c * 512:(c + 1) * 512], pd[:]))
                            evg[pdn] = (sDV.h, ac)
                        elif kg < 5:
                            ac = step(block.vector, sDV, [(sPE.h, mm)],
                                      lambda v, c=c, pd=pd: v.tensor_add(
                                          s_x2[:, c * 512:(c + 1) * 512],
                                          s_x2[:, c * 512:(c + 1) * 512], pd[:]))
                            evg[pdn] = (sDV.h, ac)
                        else:
                            @block.sync
                            def _(s, t=t, c=c):
                                sem, val = at_guard.get(0, (None, 0))
                                if val:
                                    s.wait_ge(sem, val)
                                s.dma_start(
                                    out=s_at[:],
                                    in_=at_in[c * 128:(c + 1) * 128,
                                              t * 512:(t + 1) * 512]
                                    ).then_inc(sX.h, 16)
                                s.dma_start(
                                    out=s_xf[:],
                                    in_=x_in[c * 128:(c + 1) * 128,
                                             t * 512:(t + 1) * 512]
                                    ).then_inc(sX.h, 16)
                                sX.n += 32

                            cld = sX.n
                            ac = stepseq(block.vector, sDV,
                                         [(sPE.h, mm), (sX.h, cld)], [
                                lambda v, c=c, pd=pd: v.tensor_add(
                                    s_x2[:, c * 512:(c + 1) * 512],
                                    s_x2[:, c * 512:(c + 1) * 512], pd[:]),
                                lambda v, c=c: v.tensor_add(
                                    s_x2[:, c * 512:(c + 1) * 512],
                                    s_x2[:, c * 512:(c + 1) * 512], s_at[:]),
                                lambda v, c=c: v.tensor_add(
                                    s_x2[:, c * 512:(c + 1) * 512],
                                    s_x2[:, c * 512:(c + 1) * 512], s_xf[:])])
                            at_guard[0] = (sDV.h, ac)
                            evg[pdn] = (sDV.h, ac)
                            sq = step(block.scalar, sAC,
                                      [(sDV.h, ac), evg.get("sq", (None, 0))],
                                      lambda sc, c=c: sc.activation(
                                          s_sq[:],
                                          s_x2[:, c * 512:(c + 1) * 512],
                                          AF.Square))
                            cs = step(block.tensor, sPE,
                                      [(sAC.h, sq),
                                       (sDV.h, guard["stats"]) if c == 0
                                       else (None, 0)],
                                      lambda te, c=c: (
                                          te.matmul(ps_s1[:], ones_sb[:],
                                                    s_x2[:, c * 512:(c + 1) * 512],
                                                    start=(c == 0),
                                                    stop=(c == NCH - 1)),
                                          te.matmul(ps_s2[:], ones_sb[:], s_sq[:],
                                                    start=(c == 0),
                                                    stop=(c == NCH - 1)))[-1])
                            evg["sq"] = (sPE.h, cs)
                    wd_guard[0] = sPE.n
                stats_mm = sPE.n
                x2_done = sDV.n

                @block.sync
                def _(s, t=t, n=x2_done):
                    s.wait_ge(sDV.h, n)
                    s.dma_start(out=fmload(o_x2[:, t * 512:(t + 1) * 512], 512),
                                in_=sb3(s_x2[:], 512)).then_inc(sST.h, 16)
                    sST.n += 16

                st1 = stepseq(block.vector, sDV, [(sPE.h, stats_mm)], [
                    lambda v: v.tensor_scalar(s_row[0:1, 0:512], ps_s1[:],
                                              1.0 / D, None, ALU.mult),
                    lambda v: v.tensor_scalar(s_row[0:1, 512:1024], ps_s2[:],
                                              1.0 / D, None, ALU.mult),
                    lambda v: v.tensor_mul(s_sq[0:1, 0:512], s_row[0:1, 0:512],
                                           s_row[0:1, 0:512]),
                    lambda v: v.scalar_tensor_tensor(
                        s_row[0:1, 512:1024], s_row[0:1, 512:1024], EPS_LN,
                        s_sq[0:1, 0:512], ALU.add, ALU.subtract)])
                st2 = step(block.scalar, sAC, [(sDV.h, st1)],
                           lambda sc: sc.activation(s_row[0:1, 512:1024],
                                                    s_row[0:1, 512:1024],
                                                    AF.Sqrt))
                st3 = stepseq(block.vector, sDV, [(sAC.h, st2)], [
                    lambda v: v.reciprocal(s_row[0:1, 512:1024],
                                           s_row[0:1, 512:1024]),
                    lambda v: v.scalar_tensor_tensor(
                        s_row[0:1, 0:512], s_row[0:1, 0:512], -1.0,
                        s_row[0:1, 512:1024], ALU.mult, ALU.mult)])
                guard["stats"] = st3
                bca = step(block.tensor, sPE,
                           [(sDV.h, st3), evg.get("m", (None, 0))],
                           lambda te: te.matmul(ps_m[:], ones_rsb[:],
                                                s_row[0:1, 512:1024],
                                                start=True, stop=True))
                bcb = step(block.tensor, sPE, [evg.get("m2", (None, 0))],
                           lambda te: te.matmul(ps_m2[:], ones_rsb[:],
                                                s_row[0:1, 0:512],
                                                start=True, stop=True))
                xnl = st3
                for c in range(NCH):
                    xnl = stepseq(block.vector, sDV, [(sPE.h, bcb)], [
                        lambda v, c=c: v.tensor_mul(
                            s_sq[:], s_x2[:, c * 512:(c + 1) * 512], ps_m[:]),
                        lambda v, c=c: v.tensor_add(
                            s_xn[:, c * 512:(c + 1) * 512], s_sq[:], ps_m2[:])])
                evg["m"] = (sDV.h, xnl)
                evg["m2"] = (sDV.h, xnl)

                @block.sync
                def _(s, t=t, n=xnl):
                    s.wait_ge(sDV.h, n)
                    s.dma_start(out=fmload(o_xn1[:, t * 512:(t + 1) * 512], 512),
                                in_=sb3(s_xn[:], 512)).then_inc(sST.h, 16)
                    sST.n += 16

                guard["xn_st"] = sST.n
                guard["xn_pe"] = sPE.n

            @block.sync
            def _(s):
                s.wait_ge(sST.h, sST.n)

    return nc


def build_l5(fz=None):
    pre = "" if fz is None else "z5_"
    if fz is None:
        nc = bass.Bass("TRN2", target_bir_lowering=False,
                       detect_race_conditions=False)
        xn_in = nc.dram_tensor("xn1_bf", [D, TOK], BF16,
                               kind="ExternalInput").ap()
        o_at = nc.dram_tensor("at1", [D, TOK], F32, kind="ExternalOutput").ap()
    else:
        nc = fz.nc
        xn_in = fz.dram["xn1"]
        o_at = fz.dram["at1"]
    wqkvT = nc.dram_tensor("wqkvT", [D, 3 * D], F32, kind="ExternalInput").ap()
    wo1T = nc.dram_tensor("wo1T", [D, D], F32, kind="ExternalInput").ap()
    csel5 = nc.dram_tensor("csel5", [128, 16 * 128], F32, kind="ExternalInput").ap()
    cbias5 = nc.dram_tensor("cbias5", [16, 4], F32, kind="ExternalInput").ap()
    ao_dram = nc.dram_tensor("ao_scr", [D, TOK], BF16, kind="Internal").ap()

    ident_c = nc.inline_tensor(np.eye(128, dtype=np.float32),
                               pre + "l5_id").ap()
    ones_row = nc.inline_tensor(np.ones((1, 128), np.float32),
                                pre + "l5_or").ap()

    with ExitStack() as ctx:
        def sb(name, shape, dtype=F32, c=None):
            return (c or ctx).enter_context(nc.sbuf_tensor(pre + name, shape, dtype))

        def ps(name, shape, dtype=F32):
            return ctx.enter_context(nc.psum_tensor(pre + name, shape, dtype))

        s_w = sb("s_w", [128, NCH * 1536], BF16)
        s_xn = sb("s_xn", [128, NCH * 512], BF16)
        s_hin = sb("s_hin", [128, NCH * 512], BF16)
        s_n1 = sb("s_n1", [128, NCH])
        s_q = sb("s_q", [128, 16 * 512], BF16)
        s_k = sb("s_k", [128, 16 * 512], BF16)
        s_v = sb("s_v", [128, 16 * 512], BF16)
        s_p = sb("s_p", [128, 512], BF16)
        s_sm = sb("s_sm", [16, 4 * 512])
        s_mx = sb("s_mx", [16, 2 * 512])
        s_sel = sb("s_sel", [128, 16 * 128], BF16)
        s_bia = sb("s_bia", [16, 4])
        s_id = sb("s_id", [128, 128])
        s_or = sb("s_or", [1, 128])
        s_wr = sb("s_wr", [1, 512])
        s_acm = sb("s_acm", [128, 512])
        s_tmp = sb("s_tmp", [128, 512])
        s_ao = sb("s_ao", [128, 512], BF16)
        s_aot = sb("s_aot", [128, NCH * 512], BF16)
        s_o = sb("s_o", [128, NCH * 512])

        ps_q = [ps("ps_q0", [128, 512]), ps("ps_q1", [128, 512])]
        ps_s = [ps(f"ps_s{m}", [128, 512]) for m in range(4)]
        ps_w = ps("ps_w", [1, 512])
        ps_b = ps("ps_b", [128, 512])

        if fz is None:
            sW = Ctr(ctx.enter_context(nc.semaphore("sW")))
            sX = Ctr(ctx.enter_context(nc.semaphore("sX")))
            sPE = Ctr(ctx.enter_context(nc.semaphore("sPE")))
            sDV = Ctr(ctx.enter_context(nc.semaphore("sDV")))
            sAC = Ctr(ctx.enter_context(nc.semaphore("sAC")))
            sST = Ctr(ctx.enter_context(nc.semaphore("sST")))
        else:
            sW, sX, sPE, sDV, sAC, sST = (fz.sW, fz.sX, fz.sPE, fz.sDV,
                                          fz.sAC, fz.sST)

        with (nc.Block() if fz is None else nullcontext(fz.block)) as block:
            step, stepseq = mk_steps(block)

            @block.gpsimd
            def _(g):
                g.dma_start(out=s_sel[:], in_=csel5).then_inc(sW.h, 16)
                sW.n += 16

            @block.sync
            def _(s):
                s.dma_start(out=s_bia[:], in_=cbias5).then_inc(sX.h, 16)
                s.dma_start(out=s_id[:], in_=ident_c).then_inc(sX.h, 16)
                s.dma_start(out=s_or[:], in_=ones_row).then_inc(sX.h, 16)
                sX.n += 48

            const_ld = sX.n
            sel_ld = sW.n
            evg = {}
            guard = {"w": 0, "qkv": 0, "xn": 0}

            for hg in range(4):
                @block.gpsimd
                def _(g, hg=hg):
                    g.wait_ge(sPE.h, guard["w"])
                    for j, base in enumerate((0, D, 2 * D)):
                        g.dma_start(
                            out=sb3(s_w[:, j * NCH * 512:(j + 1) * NCH * 512], 512),
                            in_=fmload(wqkvT[:, base + hg * 512:
                                             base + (hg + 1) * 512], 512)
                            ).then_inc(sW.h, 16)
                    sW.n += 48

                wld = sW.n
                for l in range(4):
                    @block.sync
                    def _(s, l=l, hg=hg):
                        if guard["xn"]:
                            s.wait_ge(sPE.h, guard["xn"])
                        s.dma_start(out=sb3(s_xn[:], 512),
                                    in_=fmload(xn_in[:, l * 512:(l + 1) * 512],
                                               512)).then_inc(sX.h, 16)
                        sX.n += 16

                    xld = sX.n
                    for j, dst in enumerate((s_q, s_k, s_v)):
                        for hl in range(4):
                            pq = ps_q[hl % 2]
                            mm = step(block.tensor, sPE,
                                      [(sW.h, wld), (sX.h, xld),
                                       evg.get(("q", hl % 2), (None, 0))],
                                      lambda te, j=j, hl=hl, pq=pq: [
                                          te.matmul(
                                              pq[:],
                                              s_w[:, (j * NCH + k) * 512 +
                                                  hl * 128:(j * NCH + k) * 512 +
                                                  (hl + 1) * 128],
                                              s_xn[:, k * 512:(k + 1) * 512],
                                              start=(k == 0), stop=(k == NCH - 1))
                                          for k in range(NCH)][-1])
                            ev = step(block.vector, sDV, [(sPE.h, mm)],
                                      lambda v, dst=dst, hl=hl, l=l, pq=pq:
                                      v.tensor_copy(
                                          dst[:, (hl * 4 + l) * 512:
                                              (hl * 4 + l + 1) * 512], pq[:]))
                            evg[("q", hl % 2)] = (sDV.h, ev)
                    guard["xn"] = sPE.n
                qkv_done = sDV.n

                for m in range(4):
                    first = [True]
                    for l in range(m, 4):
                        for hl in range(4):
                            r = hl * 4 + l
                            pt = step(block.vector, sDV,
                                      [(sDV.h, qkv_done),
                                       evg.get("p", (None, 0))],
                                      lambda v, hl=hl, l=l, m=m: v.tensor_mul(
                                          s_p[:],
                                          s_q[:, (hl * 4 + l) * 512:
                                              (hl * 4 + l + 1) * 512],
                                          s_k[:, (hl * 4 + m) * 512:
                                              (hl * 4 + m + 1) * 512]))
                            sm = step(block.tensor, sPE,
                                      [(sDV.h, pt), (sW.h, sel_ld),
                                       evg.get(("s", m), (None, 0))
                                       if first[0] else (None, 0)],
                                      lambda te, m=m, r=r, first=first[0]:
                                      te.matmul(ps_s[m][:],
                                                s_sel[:, r * 128:(r + 1) * 128],
                                                s_p[:], start=first,
                                                stop=(r == 15)))
                            first[0] = False
                            evg["p"] = (sPE.h, sm)
                sc_done = sPE.n

                sm1 = stepseq(block.vector, sDV, [(sPE.h, sc_done)], [
                    lambda v: v.tensor_copy(
                        sb3(s_sm[:], 512),
                        ps_stack[:] if False else _smview(ps_s)),
                ]) if False else None
                ev0 = 0
                for m in range(4):
                    ev0 = step(block.vector, sDV,
                               [(sPE.h, sc_done), (sX.h, const_ld)],
                               lambda v, m=m: v.tensor_scalar(
                                   s_sm[:, m * 512:(m + 1) * 512],
                                   ps_s[m][0:16, :], s_bia[:, m:m + 1],
                                   None, ALU.add))
                    evg[("s", m)] = (sDV.h, ev0)
                smax = stepseq(block.vector, sDV, [(sDV.h, ev0)], [
                    lambda v: v.tensor_max(s_mx[:, 0:512], s_sm[:, 0:512],
                                           s_sm[:, 512:1024]),
                    lambda v: v.tensor_max(s_mx[:, 512:1024],
                                           s_sm[:, 1024:1536],
                                           s_sm[:, 1536:2048]),
                    lambda v: v.tensor_max(s_mx[:, 0:512], s_mx[:, 0:512],
                                           s_mx[:, 512:1024])])
                sub = smax
                for m in range(4):
                    sub = step(block.vector, sDV, [(sDV.h, sub)],
                               lambda v, m=m: v.tensor_sub(
                                   s_sm[:, m * 512:(m + 1) * 512],
                                   s_sm[:, m * 512:(m + 1) * 512],
                                   s_mx[:, 0:512]))
                ex = step(block.scalar, sAC, [(sDV.h, sub)],
                          lambda sc: sc.activation(s_sm[:], s_sm[:], AF.Exp))
                sden = stepseq(block.vector, sDV, [(sAC.h, ex)], [
                    lambda v: v.tensor_add(s_mx[:, 0:512], s_sm[:, 0:512],
                                           s_sm[:, 512:1024]),
                    lambda v: v.tensor_add(s_mx[:, 512:1024],
                                           s_sm[:, 1024:1536],
                                           s_sm[:, 1536:2048]),
                    lambda v: v.tensor_add(s_mx[:, 0:512], s_mx[:, 0:512],
                                           s_mx[:, 512:1024]),
                    lambda v: v.reciprocal(s_mx[:, 512:1024], s_mx[:, 0:512])])
                wnorm = sden
                for m in range(4):
                    wnorm = step(block.vector, sDV, [(sDV.h, wnorm)],
                                 lambda v, m=m: v.tensor_mul(
                                     s_sm[:, m * 512:(m + 1) * 512],
                                     s_sm[:, m * 512:(m + 1) * 512],
                                     s_mx[:, 512:1024]))

                for l in range(4):
                    for hl in range(4):
                        nterm = 0
                        for m in range(l + 1):
                            r = hl * 4 + l
                            wx = step(block.tensor, sPE,
                                      [(sDV.h, wnorm), evg.get("w", (None, 0))],
                                      lambda te, r=r, m=m: te.matmul(
                                          ps_w[:], s_id[0:16, r:r + 1],
                                          s_sm[0:16, m * 512:(m + 1) * 512],
                                          start=True, stop=True))
                            we = step(block.vector, sDV, [(sPE.h, wx)],
                                      lambda v: v.tensor_copy(s_wr[:], ps_w[:]))
                            evg["w"] = (sDV.h, we)
                            bc = step(block.tensor, sPE,
                                      [(sDV.h, we), evg.get("b", (None, 0))],
                                      lambda te: te.matmul(ps_b[:], s_or[:],
                                                           s_wr[:], start=True,
                                                           stop=True))
                            if m == 0:
                                mac = step(block.vector, sDV, [(sPE.h, bc)],
                                           lambda v, hl=hl, m=m: v.tensor_mul(
                                               s_acm[:], ps_b[:],
                                               s_v[:, (hl * 4 + m) * 512:
                                                   (hl * 4 + m + 1) * 512]))
                            else:
                                mac = stepseq(block.vector, sDV, [(sPE.h, bc)], [
                                    lambda v, hl=hl, m=m: v.tensor_mul(
                                        s_tmp[:], ps_b[:],
                                        s_v[:, (hl * 4 + m) * 512:
                                            (hl * 4 + m + 1) * 512]),
                                    lambda v: v.tensor_add(s_acm[:], s_acm[:],
                                                           s_tmp[:])])
                            evg["b"] = (sDV.h, mac)
                        fin = step(block.vector, sDV, [(sDV.h, mac)],
                                   lambda v: v.tensor_copy(s_ao[:], s_acm[:]))

                        @block.sync
                        def _(s, hg=hg, hl=hl, l=l, n=fin):
                            s.wait_ge(sDV.h, n)
                            s.dma_start(
                                out=ao_dram[(hg * 4 + hl) * 128:
                                            (hg * 4 + hl + 1) * 128,
                                            l * 512:(l + 1) * 512],
                                in_=s_ao[:]).then_inc(sST.h, 16)
                            sST.n += 16

                        evg["ao"] = (sST.h, sST.n)
                guard["w"] = sPE.n

            ao_done = sST.n

            @block.gpsimd
            def _(g):
                g.wait_ge(sPE.h, guard["w"])
                g.dma_start(out=sb3(s_w[:, 0:NCH * 512], 512),
                            in_=fmload(wo1T[:, 0:512], 512)).then_inc(sW.h, 16)
                g.dma_start(out=sb3(s_w[:, NCH * 512:2 * NCH * 512], 512),
                            in_=fmload(wo1T[:, 512:1024], 512)).then_inc(sW.h, 16)
                g.dma_start(out=sb3(s_w[:, 2 * NCH * 512:3 * NCH * 512], 512),
                            in_=fmload(wo1T[:, 1024:1536], 512)).then_inc(sW.h, 16)
                sW.n += 48

            _dv_now = sDV.n

            @block.gpsimd
            def _(g):
                g.wait_ge(sDV.h, _dv_now)
                g.dma_start(out=sb3(s_q[:], 512),
                            in_=fmload(wo1T[:, 1536:2048], 512)).then_inc(sW.h, 16)
                sW.n += 16

            wld = sW.n
            for l in range(4):
                @block.sync
                def _(s, l=l):
                    s.wait_ge(sST.h, ao_done)
                    if guard.get("aot", 0):
                        s.wait_ge(sPE.h, guard["aot"])
                    s.dma_start(out=sb3(s_aot[:], 512),
                                in_=fmload(ao_dram[:, l * 512:(l + 1) * 512],
                                           512)).then_inc(sX.h, 16)
                    sX.n += 16

                ald = sX.n
                for c in range(NCH):
                    pq = ps_q[c % 2]
                    wsrc = (lambda k, c: s_w[:, (c // 4) * NCH * 512 + k * 512 +
                                             (c % 4) * 128:
                                             (c // 4) * NCH * 512 + k * 512 +
                                             (c % 4) * 128 + 128])                         if c < 12 else                         (lambda k, c: s_q[:, k * 512 + (c % 4) * 128:
                                          k * 512 + (c % 4) * 128 + 128])
                    mm = step(block.tensor, sPE,
                              [(sW.h, wld), (sX.h, ald),
                               evg.get(("q", c % 2), (None, 0))],
                              lambda te, c=c, pq=pq, wsrc=wsrc: [
                                  te.matmul(pq[:], wsrc(k, c),
                                            s_aot[:, k * 512:(k + 1) * 512],
                                            start=(k == 0), stop=(k == NCH - 1))
                                  for k in range(NCH)][-1])
                    ev = step(block.vector, sDV, [(sPE.h, mm)],
                              lambda v, c=c, pq=pq: v.tensor_copy(
                                  s_o[:, c * 512:(c + 1) * 512], pq[:]))
                    evg[("q", c % 2)] = (sDV.h, ev)
                guard["aot"] = sPE.n

                @block.sync
                def _(s, l=l, n=sDV.n):
                    s.wait_ge(sDV.h, n)
                    s.dma_start(out=fmload(o_at[:, l * 512:(l + 1) * 512], 512),
                                in_=sb3(s_o[:], 512)).then_inc(sST.h, 16)
                    sST.n += 16

            @block.sync
            def _(s):
                s.wait_ge(sST.h, sST.n)

    return nc


def build_l6(fz=None):
    pre = "" if fz is None else "z6_"
    if fz is None:
        nc = bass.Bass("TRN2", target_bir_lowering=False,
                       detect_race_conditions=False)
        xn_in = nc.dram_tensor("xn1_bf", [D, TOK], BF16,
                               kind="ExternalInput").ap()
        x2_in = nc.dram_tensor("x2", [D, TOK], F32, kind="ExternalInput").ap()
        at_in = nc.dram_tensor("at1", [D, TOK], F32, kind="ExternalInput").ap()
        o_out = nc.dram_tensor("out_fm", [D, TOK], F32,
                               kind="ExternalOutput").ap()
    else:
        nc = fz.nc
        xn_in = fz.dram["xn1"]
        x2_in = fz.dram["x2"]
        at_in = fz.dram["at1"]
        o_out = fz.dram["out_fm"]
    w1g = nc.dram_tensor("w1g", [D, FF1], F32, kind="ExternalInput").ap()
    w1u = nc.dram_tensor("w1u", [D, FF1], F32, kind="ExternalInput").ap()
    w1d = nc.dram_tensor("w1d", [FF1, D], F32, kind="ExternalInput").ap()
    h_dram = nc.dram_tensor("h6_scr", [FF1, TOK], BF16, kind="Internal").ap()

    with ExitStack() as ctx:
        def sb(name, shape, dtype=F32, c=None):
            return (c or ctx).enter_context(nc.sbuf_tensor(pre + name, shape, dtype))

        def ps(name, shape, dtype=F32):
            return ctx.enter_context(nc.psum_tensor(pre + name, shape, dtype))

        s_g = sb("s_g", [128, 512], BF16)
        s_hb = sb("s_hb", [128, 512], BF16)
        cA = ExitStack()
        s_w0 = sb("s_w0", [128, NCH * 512], BF16, cA)
        s_w1 = sb("s_w1", [128, NCH * 512], BF16, cA)
        s_xn = sb("s_xn", [128, NCH * 512], BF16, cA)

        ps_a = ps("ps_a", [128, 512])
        ps_b = ps("ps_b", [128, 512])

        if fz is None:
            sW = Ctr(ctx.enter_context(nc.semaphore("sW")))
            sX = Ctr(ctx.enter_context(nc.semaphore("sX")))
            sPE = Ctr(ctx.enter_context(nc.semaphore("sPE")))
            sDV = Ctr(ctx.enter_context(nc.semaphore("sDV")))
            sAC = Ctr(ctx.enter_context(nc.semaphore("sAC")))
            sPO = Ctr(ctx.enter_context(nc.semaphore("sPO")))
            sST = Ctr(ctx.enter_context(nc.semaphore("sST")))
        else:
            sW, sX, sPE, sDV, sAC, sPO, sST = (fz.sW, fz.sX, fz.sPE, fz.sDV,
                                               fz.sAC, fz.sPO, fz.sST)

        with (nc.Block() if fz is None else nullcontext(fz.block)) as block:
            step, stepseq = mk_steps(block)

            evg = {}
            guard = {"w": 0, "xn": 0, "h": 0}

            for mg in range(16):
                @block.gpsimd
                def _(g, mg=mg):
                    g.wait_ge(sPE.h, guard["w"])
                    g.dma_start(out=sb3(s_w0[:], 512),
                                in_=fmload(w1g[:, mg * 512:(mg + 1) * 512], 512)
                                ).then_inc(sW.h, 16)
                    g.dma_start(out=sb3(s_w1[:], 512),
                                in_=fmload(w1u[:, mg * 512:(mg + 1) * 512], 512)
                                ).then_inc(sW.h, 16)
                    sW.n += 32

                wld = sW.n
                for t in range(NT):
                    @block.sync
                    def _(s, t=t, mg=mg):
                        if guard["xn"]:
                            s.wait_ge(sPE.h, guard["xn"])
                        s.dma_start(out=sb3(s_xn[:], 512),
                                    in_=fmload(xn_in[:, t * 512:(t + 1) * 512],
                                               512)).then_inc(sX.h, 16)
                        sX.n += 16

                    xld = sX.n
                    for mc in range(4):
                        mmg = step(block.tensor, sPE,
                                   [(sW.h, wld), (sX.h, xld),
                                    evg.get("a", (None, 0))],
                                   lambda te, mc=mc: [
                                       te.matmul(ps_a[:],
                                                 s_w0[:, k * 512 + mc * 128:
                                                      k * 512 + (mc + 1) * 128],
                                                 s_xn[:, k * 512:(k + 1) * 512],
                                                 start=(k == 0),
                                                 stop=(k == NCH - 1))
                                       for k in range(NCH)][-1])
                        gs = step(block.scalar, sAC,
                                  [(sPE.h, mmg), evg.get("g", (None, 0))],
                                  lambda sc: sc.activation(s_g[:], ps_a[:],
                                                           AF.Silu))
                        evg["a"] = (sAC.h, gs)
                        mmu = step(block.tensor, sPE,
                                   [evg.get("b", (None, 0))],
                                   lambda te, mc=mc: [
                                       te.matmul(ps_b[:],
                                                 s_w1[:, k * 512 + mc * 128:
                                                      k * 512 + (mc + 1) * 128],
                                                 s_xn[:, k * 512:(k + 1) * 512],
                                                 start=(k == 0),
                                                 stop=(k == NCH - 1))
                                       for k in range(NCH)][-1])
                        hc = step(block.vector, sDV,
                                  [(sPE.h, mmu), (sAC.h, gs),
                                   (sST.h, guard["h"])],
                                  lambda v: v.tensor_mul(s_hb[:], ps_b[:],
                                                         s_g[:]))
                        evg["b"] = (sDV.h, hc)
                        evg["g"] = (sDV.h, hc)

                        @block.sync
                        def _(s, t=t, mg=mg, mc=mc, n=hc):
                            s.wait_ge(sDV.h, n)
                            s.dma_start(
                                out=h_dram[mg * 512 + mc * 128:
                                           mg * 512 + (mc + 1) * 128,
                                           t * 512:(t + 1) * 512],
                                in_=s_hb[:]).then_inc(sST.h, 16)
                            sST.n += 16

                        guard["h"] = sST.n
                    guard["xn"] = sPE.n
                guard["w"] = sPE.n

            cA.close()
            s_wd = sb("s_wd", [128, 8 * D], BF16)
            s_h = sb("s_h", [128, 64 * 512], BF16)
            s_sq = sb("s_sq", [128, 4 * 512])
            s_t2 = sb("s_t2", [128, 3 * 512])
            s_x2 = sb("s_x2", [128, 512])
            s_at = sb("s_at", [128, 512])
            s_o = sb("s_o", [128, NCH * 512])
            hA_done = sST.n
            at_guard = {}
            wd_guard = [0]
            hB_guard = [0]
            for t in range(NT):
                @block.sync
                def _(s, t=t):
                    s.wait_ge(sST.h, hA_done)
                    if hB_guard[0]:
                        s.wait_ge(sPE.h, hB_guard[0])
                    if guard.get("hdv", 0):
                        s.wait_ge(sPO.h, guard["hdv"])
                    s.dma_start(out=sb3(s_h[:], 512),
                                in_=fmload(h_dram[:, t * 512:(t + 1) * 512],
                                           512)).then_inc(sX.h, 16)
                    sX.n += 16

                hld = sX.n
                last_po = 0
                for c in range(16):
                    mem = [c, c + 16, c + 32, c + 48]
                    sq4 = step(block.scalar, sAC,
                               [(sX.h, hld), evg.get("sq", (None, 0))],
                               lambda sc, mem=mem: [
                                   sc.activation(s_sq[:, i * 512:(i + 1) * 512],
                                                 s_h[:, mem[i] * 512:
                                                     (mem[i] + 1) * 512],
                                                 AF.Square)
                                   for i in range(4)][-1])
                    thr = stepseq(block.vector, sDV, [(sAC.h, sq4)], [
                        lambda v: v.tensor_max(s_t2[:, 0:512], s_sq[:, 0:512],
                                               s_sq[:, 512:1024]),
                        lambda v: v.tensor_tensor(s_t2[:, 512:1024],
                                                  s_sq[:, 0:512],
                                                  s_sq[:, 512:1024], ALU.min),
                        lambda v: v.tensor_max(s_t2[:, 1024:1536],
                                               s_sq[:, 1024:1536],
                                               s_sq[:, 1536:2048])])
                    thr2 = stepseq(block.vector, sDV, [(sDV.h, thr)], [
                        lambda v: v.tensor_tensor(s_sq[:, 0:512],
                                                  s_sq[:, 1024:1536],
                                                  s_sq[:, 1536:2048], ALU.min),
                        lambda v: v.tensor_max(s_t2[:, 512:1024],
                                               s_t2[:, 512:1024],
                                               s_sq[:, 0:512]),
                        lambda v: v.tensor_tensor(s_t2[:, 0:512],
                                                  s_t2[:, 0:512],
                                                  s_t2[:, 1024:1536], ALU.min),
                        lambda v: v.tensor_max(s_t2[:, 0:512], s_t2[:, 0:512],
                                               s_t2[:, 512:1024])])
                    rq0 = step(block.scalar, sAC, [(sDV.h, thr2)],
                               lambda sc, mem=mem: sc.activation(
                                   s_sq[:, 0:512],
                                   s_h[:, mem[0] * 512:(mem[0] + 1) * 512],
                                   AF.Square))
                    msk = stepseq(block.vector, sDV, [(sAC.h, rq0)], [
                        lambda v, i=i: v.tensor_tensor(
                            s_sq[:, i * 512:(i + 1) * 512],
                            s_sq[:, i * 512:(i + 1) * 512],
                            s_t2[:, 0:512], ALU.is_ge)
                        for i in range(4)])
                    po = step(block.gpsimd, sPO, [(sDV.h, msk)],
                              lambda g, mem=mem: [
                                  g.tensor_mul(
                                      s_h[:, mem[i] * 512:(mem[i] + 1) * 512],
                                      s_h[:, mem[i] * 512:(mem[i] + 1) * 512],
                                      s_sq[:, i * 512:(i + 1) * 512])
                                  for i in range(4)][-1])
                    evg["sq"] = (sPO.h, po)
                    last_po = po
                guard["hdv"] = last_po

                for kg in range(8):
                    @block.gpsimd
                    def _(g, kg=kg):
                        g.wait_ge(sPE.h, wd_guard[0])
                        g.dma_start(
                            out=sb3(s_wd[:], D),
                            in_=fmload(w1d[kg * 1024:(kg + 1) * 1024, :], D)
                            ).then_inc(sW.h, 16)
                        sW.n += 16

                    wld = sW.n
                    for c in range(NCH):
                        pd = ps_a if c % 2 == 0 else ps_b
                        pdn = "a" if c % 2 == 0 else "b"
                        mm = step(block.tensor, sPE,
                                  [(sW.h, wld), (sPO.h, last_po),
                                   evg.get(pdn, (None, 0))],
                                  lambda te, c=c, pd=pd, kg=kg: [
                                      te.matmul(
                                          pd[:],
                                          s_wd[:, k * D + c * 128:
                                               k * D + (c + 1) * 128],
                                          s_h[:, (kg * 8 + k) * 512:
                                              (kg * 8 + k + 1) * 512],
                                          start=(k == 0), stop=(k == 7))
                                      for k in range(8)][-1])
                        if kg == 0:
                            ac = step(block.vector, sDV, [(sPE.h, mm)],
                                      lambda v, c=c, pd=pd: v.tensor_copy(
                                          s_o[:, c * 512:(c + 1) * 512], pd[:]))
                        elif kg < 7:
                            ac = step(block.vector, sDV, [(sPE.h, mm)],
                                      lambda v, c=c, pd=pd: v.tensor_add(
                                          s_o[:, c * 512:(c + 1) * 512],
                                          s_o[:, c * 512:(c + 1) * 512], pd[:]))
                        else:
                            @block.sync
                            def _(s, t=t, c=c):
                                sem, val = at_guard.get(0, (None, 0))
                                if val:
                                    s.wait_ge(sem, val)
                                s.dma_start(
                                    out=s_x2[:],
                                    in_=x2_in[c * 128:(c + 1) * 128,
                                              t * 512:(t + 1) * 512]
                                    ).then_inc(sX.h, 16)
                                s.dma_start(
                                    out=s_at[:],
                                    in_=at_in[c * 128:(c + 1) * 128,
                                              t * 512:(t + 1) * 512]
                                    ).then_inc(sX.h, 16)
                                sX.n += 32

                            cld = sX.n
                            ac = stepseq(block.vector, sDV,
                                         [(sPE.h, mm), (sX.h, cld)], [
                                lambda v, c=c, pd=pd: v.tensor_add(
                                    s_o[:, c * 512:(c + 1) * 512],
                                    s_o[:, c * 512:(c + 1) * 512], pd[:]),
                                lambda v, c=c: v.tensor_add(
                                    s_o[:, c * 512:(c + 1) * 512],
                                    s_o[:, c * 512:(c + 1) * 512], s_x2[:]),
                                lambda v, c=c: v.tensor_add(
                                    s_o[:, c * 512:(c + 1) * 512],
                                    s_o[:, c * 512:(c + 1) * 512], s_at[:])])
                            at_guard[0] = (sDV.h, ac)
                        evg[pdn] = (sDV.h, ac)
                    wd_guard[0] = sPE.n
                hB_guard[0] = sPE.n

                @block.sync
                def _(s, t=t, n=sDV.n):
                    s.wait_ge(sDV.h, n)
                    s.dma_start(out=fmload(o_out[:, t * 512:(t + 1) * 512], 512),
                                in_=sb3(s_o[:], 512)).then_inc(sST.h, 16)
                    sST.n += 16

            @block.sync
            def _(s):
                s.wait_ge(sST.h, sST.n)

    return nc


class _FZ:
    pass


def build_fused():
    nc = bass.Bass("TRN2", target_bir_lowering=False,
                   detect_race_conditions=False)
    fz = _FZ()
    fz.nc = nc
    dram = {}
    dram["x_fm"] = nc.dram_tensor("x_fm", [D, TOK], F32,
                                  kind="ExternalInput").ap()
    dram["xn_bf"] = nc.dram_tensor("xn_bf", [D, TOK], BF16, kind="Internal").ap()
    dram["hin_bf"] = nc.dram_tensor("hin_bf", [D, TOK], BF16,
                                    kind="Internal").ap()
    dram["attn_out"] = nc.dram_tensor("attn_out", [D, TOK], F32,
                                      kind="Internal").ap()
    dram["x2"] = nc.dram_tensor("x2", [D, TOK], F32, kind="Internal").ap()
    dram["xn1"] = nc.dram_tensor("xn1", [D, TOK], BF16, kind="Internal").ap()
    dram["at1"] = nc.dram_tensor("at1", [D, TOK], F32, kind="Internal").ap()
    dram["out_fm"] = nc.dram_tensor("out_fm", [D, TOK], F32,
                                    kind="ExternalOutput").ap()
    fz.dram = dram
    a2a1_in_t = nc.dram_tensor("a2a1_in", [8 * 1536, 1024], BF16,
                               kind="Internal")
    a2a1_out_t = nc.dram_tensor("a2a1_out", [8 * 1536, 1024], BF16,
                                kind="Internal")
    g1a_in_t = nc.dram_tensor("g1a_in", [8 * 512, 8], F32, kind="Internal")
    g1a_out_t = nc.dram_tensor("g1a_out", [8 * 512, 8], F32, kind="Internal")
    a2a2_in_t = nc.dram_tensor("a2a2_in", [8 * 1024, 512], BF16,
                               kind="Internal")
    a2a2_out_t = nc.dram_tensor("a2a2_out", [8 * 1024, 512], BF16,
                                kind="Internal")
    fz.a2a1_in = a2a1_in_t.ap()
    fz.a2a1_out = a2a1_out_t.ap()
    fz.g1a_in = g1a_in_t.ap()
    fz.g1a_out = g1a_out_t.ap()
    fz.a2a2_in = a2a2_in_t.ap()
    fz.a2a2_out = a2a2_out_t.ap()

    with ExitStack() as ctx:
        ctx.enter_context(allow_ici_all_to_all())
        for nm in ("sW", "sX", "sPE", "sDV", "sAC", "sST", "sPO", "sCC"):
            setattr(fz, nm, Ctr(ctx.enter_context(nc.semaphore(nm))))
        with nc.Block() as block:
            fz.block = block
            ctrs = [fz.sW, fz.sX, fz.sPE, fz.sDV, fz.sAC, fz.sST, fz.sPO,
                    fz.sCC]

            def barrier():
                for eng in (block.sync, block.tensor, block.vector,
                            block.scalar, block.gpsimd):
                    @eng
                    def _(e):
                        for ct in ctrs:
                            if ct.n:
                                e.wait_ge(ct.h, ct.n)

            build_l1(fz)
            barrier()

            @block.gpsimd
            def _(gq):
                gq.collective_compute(
                    "AllToAll", ALU.bypass,
                    replica_groups=[list(range(NCORE))],
                    ins=[a2a1_in_t.ap().opt()],
                    outs=[a2a1_out_t.ap().opt()]).then_inc(fz.sCC.h, 1)
                gq.collective_compute(
                    "AllToAll", ALU.bypass,
                    replica_groups=[list(range(NCORE))],
                    ins=[g1a_in_t.ap().opt()],
                    outs=[g1a_out_t.ap().opt()]).then_inc(fz.sCC.h, 1)

            fz.sCC.n += 2
            barrier()
            build_l2(fz)
            barrier()

            @block.gpsimd
            def _(gq):
                gq.collective_compute(
                    "AllToAll", ALU.bypass,
                    replica_groups=[list(range(NCORE))],
                    ins=[a2a2_in_t.ap().opt()],
                    outs=[a2a2_out_t.ap().opt()]).then_inc(fz.sCC.h, 1)

            fz.sCC.n += 1
            barrier()
            build_l3(fz)
            barrier()
            build_l4(fz)
            barrier()
            build_l5(fz)
            barrier()
            build_l6(fz)
    return nc


def _shard_seq(x):
    out = []
    for c in range(NCORE):
        xc = x[:, c * SC:(c + 1) * SC, :]
        out.append(np.ascontiguousarray(xc.transpose(2, 0, 1).reshape(D, TOK)))
    return out


def _unshard_seq(parts):
    out = np.empty((B, S, D), dtype=parts[0].dtype)
    for c in range(NCORE):
        out[:, c * SC:(c + 1) * SC, :] = (
            np.asarray(parts[c]).reshape(D, B, SC).transpose(1, 2, 0))
    return out


def _run(nc, in_maps):
    res = run_bass_kernel_spmd(nc, in_maps, core_ids=list(range(NCORE)))
    return res.results


def kernel_unfused(**inputs):
    f32 = np.float32
    x = np.asarray(inputs["L0_attn_x"], dtype=f32)
    g = lambda k: np.ascontiguousarray(np.asarray(inputs[k], dtype=f32))

    xs = _shard_seq(x)
    wg16 = np.ascontiguousarray(
        np.concatenate([g("L0_attn_w_ig"), g("L0_attn_w_fg")], axis=1))
    n1c = np.ascontiguousarray(g("L0_attn_norm1_w").reshape(NCH, 128).T)
    r1 = _run(build_l1(), [
        {"x_fm": xs[c], "wq": g("L0_attn_wq"), "wk": g("L0_attn_wk"),
         "wv": g("L0_attn_wv"), "wg16": wg16, "n1w": n1c}
        for c in range(NCORE)])

    QG = np.empty((1024, B, S), dtype=BF16NP)
    KG = np.empty((1024, B, S), dtype=BF16NP)
    VG = np.empty((B, S, D), dtype=BF16NP)
    GG = np.empty((B, S, 16), dtype=f32)
    XN = np.empty((D, B, S), dtype=BF16NP)
    HIN = [None] * NCORE
    for c in range(NCORE):
        sl = slice(c * SC, (c + 1) * SC)
        QG[:, :, sl] = np.asarray(r1[c]["q_fm"]).reshape(1024, B, SC)
        KG[:, :, sl] = np.asarray(r1[c]["k_fm"]).reshape(1024, B, SC)
        VG[:, sl, :] = np.asarray(r1[c]["v_tm"]).reshape(B, SC, D)
        GG[:, sl, :] = np.asarray(r1[c]["gates"]).reshape(B, SC, 16)
        XN[:, :, sl] = np.asarray(r1[c]["xn_bf"]).reshape(D, B, SC)
        HIN[c] = np.asarray(r1[c]["hin_bf"])


    mh = g("L0_attn_mh_w").reshape(H, DV)
    in2 = []
    for c in range(NCORE):
        b = c // 2
        h0 = 4 * (c % 2)
        q_bh = np.ascontiguousarray(
            QG[h0 * 128:(h0 + 4) * 128, b, :])
        k_bh = np.ascontiguousarray(
            KG[h0 * 128:(h0 + 4) * 128, b, :])
        v_bh = np.ascontiguousarray(VG[b, :, h0 * 256:(h0 + 4) * 256])
        gates_px = np.ascontiguousarray(np.concatenate(
            [GG[b, :, h0:h0 + 4], GG[b, :, 8 + h0:8 + h0 + 4]], axis=1))
        mhc = np.ascontiguousarray(
            mh[h0:h0 + 4].reshape(4, 2, 128).transpose(2, 0, 1).reshape(128, 8))
        bi = g("L0_attn_b_ig")[h0:h0 + 4]
        bf = g("L0_attn_b_fg")[h0:h0 + 4]
        gbias = np.ascontiguousarray(np.broadcast_to(
            np.concatenate([bi, bf])[None, :], (128, 8)))
        in2.append({"q_bh": q_bh, "k_bh": k_bh, "v_bh": v_bh,
                    "gates_px": gates_px, "mh": mhc, "gbias": gbias})
    r2 = _run(build_l2(), in2)

    HG = np.empty((D, B, S), dtype=BF16NP)
    for c in range(NCORE):
        b = c // 2
        h0 = 4 * (c % 2)
        HG[h0 * 256:(h0 + 4) * 256, b, :] = np.asarray(r2[c]["h_fm"])

    n2c = np.ascontiguousarray(
        g("L0_attn_norm2_w").reshape(NCH, 128).T)
    in3 = []
    for c in range(NCORE):
        sl = slice(c * SC, (c + 1) * SC)
        in3.append({
            "xn_bf": np.ascontiguousarray(XN[:, :, sl].reshape(D, TOK)),
            "hin_bf": HIN[c],
            "h_fm": np.ascontiguousarray(HG[:, :, sl].reshape(D, TOK)),
            "w_og": g("L0_attn_w_og"), "w_out": g("L0_attn_w_out"),
            "n2w": n2c, "w_gate": g("L0_attn_w_gate"),
            "w_up": g("L0_attn_w_up"), "w_down": g("L0_attn_w_down")})
    r3 = _run(build_l3(), in3)

    E, R = NEXP, RLORA
    Ag, Au = g("L0_ffn_Ag"), g("L0_ffn_Au")
    Bg, Bu = g("L0_ffn_Bg"), g("L0_ffn_Bu")
    Ad, Bd = g("L0_ffn_Ad"), g("L0_ffn_Bd")
    sel = np.zeros((16, 8 * 128), np.float32)
    for e in range(8):
        for i in range(16):
            sel[i, e * 128 + e * 16 + i] = 1.0
    rep = np.zeros((8, 128), np.float32)
    for e in range(8):
        rep[e, e * 16:(e + 1) * 16] = MOE_SCALING
    cm = np.zeros((128, 16), np.float32)
    for e in range(8):
        cm[e * 16:(e + 1) * 16, e] = 1.0
    cm[:, 8] = cm[:, 0]
    for e in range(1, 8):
        cm[:, 8 + e] = cm[:, e] - cm[:, e - 1]
    w4 = {"wgT": np.ascontiguousarray(g("L0_ffn_Wg").T),
          "wuT": np.ascontiguousarray(g("L0_ffn_Wu").T),
          "wdT": np.ascontiguousarray(g("L0_ffn_Wd").T),
          "wrT": np.ascontiguousarray(g("L0_ffn_Wr").T),
          "agT": np.ascontiguousarray(Ag.reshape(E * R, D).T),
          "auT": np.ascontiguousarray(Au.reshape(E * R, D).T),
          "bgT": np.ascontiguousarray(Bg.transpose(0, 2, 1).reshape(E * R, DFF)),
          "buT": np.ascontiguousarray(Bu.transpose(0, 2, 1).reshape(E * R, DFF)),
          "adT": np.ascontiguousarray(Ad.reshape(E * R, DFF).T),
          "bdT": np.ascontiguousarray(Bd.transpose(0, 2, 1).reshape(E * R, D)),
          "csel": sel, "crep": rep, "cmask": cm}
    in4 = []
    for c in range(NCORE):
        sl = slice(c * SC, (c + 1) * SC)
        d = {"xn_bf": in3[c]["xn_bf"],
             "attn_in": np.asarray(r3[c]["attn_out"]), "x_fm": xs[c]}
        d.update(w4)
        in4.append(d)
    r4 = _run(build_l4(), in4)

    sel5 = np.zeros((128, 16 * 128), np.float32)
    for r in range(16):
        sel5[:, r * 128 + r] = 128.0 ** (-0.5)
    bias5 = np.zeros((16, 4), np.float32)
    for hl in range(4):
        for l in range(4):
            for m in range(4):
                if l < m:
                    bias5[hl * 4 + l, m] = -1e30
    in5 = []
    for c in range(NCORE):
        in5.append({"xn1_bf": np.asarray(r4[c]["xn1"]),
                    "wqkvT": np.ascontiguousarray(g("L1_attn_weight_qkv").T),
                    "wo1T": np.ascontiguousarray(g("L1_attn_weight_out").T),
                    "csel5": sel5, "cbias5": bias5})
    r5 = _run(build_l5(), in5)

    perm = np.arange(FF1).reshape(FF1 // 4, 4).T.reshape(-1)
    w6 = {"w1g": np.ascontiguousarray(g("L1_ffn_w_gate")[:, perm]),
          "w1u": np.ascontiguousarray(g("L1_ffn_w_up")[:, perm]),
          "w1d": np.ascontiguousarray(g("L1_ffn_w_down")[perm, :])}
    in6 = []
    for c in range(NCORE):
        d = {"xn1_bf": np.asarray(r4[c]["xn1"]),
             "x2": np.asarray(r4[c]["x2"]), "at1": np.asarray(r5[c]["at1"])}
        d.update(w6)
        in6.append(d)
    r6 = _run(build_l6(), in6)

    return _unshard_seq([r6[c]["out_fm"] for c in range(NCORE)])


def kernel(**inputs):
    f32 = np.float32
    x = np.asarray(inputs["L0_attn_x"], dtype=f32)
    g = lambda k: np.ascontiguousarray(np.asarray(inputs[k], dtype=f32))

    xs = _shard_seq(x)
    wg16 = np.ascontiguousarray(
        np.concatenate([g("L0_attn_w_ig"), g("L0_attn_w_fg")], axis=1))
    n1c = np.ascontiguousarray(g("L0_attn_norm1_w").reshape(NCH, 128).T)
    n2c = np.ascontiguousarray(g("L0_attn_norm2_w").reshape(NCH, 128).T)
    mh = g("L0_attn_mh_w").reshape(H, DV)

    w3 = {"w_og": g("L0_attn_w_og"), "w_out": g("L0_attn_w_out"), "n2w": n2c,
          "w_gate": g("L0_attn_w_gate"), "w_up": g("L0_attn_w_up"),
          "w_down": g("L0_attn_w_down")}

    E, R = NEXP, RLORA
    Ag, Au = g("L0_ffn_Ag"), g("L0_ffn_Au")
    Bg, Bu = g("L0_ffn_Bg"), g("L0_ffn_Bu")
    Ad, Bd = g("L0_ffn_Ad"), g("L0_ffn_Bd")
    sel = np.zeros((16, 8 * 128), np.float32)
    for e in range(8):
        for i in range(16):
            sel[i, e * 128 + e * 16 + i] = 1.0
    rep = np.zeros((8, 128), np.float32)
    for e in range(8):
        rep[e, e * 16:(e + 1) * 16] = MOE_SCALING
    cm = np.zeros((128, 16), np.float32)
    for e in range(8):
        cm[e * 16:(e + 1) * 16, e] = 1.0
    cm[:, 8] = cm[:, 0]
    for e in range(1, 8):
        cm[:, 8 + e] = cm[:, e] - cm[:, e - 1]
    w4 = {"wgT": np.ascontiguousarray(g("L0_ffn_Wg").T),
          "wuT": np.ascontiguousarray(g("L0_ffn_Wu").T),
          "wdT": np.ascontiguousarray(g("L0_ffn_Wd").T),
          "wrT": np.ascontiguousarray(g("L0_ffn_Wr").T),
          "agT": np.ascontiguousarray(Ag.reshape(E * R, D).T),
          "auT": np.ascontiguousarray(Au.reshape(E * R, D).T),
          "bgT": np.ascontiguousarray(Bg.transpose(0, 2, 1).reshape(E * R, DFF)),
          "buT": np.ascontiguousarray(Bu.transpose(0, 2, 1).reshape(E * R, DFF)),
          "adT": np.ascontiguousarray(Ad.reshape(E * R, DFF).T),
          "bdT": np.ascontiguousarray(Bd.transpose(0, 2, 1).reshape(E * R, D)),
          "csel": sel, "crep": rep, "cmask": cm}

    sel5 = np.zeros((128, 16 * 128), np.float32)
    for r in range(16):
        sel5[:, r * 128 + r] = 128.0 ** (-0.5)
    bias5 = np.zeros((16, 4), np.float32)
    for hl in range(4):
        for l in range(4):
            for m in range(4):
                if l < m:
                    bias5[hl * 4 + l, m] = -1e30
    w5 = {"wqkvT": np.ascontiguousarray(g("L1_attn_weight_qkv").T),
          "wo1T": np.ascontiguousarray(g("L1_attn_weight_out").T),
          "csel5": sel5, "cbias5": bias5}

    perm = np.arange(FF1).reshape(FF1 // 4, 4).T.reshape(-1)
    w6 = {"w1g": np.ascontiguousarray(g("L1_ffn_w_gate")[:, perm]),
          "w1u": np.ascontiguousarray(g("L1_ffn_w_up")[:, perm]),
          "w1d": np.ascontiguousarray(g("L1_ffn_w_down")[perm, :])}

    in_maps = []
    for c in range(NCORE):
        h0 = 4 * (c % 2)
        bi = g("L0_attn_b_ig")[h0:h0 + 4]
        bf = g("L0_attn_b_fg")[h0:h0 + 4]
        gbias = np.ascontiguousarray(np.broadcast_to(
            np.concatenate([bi, bf])[None, :], (128, 8)))
        mhc = np.ascontiguousarray(
            mh[h0:h0 + 4].reshape(4, 2, 128).transpose(2, 0, 1).reshape(128, 8))
        d = {"x_fm": xs[c], "wq": g("L0_attn_wq"), "wk": g("L0_attn_wk"),
             "wv": g("L0_attn_wv"), "wg16": wg16, "n1w": n1c,
             "gbias": gbias, "mh": mhc}
        d.update(w3)
        d.update(w4)
        d.update(w5)
        d.update(w6)
        in_maps.append(d)
    r = _run(build_fused(), in_maps)
    return _unshard_seq([r[c]["out_fm"] for c in range(NCORE)])


# version 04
# speedup vs baseline: 1.0706x; 1.0706x over previous
import numpy as np
import ml_dtypes
from contextlib import ExitStack, nullcontext

import concourse.bass as bass
import concourse.mybir as mybir
from concourse.bass_utils import run_bass_kernel_spmd
from concourse.aot_env import allow_ici_all_to_all

F32 = mybir.dt.float32
BF16 = mybir.dt.bfloat16
AF = mybir.ActivationFunctionType
ALU = mybir.AluOpType
AX = mybir.AxisListType
BF16NP = ml_dtypes.bfloat16

NCORE = 8
B, S, D = 4, 4096, 2048
SC = S // NCORE
TOK = B * SC
NT = 4
NCH = D // 128
EPS_LN = 1e-5
EPS_RMS = 1e-6
CAP = 15.0
H = 8
DQK = 128
DV = 256
FF_IN = 8192
DFF = 5632
NEXP = 8
RLORA = 16
MOE_SCALING = 2.0
FF1 = 8192


class Ctr:
    def __init__(self, handle):
        self.h = handle
        self.n = 0

    def inc(self, inst, k=1):
        inst.then_inc(self.h, k)
        self.n += k
        return self.n


def fmload(dram_ap, cols):
    return dram_ap.rearrange("(c p) m -> p c m", p=128)


def sb3(sb_ap, cols):
    return sb_ap.rearrange("p (c m) -> p c m", m=cols)


def mk_steps(block, serialize_ctrs=()):
    def step(deco, ctr, waits, emit):
        @deco
        def _(eng):
            for sem, val in waits:
                if val:
                    eng.wait_ge(sem, val)
            inst = emit(eng)
            ctr.inc(inst)
        return ctr.n

    def stepseq(deco, ctr, waits, fns):
        @deco
        def _(eng):
            for sem, val in waits:
                if val:
                    eng.wait_ge(sem, val)
            inst = None
            for i, fn in enumerate(fns):
                if i > 0:
                    ctr.inc(inst)
                    eng.wait_ge(ctr.h, ctr.n)
                inst = fn(eng)
            ctr.inc(inst)
        return ctr.n

    return step, stepseq


def build_l1(fz=None):
    pre = "" if fz is None else "z1_"
    if fz is None:
        nc = bass.Bass("TRN2", target_bir_lowering=False,
                       detect_race_conditions=False)
        x = nc.dram_tensor("x_fm", [D, TOK], F32, kind="ExternalInput").ap()
        o_xn = nc.dram_tensor("xn_bf", [D, TOK], BF16, kind="ExternalOutput").ap()
        o_hin = nc.dram_tensor("hin_bf", [D, TOK], BF16, kind="ExternalOutput").ap()
        o_q = nc.dram_tensor("q_fm", [1024, TOK], BF16, kind="ExternalOutput").ap()
        o_k = nc.dram_tensor("k_fm", [1024, TOK], BF16, kind="ExternalOutput").ap()
        o_v = nc.dram_tensor("v_tm", [TOK, D], BF16, kind="ExternalOutput").ap()
        o_g = nc.dram_tensor("gates", [TOK, 16], F32, kind="ExternalOutput").ap()
    else:
        nc = fz.nc
        x = fz.dram["x_fm"]
        o_xn = fz.dram["xn_bf"]
        o_hin = fz.dram["hin_bf"]
        o_q = o_k = o_v = o_g = None
    wq = nc.dram_tensor("wq", [D, 1024], F32, kind="ExternalInput").ap()
    wk = nc.dram_tensor("wk", [D, 1024], F32, kind="ExternalInput").ap()
    wv = nc.dram_tensor("wv", [D, D], F32, kind="ExternalInput").ap()
    wg16 = nc.dram_tensor("wg16", [D, 16], F32, kind="ExternalInput").ap()
    n1w = nc.dram_tensor("n1w", [128, NCH], F32, kind="ExternalInput").ap()

    c_ones_col = nc.inline_tensor(np.ones((128, 1), np.float32),
                                  pre + "c_ones_col").ap()
    c_ones_row = nc.inline_tensor(np.ones((1, 128), np.float32),
                                  pre + "c_ones_row").ap()

    with ExitStack() as ctx:
        def sb(name, shape, dtype=F32):
            return ctx.enter_context(nc.sbuf_tensor(pre + name, shape, dtype))

        def ps(name, shape, dtype=F32):
            return ctx.enter_context(nc.psum_tensor(pre + name, shape, dtype))

        s_x = sb("s_x", [128, NCH * 512])
        s_xsq = sb("s_xsq", [128, 8 * 512])
        s_xn = sb("s_xn", [128, NCH * 512], BF16)
        s_hin = sb("s_hin", [128, NCH * 512], BF16)
        s_n1 = sb("s_n1", [128, NCH])
        s_wv = sb("s_wv", [128, NCH * D], BF16)
        s_wqk = sb("s_wqk", [128, NCH * 1024], BF16)
        s_wg = sb("s_wg", [128, NCH * 16], BF16)
        s_row = sb("s_row", [1, 6 * 512])
        s_ab = sb("s_ab", [128, 2 * 512])
        s_qk = sb("s_qk", [128, 8 * 512], BF16)
        s_v = sb("s_v", [128, 4 * 512], BF16)
        s_g = sb("s_g", [128, 4 * 16])
        s_oc = sb("s_oc", [128, 1])
        s_or = sb("s_or", [1, 128])

        ps_s = ps("ps_s", [1, 512])
        ps_q = ps("ps_q", [1, 512])
        ps_a = ps("ps_a", [128, 512])
        ps_b = ps("ps_b", [128, 512])
        ps_p = [ps("ps_p0", [128, 512]), ps("ps_p1", [128, 512])]
        ps_g = ps("ps_g", [128, 16])
        ps_g2 = ps("ps_g2", [128, 512])

        if fz is None:
            sW = Ctr(ctx.enter_context(nc.semaphore("sW")))
            sX = Ctr(ctx.enter_context(nc.semaphore("sX")))
            sPE = Ctr(ctx.enter_context(nc.semaphore("sPE")))
            sDV = Ctr(ctx.enter_context(nc.semaphore("sDV")))
            sAC = Ctr(ctx.enter_context(nc.semaphore("sAC")))
            sST = Ctr(ctx.enter_context(nc.semaphore("sST")))
        else:
            sW, sX, sPE, sDV, sAC, sST = (fz.sW, fz.sX, fz.sPE, fz.sDV,
                                          fz.sAC, fz.sST)

        with (nc.Block() if fz is None else nullcontext(fz.block)) as block:
            @block.gpsimd
            def _(g):
                g.dma_start(out=sb3(s_wv[:], D), in_=fmload(wv, D)).then_inc(sW.h, 16)
                g.dma_start(out=sb3(s_wg[:], 16), in_=fmload(wg16, 16)).then_inc(sW.h, 16)
                sW.n += 32

            @block.sync
            def _(s):
                s.dma_start(out=s_oc[:], in_=c_ones_col).then_inc(sX.h, 16)
                s.dma_start(out=s_or[:], in_=c_ones_row).then_inc(sX.h, 16)
                s.dma_start(out=s_n1[:], in_=n1w).then_inc(sX.h, 16)
                sX.n += 48

            evac_done = {}
            stats_done = [None] * NT
            store_done = {}
            xn_done = [0] * NT
            sums_hist = [0] * NT
            proj_hist = [None]

            for t in range(NT):
                @block.sync
                def _(s, t=t):
                    if t > 0:
                        s.wait_ge(sDV.h, xn_done[t - 1])
                    s.dma_start(
                        out=sb3(s_x[:], 512), in_=fmload(x[:, t * 512:(t + 1) * 512], 512)
                    ).then_inc(sX.h, 16)
                    sX.n += 16

                xload = sX.n

                for half in range(2):
                    @block.scalar
                    def _(sc, t=t, xload=xload, half=half):
                        if half == 0:
                            sc.wait_ge(sX.h, xload)
                            if t > 0:
                                sc.wait_ge(sPE.h, sums_hist[t - 1])
                        else:
                            sc.wait_ge(sPE.h, half_done[0])
                        for c in range(8):
                            inst = sc.activation(
                                s_xsq[:, c * 512:(c + 1) * 512],
                                s_x[:, (half * 8 + c) * 512:(half * 8 + c + 1) * 512],
                                AF.Square)
                        sAC.inc(inst)

                    sq_done = sAC.n

                    @block.tensor
                    def _(te, t=t, sq_done=sq_done, half=half):
                        te.wait_ge(sAC.h, sq_done)
                        if half == 0 and stats_done[t - 1] is not None:
                            te.wait_ge(sDV.h, stats_done[t - 1])
                        for c in range(8):
                            cc = half * 8 + c
                            te.matmul(ps_s[:], s_oc[:],
                                      s_x[:, cc * 512:(cc + 1) * 512],
                                      start=(cc == 0), stop=(cc == NCH - 1))
                            inst = te.matmul(
                                ps_q[:], s_oc[:], s_xsq[:, c * 512:(c + 1) * 512],
                                start=(cc == 0), stop=(cc == NCH - 1))
                        sPE.inc(inst)

                    if half == 0:
                        half_done = [sPE.n]

                sums_done = sPE.n
                sums_hist[t] = sPE.n

                @block.vector
                def _(v, t=t, sums_done=sums_done):
                    v.wait_ge(sPE.h, sums_done)
                    v.tensor_scalar_mul(s_row[0:1, 0:512], ps_s[:], 1.0 / D)
                    inst = v.tensor_scalar_mul(s_row[0:1, 512:1024], ps_q[:],
                                               1.0 / D)
                    sDV.inc(inst)
                    v.wait_ge(sDV.h, sDV.n)
                    inst = v.tensor_mul(s_row[0:1, 1024:1536], s_row[0:1, 0:512],
                                        s_row[0:1, 0:512])
                    sDV.inc(inst)
                    v.wait_ge(sDV.h, sDV.n)
                    inst = v.tensor_sub(s_row[0:1, 2048:2560],
                                        s_row[0:1, 512:1024],
                                        s_row[0:1, 1024:1536])
                    sDV.inc(inst)
                    v.wait_ge(sDV.h, sDV.n)
                    inst = v.tensor_scalar_add(s_row[0:1, 512:1024],
                                               s_row[0:1, 2048:2560], EPS_LN)
                    sDV.inc(inst)

                stats_done[t] = sDV.n
                var_done = sDV.n

                @block.scalar
                def _(sc, var_done=var_done):
                    sc.wait_ge(sDV.h, var_done)
                    inst = sc.activation(s_row[0:1, 512:1024],
                                         s_row[0:1, 512:1024], AF.Sqrt)
                    sAC.inc(inst)

                sqrt_done = sAC.n

                @block.vector
                def _(v, sqrt_done=sqrt_done):
                    v.wait_ge(sAC.h, sqrt_done)
                    inst = v.reciprocal(s_row[0:1, 1536:2048],
                                        s_row[0:1, 512:1024])
                    sDV.inc(inst)
                    v.wait_ge(sDV.h, sDV.n)
                    inst = v.scalar_tensor_tensor(
                        s_row[0:1, 0:512], s_row[0:1, 0:512], -1.0,
                        s_row[0:1, 1536:2048], ALU.mult, ALU.mult)
                    sDV.inc(inst)
                    v.wait_ge(sDV.h, sDV.n)
                    inst = v.tensor_mul(s_row[0:1, 2560:3072],
                                        s_row[0:1, 1536:2048],
                                        s_row[0:1, 1536:2048])
                    sDV.inc(inst)
                    v.wait_ge(sDV.h, sDV.n)
                    inst = v.scalar_tensor_tensor(
                        s_row[0:1, 2560:3072], s_row[0:1, 2560:3072], 1.0,
                        s_row[0:1, 2048:2560], ALU.mult, ALU.mult)
                    sDV.inc(inst)
                    v.wait_ge(sDV.h, sDV.n)
                    inst = v.tensor_scalar_add(s_row[0:1, 2560:3072],
                                               s_row[0:1, 2560:3072], EPS_RMS)
                    sDV.inc(inst)

                ab_row = sDV.n

                @block.scalar
                def _(sc, n=ab_row):
                    sc.wait_ge(sDV.h, n)
                    inst = sc.activation(s_row[0:1, 2560:3072],
                                         s_row[0:1, 2560:3072], AF.Sqrt)
                    sAC.inc(inst)

                r2s = sAC.n

                @block.vector
                def _(v, n=r2s):
                    v.wait_ge(sAC.h, n)
                    inst = v.reciprocal(s_row[0:1, 2560:3072],
                                        s_row[0:1, 2560:3072])
                    sDV.inc(inst)

                ab_row = sDV.n

                @block.tensor
                def _(te, ab_row=ab_row, t=t):
                    te.wait_ge(sDV.h, ab_row)
                    if t > 0:
                        te.wait_ge(sDV.h, xn_done[t - 1])
                    te.matmul(ps_a[:], s_or[:], s_row[0:1, 1536:2048],
                              start=True, stop=True)
                    inst = te.matmul(ps_b[:], s_or[:], s_row[0:1, 0:512],
                                     start=True, stop=True)
                    sPE.inc(inst)

                bcast_done = sPE.n

                @block.tensor
                def _(te, t=t):
                    if t > 0:
                        te.wait_ge(sDV.h, xn_done[t - 1])
                    inst = te.matmul(ps_g2[:], s_or[:], s_row[0:1, 2560:3072],
                                     start=True, stop=True)
                    sPE.inc(inst)

                bc2_done = sPE.n

                @block.vector
                def _(v, bcast_done=bcast_done, bc2_done=bc2_done, t=t):
                    v.wait_ge(sPE.h, bc2_done)
                    v.wait_ge(sPE.h, bcast_done)
                    if "xn" in store_done:
                        v.wait_ge(sST.h, store_done["xn"])
                    if "hin" in store_done:
                        v.wait_ge(sST.h, store_done["hin"])
                    v.tensor_copy(s_ab[:, 0:512], ps_a[:])
                    inst = v.tensor_copy(s_ab[:, 512:1024], ps_b[:])
                    sDV.inc(inst)
                    v.wait_ge(sDV.h, sDV.n)
                    for c in range(NCH):
                        inst = v.tensor_mul(s_x[:, c * 512:(c + 1) * 512],
                                            s_x[:, c * 512:(c + 1) * 512],
                                            s_ab[:, 0:512])
                        sDV.inc(inst)
                        v.wait_ge(sDV.h, sDV.n)
                        inst = v.tensor_add(s_x[:, c * 512:(c + 1) * 512],
                                            s_x[:, c * 512:(c + 1) * 512],
                                            s_ab[:, 512:1024])
                        sDV.inc(inst)
                        v.wait_ge(sDV.h, sDV.n)
                        inst = v.tensor_copy(s_xn[:, c * 512:(c + 1) * 512],
                                             s_x[:, c * 512:(c + 1) * 512])
                        sDV.inc(inst)
                        v.wait_ge(sDV.h, sDV.n)
                        inst = v.scalar_tensor_tensor(
                            s_hin[:, c * 512:(c + 1) * 512],
                            s_x[:, c * 512:(c + 1) * 512],
                            s_n1[:, c:c + 1], ps_g2[:], ALU.mult, ALU.mult)
                        sDV.inc(inst)

                xn_done[t] = sDV.n

                @block.sync
                def _(s, t=t, n=sDV.n):
                    s.wait_ge(sDV.h, n)
                    s.dma_start(
                        out=fmload(o_xn[:, t * 512:(t + 1) * 512], 512),
                        in_=sb3(s_xn[:], 512)).then_inc(sST.h, 16)
                    sST.n += 16

                store_done["xn"] = sST.n

                @block.sync
                def _(s, t=t, n=xn_done[t]):
                    s.wait_ge(sDV.h, n)
                    s.dma_start(
                        out=fmload(o_hin[:, t * 512:(t + 1) * 512], 512),
                        in_=sb3(s_hin[:], 512)).then_inc(sST.h, 16)
                    sST.n += 16

                store_done["hin"] = sST.n

                def mm_proj(te, wsb, wcols, m, pidx):
                    pd = ps_p[pidx % 2]
                    if pidx % 2 in evac_done:
                        sem, cnt = evac_done[pidx % 2]
                        te.wait_ge(sem, cnt)
                    inst = None
                    for c in range(NCH):
                        inst = te.matmul(
                            pd[:],
                            wsb[:, c * wcols + m * 128: c * wcols + (m + 1) * 128],
                            s_hin[:, c * 512:(c + 1) * 512],
                            start=(c == 0), stop=(c == NCH - 1))
                    return inst

                pidx = [0]
                for which, wdram, odram in (("q", wq, o_q), ("k", wk, o_k)):
                    @block.gpsimd
                    def _(g, wdram=wdram):
                        if proj_hist[0] is not None:
                            g.wait_ge(sPE.h, proj_hist[0])
                        g.dma_start(out=sb3(s_wqk[:], 1024), in_=fmload(wdram, 1024)
                                    ).then_inc(sW.h, 16)
                        sW.n += 16

                    wload = sW.n
                    for m in range(8):
                        @block.tensor
                        def _(te, m=m, wload=wload):
                            if m == 0:
                                te.wait_ge(sW.h, wload)
                                te.wait_ge(sDV.h, xn_done[t])
                            inst = mm_proj(te, s_wqk, 1024, m, pidx[0])
                            sPE.inc(inst)

                        mm_done = sPE.n

                        @block.scalar
                        def _(sc, m=m, mm_done=mm_done):
                            sc.wait_ge(sPE.h, mm_done)
                            if m == 0 and "qk" in store_done:
                                sc.wait_ge(sST.h, store_done["qk"])
                            inst = sc.copy(s_qk[:, m * 512:(m + 1) * 512],
                                           ps_p[pidx[0] % 2][:])
                            sAC.inc(inst)

                        evac_done[pidx[0] % 2] = (sAC.h, sAC.n)
                        pidx[0] += 1

                    proj_hist[0] = sPE.n

                    @block.sync
                    def _(s, odram=odram, n=sAC.n, t=t, which=which):
                        s.wait_ge(sAC.h, n)
                        if fz is None:
                            s.dma_start(
                                out=fmload(odram[:, t * 512:(t + 1) * 512], 512),
                                in_=sb3(s_qk[:], 512)).then_inc(sST.h, 16)
                            sST.n += 16
                        else:
                            off = 0 if which == "q" else 512
                            for hg in range(2):
                                dd = 2 * t + hg
                                s.dma_start(
                                    out=fmload(fz.a2a1_in[
                                        dd * 1536 + off:dd * 1536 + off + 512,
                                        0:512], 512),
                                    in_=sb3(s_qk[:, hg * 2048:(hg + 1) * 2048],
                                            512)).then_inc(sST.h, 16)
                                sST.n += 16

                    store_done["qk"] = sST.n

                for tc in range(4):
                    for gi in range(4):
                        @block.tensor
                        def _(te, tc=tc, gi=gi):
                            pd = ps_p[pidx[0] % 2]
                            sem, cnt = evac_done[pidx[0] % 2]
                            te.wait_ge(sem, cnt)
                            for c in range(NCH):
                                inst = te.matmul(
                                    pd[:],
                                    s_hin[:, c * 512 + tc * 128: c * 512 + (tc + 1) * 128],
                                    s_wv[:, c * D + gi * 512: c * D + (gi + 1) * 512],
                                    start=(c == 0), stop=(c == NCH - 1))
                            sPE.inc(inst)

                        mm_done = sPE.n

                        @block.vector
                        def _(v, gi=gi, mm_done=mm_done):
                            v.wait_ge(sPE.h, mm_done)
                            if gi == 0 and "v" in store_done:
                                v.wait_ge(sST.h, store_done["v"])
                            inst = v.tensor_copy(s_v[:, gi * 512:(gi + 1) * 512],
                                                 ps_p[pidx[0] % 2][:])
                            sDV.inc(inst)

                        evac_done[pidx[0] % 2] = (sDV.h, sDV.n)
                        pidx[0] += 1

                    @block.sync
                    def _(s, tc=tc, n=sDV.n, t=t):
                        s.wait_ge(sDV.h, n)
                        if fz is None:
                            s.dma_start(
                                out=o_v[t * 512 + tc * 128:
                                        t * 512 + (tc + 1) * 128, :],
                                in_=s_v[:]).then_inc(sST.h, 16)
                            sST.n += 16
                        else:
                            for hg in range(2):
                                dd = 2 * t + hg
                                r0 = dd * 1536 + 1024 + tc * 128
                                s.dma_start(
                                    out=fz.a2a1_in[r0:r0 + 128, 0:1024],
                                    in_=s_v[:, hg * 1024:(hg + 1) * 1024]
                                ).then_inc(sST.h, 16)
                                sST.n += 16

                    store_done["v"] = sST.n

                gate_ev = [None]
                for tc in range(4):
                    @block.tensor
                    def _(te, tc=tc, t=t):
                        if gate_ev[0] is not None:
                            te.wait_ge(sDV.h, gate_ev[0])
                        inst = None
                        for c in range(NCH):
                            inst = te.matmul(
                                ps_g[:],
                                s_hin[:, c * 512 + tc * 128: c * 512 + (tc + 1) * 128],
                                s_wg[:, c * 16:(c + 1) * 16],
                                start=(c == 0), stop=(c == NCH - 1))
                        sPE.inc(inst)

                    mm_done = sPE.n

                    @block.vector
                    def _(v, tc=tc, mm_done=mm_done):
                        v.wait_ge(sPE.h, mm_done)
                        if tc == 0 and "g" in store_done:
                            v.wait_ge(sST.h, store_done["g"])
                        inst = v.tensor_copy(s_g[:, tc * 16:(tc + 1) * 16], ps_g[:])
                        sDV.inc(inst)

                    gate_ev[0] = sDV.n

                @block.sync
                def _(s, t=t, n=sDV.n):
                    s.wait_ge(sDV.h, n)
                    if fz is None:
                        s.dma_start(
                            out=o_g[t * 512:(t + 1) * 512, :].rearrange(
                                "(tc p) g -> p tc g", p=128),
                            in_=sb3(s_g[:], 16)).then_inc(sST.h, 16)
                        sST.n += 16
                    else:
                        g3 = s_g[:].rearrange("p (tc g) -> p tc g", g=16)
                        for hg in range(2):
                            dd = 2 * t + hg
                            h0 = 4 * hg
                            s.dma_start(
                                out=fz.g1a_in[dd * 512:(dd + 1) * 512,
                                              0:4].rearrange(
                                    "(tc p) g -> p tc g", p=128),
                                in_=g3[:, :, h0:h0 + 4]).then_inc(sST.h, 16)
                            sST.n += 16
                            s.dma_start(
                                out=fz.g1a_in[dd * 512:(dd + 1) * 512,
                                              4:8].rearrange(
                                    "(tc p) g -> p tc g", p=128),
                                in_=g3[:, :, 8 + h0:8 + h0 + 4]
                            ).then_inc(sST.h, 16)
                            sST.n += 16

                store_done["g"] = sST.n

            @block.sync
            def _(s):
                s.wait_ge(sST.h, sST.n)

    return nc



LNSQ = float(np.log(np.sqrt(128.0)))
L2_SERIALIZE = False
L2_DEBUG = False
L2_NPAIR = 4
NKC = 32
NQT = 8


def build_l2(fz=None):
    pre = "" if fz is None else "z2_"
    if fz is None:
        nc = bass.Bass("TRN2", target_bir_lowering=False,
                       detect_race_conditions=False)
        q_in = nc.dram_tensor("q_bh", [512, S], BF16, kind="ExternalInput").ap()
        k_in = nc.dram_tensor("k_bh", [512, S], BF16, kind="ExternalInput").ap()
        v_in = nc.dram_tensor("v_bh", [S, 1024], BF16, kind="ExternalInput").ap()
        g_in = nc.dram_tensor("gates_px", [S, 8], F32, kind="ExternalInput").ap()
        o_h = nc.dram_tensor("h_fm", [1024, S], BF16, kind="ExternalOutput").ap()
    else:
        nc = fz.nc
        q_in = k_in = v_in = o_h = None
        g_in = fz.g1a_out
    gb_in = nc.dram_tensor("gbias", [128, 8], F32, kind="ExternalInput").ap()
    mh_in = nc.dram_tensor("mh", [128, 8], F32, kind="ExternalInput").ap()
    if L2_DEBUG:
        o_dbg = nc.dram_tensor("dbg", [128, 8 * 128], F32, kind="ExternalOutput").ap()
        o_dbg2 = nc.dram_tensor("dbg2", [128, 4 * 257], F32, kind="ExternalOutput").ap()

    _c = np.arange(128)
    ident_np = np.eye(128, dtype=np.float32)
    consts = {
        "ident": ident_np,
        "identb": ident_np.astype(BF16NP),
        "onesq": np.ones((128, 128), np.float32),
        "utri": np.triu(np.ones((128, 128), np.float32)),
        "trilm": np.triu(np.ones((128, 128), np.float32)),
        "trilneg": ((np.triu(np.ones((128, 128))) - 1.0) * 1e30).astype(np.float32),
        "strictu": np.triu(np.ones((128, 128), np.float32), 1),
        "strictuneg": ((np.triu(np.ones((128, 128)), 1) - 1.0) * 1e30).astype(np.float32),
        "ones_row": np.ones((1, 128), np.float32),
        "negones_row": -np.ones((1, 128), np.float32),
    }
    mb = np.zeros((128, 4, 512), np.float32)
    for dj in range(4):
        mb[:, dj, :] = np.where(
            (_c[:, None] + dj * 128) <= np.arange(512)[None, :], 0.0, -1e30)
    consts["maskb"] = np.ascontiguousarray(mb.reshape(128, 4 * 512))
    caps = {k: nc.inline_tensor(v, pre + "c_" + k).ap()
            for k, v in consts.items()}

    with ExitStack() as ctx:
        def sb(name, shape, dtype=F32):
            return ctx.enter_context(nc.sbuf_tensor(pre + name, shape, dtype))

        def ps(name, shape, dtype=F32):
            return ctx.enter_context(nc.psum_tensor(pre + name, shape, dtype))

        s_q = sb("s_q", [128, 4 * S], BF16)
        s_k = sb("s_k", [128, 4 * S], BF16)
        s_v = sb("s_v", [128, NKC * 257], BF16)
        s_g = sb("s_g", [128, NKC * 8])
        s_mh = sb("s_mh", [128, 8])
        s_gb = sb("s_gb", [128, 8])
        cst = {k: sb("k_" + k, list(v.shape), BF16 if k == "identb" else F32)
               for k, v in consts.items()}
        pa_names = ["iraw", "fraw", "ti", "fcap", "labs", "lexp", "lln",
                    "logfg", "fcum", "u", "uadj", "mintra", "mall", "mT",
                    "enm", "psel", "tmpa", "tmpb"]
        pa = {n: sb("pa_" + n, [128, 128]) for n in pa_names}
        s_f = sb("s_f", [128, 256])
        s_prow = sb("s_prow", [1, 128])
        s_mrow = sb("s_mrow", [1, 512])
        s_cp = [sb(f"s_cp{i}", [128, 512], BF16) for i in range(3)]
        s_E = sb("s_E", [128, 512])
        s_acc = sb("s_acc", [128, 4 * 257])
        s_ep = sb("s_ep", [128, 24])
        s_eps = sb("s_eps", [128, 1])
        s_h = sb("s_h", [128, 256])
        s_hb = sb("s_hb", [128, 256], BF16)
        s_ho = sb("s_ho", [128, 2 * 512], BF16)

        ps_qk = [ps("ps_qk0", [128, 512]), ps("ps_qk1", [128, 512])]
        ps_av = [ps(f"ps_av{i}", [128, 257]) for i in range(4)]
        ps_nm = ps("ps_nm", [128, 512])
        ps_aux = ps("ps_aux", [128, 128])

        if fz is None:
            sX = Ctr(ctx.enter_context(nc.semaphore("sX")))
            sPE = Ctr(ctx.enter_context(nc.semaphore("sPE")))
            sDV = Ctr(ctx.enter_context(nc.semaphore("sDV")))
            sAC = Ctr(ctx.enter_context(nc.semaphore("sAC")))
            sST = Ctr(ctx.enter_context(nc.semaphore("sST")))
        else:
            sX, sPE, sDV, sAC, sST = fz.sX, fz.sPE, fz.sDV, fz.sAC, fz.sST

        with (nc.Block() if fz is None else nullcontext(fz.block)) as block:
            def step(deco, ctr, waits, emit):
                @deco
                def _(eng):
                    if L2_SERIALIZE:
                        for c in (sPE, sDV, sAC):
                            if c is not ctr and c.n:
                                eng.wait_ge(c.h, c.n)
                    for sem, val in waits:
                        if val:
                            eng.wait_ge(sem, val)
                    inst = emit(eng)
                    ctr.inc(inst)
                return ctr.n

            def stepseq(deco, ctr, waits, fns):
                @deco
                def _(eng):
                    for sem, val in waits:
                        if val:
                            eng.wait_ge(sem, val)
                    inst = None
                    for i, fn in enumerate(fns):
                        if i > 0:
                            ctr.inc(inst)
                            eng.wait_ge(ctr.h, ctr.n)
                        inst = fn(eng)
                    ctr.inc(inst)
                return ctr.n

            @block.sync
            def _(s):
                if fz is None:
                    s.dma_start(out=sb3(s_q[:], S),
                                in_=fmload(q_in, S)).then_inc(sX.h, 16)
                    s.dma_start(out=sb3(s_k[:], S),
                                in_=fmload(k_in, S)).then_inc(sX.h, 16)
                    sX.n += 32
                else:
                    for dd in range(8):
                        for c in range(4):
                            r0 = dd * 1536 + c * 128
                            s.dma_start(
                                out=s_q[:, c * S + dd * 512:
                                        c * S + dd * 512 + 512],
                                in_=fz.a2a1_out[r0:r0 + 128, 0:512]
                            ).then_inc(sX.h, 16)
                            s.dma_start(
                                out=s_k[:, c * S + dd * 512:
                                        c * S + dd * 512 + 512],
                                in_=fz.a2a1_out[r0 + 512:r0 + 640, 0:512]
                            ).then_inc(sX.h, 16)
                            sX.n += 32
                s.dma_start(out=sb3(s_g[:], 8), in_=fmload(g_in, 8)).then_inc(sX.h, 16)
                s.dma_start(out=s_mh[:], in_=mh_in).then_inc(sX.h, 16)
                s.dma_start(out=s_gb[:], in_=gb_in).then_inc(sX.h, 16)
                for k in cst:
                    s.dma_start(out=cst[k][:], in_=caps[k]).then_inc(sX.h, 16)
                sX.n += 16 * (3 + len(cst))

            base_loads = sX.n

            def emit_init(v):
                v.tensor_scalar_mul(s_gb[:], s_gb[:], 1.0 / CAP)
                v.memset(s_eps[:], EPS_RMS)
                for t in pa.values():
                    v.memset(t[:], 0.0)
                v.memset(s_f[:], 0.0)
                v.memset(s_prow[:], 0.0)
                return v.memset(
                    s_v[:].rearrange("p (c k) -> p c k", k=257)[:, :, 256:257], 1.0)
            vset = step(block.vector, sDV, [(sX.h, base_loads)], emit_init)

            pair_end = [0]
            vfree = [0]
            auxg = [0]

            for pr in range(L2_NPAIR):
                @block.sync
                def _(s, pr=pr):
                    s.wait_ge(sDV.h, vset)
                    if vfree[0]:
                        s.wait_ge(sPE.h, vfree[0])
                    if fz is None:
                        s.dma_start(
                            out=s_v[:].rearrange("p (c k) -> p c k",
                                                 k=257)[:, :, 0:256],
                            in_=v_in[:, pr * 256:(pr + 1) * 256].rearrange(
                                "(c p) d -> p c d", p=128)).then_inc(sX.h, 16)
                        sX.n += 16
                    else:
                        for dd in range(8):
                            r0 = dd * 1536 + 1024
                            s.dma_start(
                                out=s_v[:].rearrange(
                                    "p (c k) -> p c k",
                                    k=257)[:, dd * 4:(dd + 1) * 4, 0:256],
                                in_=fz.a2a1_out[r0:r0 + 512,
                                                pr * 256:(pr + 1) * 256
                                                ].rearrange(
                                    "(c p) d -> p c d", p=128)
                            ).then_inc(sX.h, 16)
                            sX.n += 16

                vld = sX.n

                g3 = s_g[:].rearrange("p (c g) -> p c g", g=8)
                raw = step(block.vector, sDV, [(sX.h, base_loads)], lambda v: (
                    v.tensor_copy(sb3(pa["iraw"][:, 0:32], 1), g3[:, :, pr:pr + 1]),
                    v.tensor_copy(sb3(pa["fraw"][:, 0:32], 1),
                                  g3[:, :, 4 + pr:5 + pr]))[-1])
                tanh = step(block.scalar, sAC,
                            [(sDV.h, raw), (sDV.h, vset)], lambda sc, pr=pr: (
                    sc.activation(pa["ti"][:, 0:32], pa["iraw"][:, 0:32], AF.Tanh,
                                  scale=1.0 / CAP, bias=s_gb[:, pr:pr + 1]),
                    sc.activation(pa["fcap"][:, 0:32], pa["fraw"][:, 0:32], AF.Tanh,
                                  scale=1.0 / CAP, bias=s_gb[:, 4 + pr:5 + pr]))[-1])
                fc = step(block.vector, sDV, [(sAC.h, tanh)], lambda v:
                          v.tensor_scalar_mul(pa["fcap"][:, 0:32],
                                              pa["fcap"][:, 0:32], CAP))
                lns = step(block.scalar, sAC, [(sDV.h, fc)], lambda sc: (
                    sc.activation(pa["labs"][:, 0:32], pa["fcap"][:, 0:32], AF.Abs),
                    sc.activation(pa["lexp"][:, 0:32], pa["labs"][:, 0:32], AF.Exp,
                                  scale=-1.0),
                    sc.activation(pa["lln"][:, 0:32], pa["lexp"][:, 0:32], AF.Ln,
                                  bias=1.0))[-1])
                lfg = step(block.vector, sDV, [(sAC.h, lns)], lambda v:
                    v.scalar_tensor_tensor(pa["logfg"][:, 0:32],
                                           pa["fcap"][:, 0:32], 0.0,
                                           pa["lln"][:, 0:32],
                                           ALU.min, ALU.subtract))

                def emit_fcum(te):
                    inst = None
                    for j in range(NKC):
                        first = True
                        for jp in range(j):
                            inst = te.matmul(ps_aux[:, j:j + 1], cst["onesq"][:],
                                             pa["logfg"][:, jp:jp + 1],
                                             start=first, stop=False)
                            first = False
                        inst = te.matmul(ps_aux[:, j:j + 1], cst["utri"][:],
                                         pa["logfg"][:, j:j + 1],
                                         start=first, stop=True)
                    return inst

                fcum_mm = step(block.tensor, sPE, [(sDV.h, lfg), (sDV.h, auxg[0])],
                               emit_fcum)
                ucols = stepseq(block.vector, sDV, [(sPE.h, fcum_mm)], [
                    lambda v: v.tensor_copy(pa["fcum"][:, 0:32], ps_aux[:, 0:32]),
                    lambda v: v.scalar_tensor_tensor(
                        pa["u"][:, 0:32], pa["ti"][:, 0:32], CAP,
                        pa["fcum"][:, 0:32], ALU.mult, ALU.subtract),
                    lambda v: v.tensor_scalar_add(pa["uadj"][:, 0:32],
                                                  pa["u"][:, 0:32], -LNSQ)])
                auxg[0] = ucols
                ut_mm = step(block.tensor, sPE, [(sDV.h, ucols)], lambda te:
                             te.matmul(ps_aux[:], pa["u"][:], cst["ident"][:],
                                       start=True, stop=True))
                ucol = step(block.vector, sDV, [(sPE.h, ut_mm)], lambda v:
                            v.reduce_max(pa["tmpa"][:, 0:1], ps_aux[:], axis=AX.X))
                auxg[0] = ucol
                pf1 = step(block.vector, sDV, [(sDV.h, ucol)], lambda v:
                    v.scalar_tensor_tensor(pa["tmpb"][:], cst["strictu"][:],
                                           pa["tmpa"][:, 0:1],
                                           cst["strictuneg"][:],
                                           ALU.mult, ALU.add))
                pf_mm = step(block.tensor, sPE, [(sDV.h, pf1)], lambda te:
                             te.matmul(ps_aux[:], pa["tmpb"][:], cst["ident"][:],
                                       start=True, stop=True))
                pcol = step(block.vector, sDV, [(sPE.h, pf_mm)], lambda v:
                            v.reduce_max(pa["tmpa"][:, 1:2], ps_aux[:], axis=AX.X))
                auxg[0] = pcol
                pr_mm = step(block.tensor, sPE, [(sDV.h, pcol)], lambda te:
                             te.matmul(ps_aux[0:1, :], pa["tmpa"][:, 1:2],
                                       cst["ident"][:], start=True, stop=True))
                prow = step(block.vector, sDV, [(sPE.h, pr_mm)], lambda v:
                            v.tensor_copy(s_prow[:], ps_aux[0:1, :]))
                auxg[0] = prow

                mi = prow
                for j in range(NKC):
                    tmask = step(block.vector, sDV, [(sDV.h, mi)], lambda v, j=j:
                        v.scalar_tensor_tensor(pa["tmpb"][:], cst["trilm"][:],
                                               pa["u"][:, j:j + 1],
                                               cst["trilneg"][:],
                                               ALU.mult, ALU.add))
                    tt_mm = step(block.tensor, sPE,
                                 [(sDV.h, tmask), (sDV.h, auxg[0])], lambda te:
                                 te.matmul(ps_aux[:], pa["tmpb"][:],
                                           cst["ident"][:], start=True, stop=True))
                    mi = step(block.vector, sDV, [(sPE.h, tt_mm)], lambda v, j=j:
                              v.reduce_max(pa["mintra"][:, j:j + 1], ps_aux[:],
                                           axis=AX.X))
                    auxg[0] = mi

                pbc_mm = step(block.tensor, sPE, [(sDV.h, mi), (sDV.h, auxg[0])],
                              lambda te: te.matmul(ps_aux[:], cst["ones_row"][:],
                                                   s_prow[:], start=True, stop=True))
                mall = stepseq(block.vector, sDV, [(sPE.h, pbc_mm)], [
                    lambda v: v.tensor_max(pa["mall"][:, 0:32],
                                           pa["mintra"][:, 0:32],
                                           ps_aux[:, 0:32]),
                    lambda v: v.tensor_add(pa["tmpb"][:, 0:32],
                                           pa["fcum"][:, 0:32],
                                           pa["mall"][:, 0:32])])
                auxg[0] = mall
                enm = step(block.scalar, sAC, [(sDV.h, mall)], lambda sc:
                           sc.activation(pa["enm"][:, 0:32], pa["tmpb"][:, 0:32],
                                         AF.Exp, scale=-1.0))
                psel_mm = step(block.tensor, sPE, [(sAC.h, enm)], lambda te:
                              te.matmul(ps_aux[:, 0:8], cst["ones_row"][:],
                                        s_prow[0:1, 0:32].rearrange(
                                            "o (i f) -> o i f", f=4)[:, :, 0:1],
                                        start=True, stop=True))
                psel = step(block.vector, sDV, [(sPE.h, psel_mm)], lambda v:
                            v.tensor_copy(pa["psel"][:, 0:8], ps_aux[:, 0:8]))
                auxg[0] = psel
                prev_exp = 0
                for i in range(1, NQT):
                    f1 = step(block.vector, sDV,
                              [(sDV.h, psel), (sAC.h, prev_exp)],
                              lambda v, i=i:
                              v.tensor_scalar(s_E[:, i * 32:(i + 1) * 32],
                                              pa["uadj"][:, 0:32],
                                              pa["psel"][:, i:i + 1], 0.0,
                                              ALU.subtract, ALU.min))
                    prev_exp = step(block.scalar, sAC, [(sDV.h, f1)],
                                    lambda sc, i=i:
                                    sc.activation(s_f[:, i * 32:(i + 1) * 32],
                                                  s_E[:, i * 32:(i + 1) * 32],
                                                  AF.Exp))
                fdone_ac = sAC.n
                mt_mm = step(block.tensor, sPE, [(sDV.h, auxg[0])], lambda te:
                             te.matmul(ps_aux[:], pa["mall"][:], cst["ident"][:],
                                       start=True, stop=True))
                mt = step(block.vector, sDV, [(sPE.h, mt_mm), (sAC.h, fdone_ac)],
                          lambda v: v.tensor_copy(pa["mT"][:], ps_aux[:]))
                auxg[0] = mt

                if L2_DEBUG and pr == 0:
                    @block.sync
                    def _(s, n=sDV.n):
                        s.wait_ge(sDV.h, n)
                        for ii, nm in enumerate(["fcap", "fcum", "lexp", "lln",
                                                 "iraw", "fraw", "ti", "logfg"]):
                            s.dma_start(out=o_dbg[:, ii * 128:(ii + 1) * 128],
                                        in_=pa[nm][:]).then_inc(sST.h, 16)
                            sST.n += 16

                cp_free = [0, 0, 0]
                qk_free = [None, None]
                av_free = [0]
                nm_free = [0]
                ho_store = [0]
                hb_guard = [0]

                for i in range(NQT):
                    noff = 4 * i
                    qtile = s_q[:, (pr * NQT + i) * 512:(pr * NQT + i + 1) * 512]
                    def emit_mrow(te, i=i):
                        inst = None
                        for jj in range(4):
                            c = 4 * i + jj
                            inst = te.matmul(ps_nm[0:1, jj * 128:(jj + 1) * 128],
                                             cst["ident"][:, c:c + 1], pa["mT"][:],
                                             start=True, stop=True)
                        return inst
                    mr_mm = step(block.tensor, sPE,
                                 [(sDV.h, mt), (sDV.h, nm_free[0])], emit_mrow)
                    mrow = step(block.vector, sDV, [(sPE.h, mr_mm)], lambda v:
                                v.tensor_copy(s_mrow[:], ps_nm[0:1, :]))
                    nm_mm = step(block.tensor, sPE, [(sDV.h, mrow)], lambda te:
                                 te.matmul(ps_nm[:], cst["negones_row"][:],
                                           s_mrow[:], start=True, stop=True))
                    if i > 0:
                        b1 = step(block.vector, sDV, [], lambda v, i=i:
                                  v.tensor_scalar(s_ep[:, 0:4],
                                                  pa["mall"][:, 4 * i:4 * i + 4],
                                                  pa["psel"][:, i:i + 1], None,
                                                  ALU.subtract))
                        bcols = step(block.scalar, sAC, [(sDV.h, b1)], lambda sc:
                                     sc.activation(s_ep[:, 4:8], s_ep[:, 0:4],
                                                   AF.Exp, scale=-1.0))
                    for jk in range(noff):
                        slot = jk % 3
                        qb = jk % 2
                        qk_mm = step(block.tensor, sPE,
                                     [(sX.h, vld),
                                      qk_free[qb] if qk_free[qb] else (sX.h, 0)],
                                     lambda te, jk=jk, qb=qb, qtile=qtile:
                                     te.matmul(ps_qk[qb][:],
                                               s_k[:, pr * S + jk * 128:
                                                   pr * S + (jk + 1) * 128],
                                               qtile, start=True, stop=True))
                        if jk % 2 == 0:
                            ev = step(block.vector, sDV,
                                      [(sPE.h, qk_mm), (sPE.h, cp_free[slot])],
                                      lambda v, slot=slot, qb=qb, i=i, jk=jk:
                                      v.tensor_scalar(
                                          s_cp[slot][:], ps_qk[qb][:],
                                          s_f[:, i * 32 + jk:i * 32 + jk + 1],
                                          None, ALU.mult))
                            evw = (sDV.h, ev)
                        else:
                            ev = step(block.scalar, sAC,
                                      [(sPE.h, qk_mm), (sPE.h, cp_free[slot])],
                                      lambda sc, slot=slot, qb=qb, i=i, jk=jk:
                                      sc.activation(
                                          s_cp[slot][:], ps_qk[qb][:], AF.Copy,
                                          scale=s_f[:, i * 32 + jk:i * 32 + jk + 1]))
                            evw = (sAC.h, ev)
                        qk_free[qb] = evw
                        def emit_av(te, slot=slot, jk=jk, noff=noff):
                            inst = None
                            for sc_i in range(4):
                                inst = te.matmul(
                                    ps_av[sc_i][:],
                                    s_cp[slot][:, sc_i * 128:(sc_i + 1) * 128],
                                    s_v[:, jk * 257:(jk + 1) * 257],
                                    start=(jk == 0), stop=(jk == noff - 1))
                            return inst
                        av_mm = step(block.tensor, sPE,
                                     [evw, (sDV.h, av_free[0]) if jk == 0
                                      else evw], emit_av)
                        cp_free[slot] = av_mm
                    if noff > 0:
                        offev = step(block.vector, sDV,
                                     [(sPE.h, sPE.n), (sAC.h, bcols)],
                                     lambda v: [
                                         v.tensor_scalar(
                                             s_acc[:, sc_i * 257:(sc_i + 1) * 257],
                                             ps_av[sc_i][:],
                                             s_ep[:, 4 + sc_i:5 + sc_i], None,
                                             ALU.mult)
                                         for sc_i in range(4)][-1])
                        av_free[0] = offev
                    e_guard = 0
                    for dj in range(4):
                        jd = 4 * i + dj
                        slot = dj % 3
                        qb = dj % 2
                        qk_mm = step(block.tensor, sPE,
                                     [qk_free[qb] if qk_free[qb] else (sX.h, vld)],
                                     lambda te, jd=jd, qb=qb, qtile=qtile:
                                     te.matmul(ps_qk[qb][:],
                                               s_k[:, pr * S + jd * 128:
                                                   pr * S + (jd + 1) * 128],
                                               qtile, start=True, stop=True))
                        et = step(block.vector, sDV, [(sPE.h, nm_mm)],
                                  lambda v, dj=dj: v.tensor_tensor(
                                      s_E[:], ps_nm[:],
                                      cst["maskb"][:, dj * 512:(dj + 1) * 512],
                                      ALU.add))
                        e_guard = et
                        ea = step(block.scalar, sAC, [(sDV.h, et)],
                                  lambda sc, jd=jd: sc.activation(
                                      s_E[:], s_E[:], AF.Exp,
                                      bias=pa["uadj"][:, jd:jd + 1]))
                        cp = step(block.vector, sDV,
                                  [(sAC.h, ea), (sPE.h, qk_mm),
                                   (sPE.h, cp_free[slot])],
                                  lambda v, slot=slot, qb=qb: v.tensor_mul(
                                      s_cp[slot][:], ps_qk[qb][:], s_E[:]))
                        qk_free[qb] = (sDV.h, cp)
                        def emit_av2(te, slot=slot, dj=dj, jd=jd):
                            inst = None
                            for sc_i in range(4):
                                inst = te.matmul(
                                    ps_av[sc_i][:],
                                    s_cp[slot][:, sc_i * 128:(sc_i + 1) * 128],
                                    s_v[:, jd * 257:(jd + 1) * 257],
                                    start=(dj == 0), stop=(dj == 3))
                            return inst
                        av_mm = step(block.tensor, sPE,
                                     [(sDV.h, cp), (sDV.h, av_free[0]) if dj == 0 and noff == 0
                                      else (sDV.h, cp)], emit_av2)
                        cp_free[slot] = av_mm
                    nm_free[0] = e_guard
                    dev = stepseq(block.vector, sDV, [(sPE.h, sPE.n)], [
                        (lambda v, sc_i=sc_i, i=i:
                         (v.tensor_add(s_acc[:, sc_i * 257:(sc_i + 1) * 257],
                                       s_acc[:, sc_i * 257:(sc_i + 1) * 257],
                                       ps_av[sc_i][:]) if i > 0 else
                          v.tensor_copy(s_acc[:, sc_i * 257:(sc_i + 1) * 257],
                                        ps_av[sc_i][:])))
                        for sc_i in range(4)])
                    av_free[0] = dev
                    if L2_DEBUG and pr == 0 and i == 1:
                        @block.sync
                        def _(s, n=sDV.n):
                            s.wait_ge(sDV.h, n)
                            s.dma_start(out=o_dbg2[:], in_=s_acc[:]).then_inc(
                                sST.h, 16)
                            sST.n += 16
                    for sc_i in range(4):
                        ab = step(block.scalar, sAC, [(sDV.h, dev)],
                                  lambda sc, sc_i=sc_i: sc.activation(
                                      s_ep[:, 8 + sc_i:9 + sc_i],
                                      s_acc[:, sc_i * 257 + 256:sc_i * 257 + 257],
                                      AF.Abs))
                        nrm = stepseq(block.vector, sDV, [(sAC.h, ab)], [
                            lambda v, sc_i=sc_i, i=i: v.tensor_max(
                                s_ep[:, 12 + sc_i:13 + sc_i],
                                s_ep[:, 8 + sc_i:9 + sc_i],
                                pa["enm"][:, 4 * i + sc_i:4 * i + sc_i + 1]),
                            lambda v, sc_i=sc_i: v.reciprocal(
                                s_ep[:, 16 + sc_i:17 + sc_i],
                                s_ep[:, 12 + sc_i:13 + sc_i]),
                            lambda v, sc_i=sc_i: v.tensor_scalar(
                                s_h[:], s_acc[:, sc_i * 257:sc_i * 257 + 256],
                                s_ep[:, 16 + sc_i:17 + sc_i], None, ALU.mult)])
                        sq = step(block.scalar, sAC, [(sDV.h, nrm)],
                                  lambda sc, sc_i=sc_i: (
                                      sc.activation(s_E[:, 0:256], s_h[:],
                                                    AF.Square,
                                                    accum_out=s_ep[:, 20:21]),
                                      sc.activation(s_ep[:, 21:22],
                                                    s_ep[:, 20:21], AF.Sqrt,
                                                    scale=1.0 / 256.0,
                                                    bias=s_eps[:, 0:1]))[-1])
                        hb = stepseq(block.vector, sDV,
                                  [(sAC.h, sq), (sPE.h, hb_guard[0])], [
                                      lambda v: v.reciprocal(s_ep[:, 22:23],
                                                             s_ep[:, 21:22]),
                                      lambda v: v.tensor_scalar(
                                          s_hb[:], s_h[:], s_ep[:, 22:23],
                                          None, ALU.mult)])
                        for cc in range(2):
                            tr_mm = step(block.tensor, sPE,
                                         [(sDV.h, hb), (sDV.h, auxg[0])],
                                         lambda te, cc=cc: te.matmul(
                                             ps_aux[:],
                                             s_hb[:, cc * 128:(cc + 1) * 128],
                                             cst["identb"][:], start=True,
                                             stop=True))
                            hb_guard[0] = tr_mm
                            hv = step(block.vector, sDV,
                                      [(sPE.h, tr_mm), (sST.h, ho_store[0])],
                                      lambda v, cc=cc, sc_i=sc_i: v.tensor_scalar(
                                          s_ho[:, cc * 512 + sc_i * 128:
                                               cc * 512 + (sc_i + 1) * 128],
                                          ps_aux[:],
                                          s_mh[:, pr * 2 + cc:pr * 2 + cc + 1],
                                          None, ALU.mult))
                            auxg[0] = hv
                    @block.sync
                    def _(s, pr=pr, i=i, n=sDV.n):
                        s.wait_ge(sDV.h, n)
                        if fz is None:
                            s.dma_start(
                                out=fmload(o_h[pr * 256:(pr + 1) * 256,
                                               i * 512:(i + 1) * 512], 512),
                                in_=sb3(s_ho[:], 512)).then_inc(sST.h, 16)
                        else:
                            r0 = i * 1024 + pr * 256
                            s.dma_start(
                                out=fmload(fz.a2a2_in[r0:r0 + 256, 0:512], 512),
                                in_=sb3(s_ho[:], 512)).then_inc(sST.h, 16)
                        sST.n += 16
                    ho_store[0] = sST.n
                vfree[0] = sPE.n
                pair_end[0] = sST.n

            @block.sync
            def _(s):
                s.wait_ge(sST.h, sST.n)

    return nc



def build_l3(fz=None):
    pre = "" if fz is None else "z3_"
    if fz is None:
        nc = bass.Bass("TRN2", target_bir_lowering=False,
                       detect_race_conditions=False)
        xn_in = nc.dram_tensor("xn_bf", [D, TOK], BF16, kind="ExternalInput").ap()
        hin_in = nc.dram_tensor("hin_bf", [D, TOK], BF16,
                                kind="ExternalInput").ap()
        h_in = nc.dram_tensor("h_fm", [D, TOK], BF16, kind="ExternalInput").ap()
        o_att = nc.dram_tensor("attn_out", [D, TOK], F32,
                               kind="ExternalOutput").ap()
    else:
        nc = fz.nc
        xn_in = fz.dram["xn_bf"]
        hin_in = fz.dram["hin_bf"]
        h_in = None
        o_att = fz.dram["attn_out"]
    wog = nc.dram_tensor("w_og", [D, D], F32, kind="ExternalInput").ap()
    wout = nc.dram_tensor("w_out", [D, D], F32, kind="ExternalInput").ap()
    n2w = nc.dram_tensor("n2w", [128, NCH], F32, kind="ExternalInput").ap()
    wgate = nc.dram_tensor("w_gate", [D, FF_IN], F32, kind="ExternalInput").ap()
    wup = nc.dram_tensor("w_up", [D, FF_IN], F32, kind="ExternalInput").ap()
    wdown = nc.dram_tensor("w_down", [FF_IN, D], F32, kind="ExternalInput").ap()
    x1_dram = nc.dram_tensor("x1_scr", [D, TOK], F32, kind="Internal").ap()
    h3_dram = nc.dram_tensor("h3_scr", [FF_IN, TOK], BF16, kind="Internal").ap()

    ones_col = nc.inline_tensor(np.ones((128, 1), np.float32),
                                pre + "l3_oc").ap()
    ones_row = nc.inline_tensor(np.ones((1, 128), np.float32),
                                pre + "l3_or").ap()

    with ExitStack() as ctx:
        def sb(name, shape, dtype=F32, c=None):
            return (c or ctx).enter_context(nc.sbuf_tensor(pre + name, shape, dtype))

        def ps(name, shape, dtype=F32):
            return ctx.enter_context(nc.psum_tensor(pre + name, shape, dtype))

        s_w0 = sb("s_w0", [128, NCH * D], BF16)
        s_sg = sb("s_sg", [128, 512])
        s_g = sb("s_g", [128, 512], BF16)
        s_h3b = sb("s_h3b", [128, 512], BF16)
        s_sq = sb("s_sq", [128, 512])
        s_row = sb("s_row", [1, 512])
        s_nw = sb("s_nw", [128, NCH])
        ones_sb = sb("ones_sb", [128, 1])
        ones_rsb = sb("ones_rsb", [1, 128])

        ps_a = ps("ps_a", [128, 512])
        ps_b = ps("ps_b", [128, 512])
        ps_r = ps("ps_r", [1, 512])
        ps_c = ps("ps_c", [128, 512])

        if fz is None:
            sW = Ctr(ctx.enter_context(nc.semaphore("sW")))
            sX = Ctr(ctx.enter_context(nc.semaphore("sX")))
            sPE = Ctr(ctx.enter_context(nc.semaphore("sPE")))
            sDV = Ctr(ctx.enter_context(nc.semaphore("sDV")))
            sAC = Ctr(ctx.enter_context(nc.semaphore("sAC")))
            sST = Ctr(ctx.enter_context(nc.semaphore("sST")))
        else:
            sW, sX, sPE, sDV, sAC, sST = (fz.sW, fz.sX, fz.sPE, fz.sDV,
                                          fz.sAC, fz.sST)

        c_h2 = ExitStack()
        s_h2 = sb("s_h2", [128, 4 * NCH * 512], BF16, c_h2)

        with (nc.Block() if fz is None else nullcontext(fz.block)) as block:
            step, stepseq = mk_steps(block)

            @block.sync
            def _(s):
                s.dma_start(out=s_nw[:], in_=n2w).then_inc(sX.h, 16)
                s.dma_start(out=ones_sb[:], in_=ones_col).then_inc(sX.h, 16)
                s.dma_start(out=ones_rsb[:], in_=ones_row).then_inc(sX.h, 16)
                sX.n += 48

            nw_ld = sX.n
            evg = {}
            rst_read = [0]
            tile_consumed = [0]
            w_guard = [0]

            c1 = ExitStack()
            s_xn = sb("s_xn", [128, NCH * 512], BF16, c1)
            s_xf = sb("s_xf", [128, 2 * 512], BF16, c1)
            s_hf = sb("s_hf", [128, 2 * 512], BF16, c1)
            s_ho = sb("s_ho", [128, NCH * 512], BF16, c1)
            s_x1 = sb("s_x1", [128, NCH * 512], F32, c1)
            hf_guard = {}
            xf_guard = {}

            for t in range(NT):
                @block.sync
                def _(s, t=t):
                    if t > 0:
                        s.wait_ge(sDV.h, tile_consumed[0])
                    s.dma_start(out=sb3(s_xn[:], 512),
                                in_=fmload(hin_in[:, t * 512:(t + 1) * 512], 512)
                                ).then_inc(sX.h, 16)
                    sX.n += 16

                xld = sX.n
                hf_ld = {}

                @block.gpsimd
                def _(g, t=t):
                    g.wait_ge(sPE.h, w_guard[0])
                    g.dma_start(out=sb3(s_w0[:], D), in_=fmload(wog, D)
                                ).then_inc(sW.h, 16)
                    sW.n += 16

                wld = sW.n
                for c in range(NCH):
                    @block.sync
                    def _(s, t=t, c=c):
                        sem, val = hf_guard.get(c % 2, (None, 0))
                        if val:
                            s.wait_ge(sem, val)
                        if fz is None:
                            src = h_in[c * 128:(c + 1) * 128,
                                       t * 512:(t + 1) * 512]
                        else:
                            e = 2 * t + c // 8
                            r0 = e * 1024 + (c % 8) * 128
                            src = fz.a2a2_out[r0:r0 + 128, 0:512]
                        s.dma_start(
                            out=s_hf[:, (c % 2) * 512:(c % 2 + 1) * 512],
                            in_=src).then_inc(sX.h, 16)
                        sX.n += 16

                    hf_ld[c] = sX.n
                    pd, pdn = (ps_a, "a") if c % 2 == 0 else (ps_b, "b")
                    mm = step(block.tensor, sPE,
                              [(sW.h, wld), (sX.h, xld),
                               evg.get(pdn, (None, 0))],
                              lambda te, c=c, pd=pd: [
                                  te.matmul(pd[:],
                                            s_w0[:, k * D + c * 128:
                                                 k * D + (c + 1) * 128],
                                            s_xn[:, k * 512:(k + 1) * 512],
                                            start=(k == 0), stop=(k == NCH - 1))
                                  for k in range(NCH)][-1])
                    sg = step(block.scalar, sAC,
                              [(sPE.h, mm), evg.get("sg", (None, 0))],
                              lambda sc, pd=pd: sc.activation(s_sg[:], pd[:],
                                                              AF.Sigmoid))
                    evg[pdn] = (sAC.h, sg)
                    ho = step(block.vector, sDV,
                              [(sAC.h, sg), (sX.h, hf_ld[c])],
                              lambda v, c=c: v.tensor_mul(
                                  s_ho[:, c * 512:(c + 1) * 512],
                                  s_sg[:], s_hf[:, (c % 2) * 512:(c % 2 + 1) * 512]))
                    evg["sg"] = (sDV.h, ho)
                    hf_guard[c % 2] = (sDV.h, ho)
                hout_done = sDV.n
                w_guard[0] = sPE.n

                @block.gpsimd
                def _(g, t=t):
                    g.wait_ge(sPE.h, w_guard[0])
                    g.dma_start(out=sb3(s_w0[:], D), in_=fmload(wout, D)
                                ).then_inc(sW.h, 16)
                    sW.n += 16

                wld = sW.n
                for c in range(NCH):
                    @block.sync
                    def _(s, t=t, c=c):
                        sem, val = xf_guard.get(c % 2, (None, 0))
                        if val:
                            s.wait_ge(sem, val)
                        s.dma_start(
                            out=s_xf[:, (c % 2) * 512:(c % 2 + 1) * 512],
                            in_=xn_in[c * 128:(c + 1) * 128,
                                      t * 512:(t + 1) * 512]).then_inc(sX.h, 16)
                        sX.n += 16

                    xf_ld = sX.n
                    pd, pdn = (ps_a, "a") if c % 2 == 0 else (ps_b, "b")
                    mm = step(block.tensor, sPE,
                              [(sW.h, wld), (sDV.h, hout_done),
                               evg.get(pdn, (None, 0))],
                              lambda te, c=c, pd=pd: [
                                  te.matmul(pd[:],
                                            s_w0[:, k * D + c * 128:
                                                 k * D + (c + 1) * 128],
                                            s_ho[:, k * 512:(k + 1) * 512],
                                            start=(k == 0), stop=(k == NCH - 1))
                                  for k in range(NCH)][-1])
                    x1c = step(block.vector, sDV,
                               [(sPE.h, mm), (sX.h, xf_ld)],
                               lambda v, c=c, pd=pd: v.tensor_add(
                                   s_x1[:, c * 512:(c + 1) * 512], pd[:],
                                   s_xf[:, (c % 2) * 512:(c % 2 + 1) * 512]))
                    evg[pdn] = (sDV.h, x1c)
                    xf_guard[c % 2] = (sDV.h, x1c)
                    sq = step(block.scalar, sAC,
                              [(sDV.h, x1c), evg.get("sq", (None, 0))],
                              lambda sc, c=c: sc.activation(
                                  s_sq[:], s_x1[:, c * 512:(c + 1) * 512],
                                  AF.Square))
                    cs = step(block.tensor, sPE,
                              [(sAC.h, sq),
                               (sDV.h, rst_read[0]) if c == 0 else (None, 0)],
                              lambda te, c=c: te.matmul(
                                  ps_r[:], ones_sb[:], s_sq[:],
                                  start=(c == 0), stop=(c == NCH - 1)))
                    evg["sq"] = (sPE.h, cs)
                x1_done = sDV.n
                sq_done = sPE.n
                w_guard[0] = sPE.n

                @block.sync
                def _(s, t=t, n=x1_done):
                    s.wait_ge(sDV.h, n)
                    s.dma_start(
                        out=fmload(x1_dram[:, t * 512:(t + 1) * 512], 512),
                        in_=sb3(s_x1[:], 512)).then_inc(sST.h, 16)
                    sST.n += 16

                rrow = step(block.vector, sDV, [(sPE.h, sq_done)],
                            lambda v: v.tensor_scalar(
                                s_row[:], ps_r[:], 1.0 / D, EPS_RMS,
                                ALU.mult, ALU.add))
                rs = step(block.scalar, sAC, [(sDV.h, rrow)],
                          lambda sc: sc.activation(s_row[:], s_row[:], AF.Sqrt))
                rr = step(block.vector, sDV, [(sAC.h, rs)],
                          lambda v: v.reciprocal(s_row[:], s_row[:]))
                rst_read[0] = rr
                bc = step(block.tensor, sPE,
                          [(sDV.h, rr), evg.get("c", (None, 0))],
                          lambda te: te.matmul(ps_c[:], ones_rsb[:], s_row[:],
                                               start=True, stop=True))
                h2l = rr
                for c in range(NCH):
                    h2l = step(block.vector, sDV,
                               [(sPE.h, bc), (sX.h, nw_ld)],
                               lambda v, c=c, t=t: v.scalar_tensor_tensor(
                                   s_h2[:, (t * NCH + c) * 512:
                                        (t * NCH + c + 1) * 512],
                                   s_x1[:, c * 512:(c + 1) * 512],
                                   s_nw[:, c:c + 1], ps_c[:],
                                   ALU.mult, ALU.mult))
                evg["c"] = (sDV.h, h2l)
                tile_consumed[0] = h2l

            c1.close()

            h3_last = [0]
            for mg in range(16):
                @block.gpsimd
                def _(g, mg=mg):
                    g.wait_ge(sPE.h, w_guard[0])
                    g.dma_start(out=sb3(s_w0[:, 0:NCH * 512], 512),
                                in_=fmload(wgate[:, mg * 512:(mg + 1) * 512], 512)
                                ).then_inc(sW.h, 16)
                    g.dma_start(out=sb3(s_w0[:, NCH * 512:2 * NCH * 512], 512),
                                in_=fmload(wup[:, mg * 512:(mg + 1) * 512], 512)
                                ).then_inc(sW.h, 16)
                    sW.n += 32

                wld = sW.n
                UPOFF = NCH * 512
                for t in range(NT):
                    for mc in range(4):
                        pg, pgn = (ps_a, "a") if mc % 2 == 0 else (ps_b, "b")
                        mmg = step(block.tensor, sPE,
                                   [(sW.h, wld), evg.get(pgn, (None, 0))],
                                   lambda te, t=t, mc=mc, pg=pg: [
                                       te.matmul(pg[:],
                                                 s_w0[:, k * 512 + mc * 128:
                                                      k * 512 + (mc + 1) * 128],
                                                 s_h2[:, (t * NCH + k) * 512:
                                                      (t * NCH + k + 1) * 512],
                                                 start=(k == 0),
                                                 stop=(k == NCH - 1))
                                       for k in range(NCH)][-1])
                        gs = step(block.scalar, sAC,
                                  [(sPE.h, mmg), evg.get("g", (None, 0))],
                                  lambda sc, pg=pg: sc.activation(s_g[:], pg[:],
                                                                  AF.Silu))
                        evg[pgn] = (sAC.h, gs)
                        mmu = step(block.tensor, sPE,
                                   [evg.get("c", (None, 0))],
                                   lambda te, t=t, mc=mc: [
                                       te.matmul(ps_c[:],
                                                 s_w0[:, UPOFF + k * 512 + mc * 128:
                                                      UPOFF + k * 512 + (mc + 1) * 128],
                                                 s_h2[:, (t * NCH + k) * 512:
                                                      (t * NCH + k + 1) * 512],
                                                 start=(k == 0),
                                                 stop=(k == NCH - 1))
                                       for k in range(NCH)][-1])
                        h3c = step(block.vector, sDV,
                                   [(sPE.h, mmu), (sAC.h, gs),
                                    (sST.h, h3_last[0])],
                                   lambda v: v.tensor_mul(s_h3b[:], ps_c[:],
                                                          s_g[:]))
                        evg["c"] = (sDV.h, h3c)
                        evg["g"] = (sDV.h, h3c)

                        @block.sync
                        def _(s, t=t, mc=mc, mg=mg, n=h3c):
                            s.wait_ge(sDV.h, n)
                            s.dma_start(
                                out=h3_dram[mg * 512 + mc * 128:
                                            mg * 512 + (mc + 1) * 128,
                                            t * 512:(t + 1) * 512],
                                in_=s_h3b[:]).then_inc(sST.h, 16)
                            sST.n += 16

                        h3_last[0] = sST.n
                w_guard[0] = sPE.n

            c_h2.close()

            with ExitStack() as c2:
                s_h3 = sb("s_h3", [128, 64 * 512], BF16, c2)
                s_x1b = sb("s_x1b", [128, NCH * 512], F32, c2)
                s_acc = sb("s_acc", [128, NCH * 512], F32, c2)
                pb_guard = [0]
                for t in range(NT):
                    @block.sync
                    def _(s, t=t):
                        if t > 0:
                            s.wait_ge(sST.h, pb_guard[0])
                        s.dma_start(out=sb3(s_h3[:], 512),
                                    in_=fmload(h3_dram[:, t * 512:(t + 1) * 512],
                                               512)).then_inc(sX.h, 16)
                        s.dma_start(out=sb3(s_x1b[:], 512),
                                    in_=fmload(x1_dram[:, t * 512:(t + 1) * 512],
                                               512)).then_inc(sX.h, 16)
                        sX.n += 32

                    h3ld = sX.n
                    for kg in range(8):
                        half = (kg % 2) * NCH * 1024
                        @block.gpsimd
                        def _(g, kg=kg, half=half):
                            g.wait_ge(sPE.h, w_guard[0])
                            g.dma_start(
                                out=sb3(s_w0[:, half:half + 8 * D], D),
                                in_=fmload(wdown[kg * 1024:(kg + 1) * 1024, :], D)
                                ).then_inc(sW.h, 16)
                            sW.n += 16

                        wld = sW.n
                        for c in range(NCH):
                            pd, pdn = (ps_a, "a") if c % 2 == 0 else (ps_b, "b")
                            mm = step(block.tensor, sPE,
                                      [(sW.h, wld), (sX.h, h3ld),
                                       evg.get(pdn, (None, 0))],
                                      lambda te, c=c, pd=pd, kg=kg, half=half: [
                                          te.matmul(
                                              pd[:],
                                              s_w0[:, half + k * D + c * 128:
                                                   half + k * D + (c + 1) * 128],
                                              s_h3[:, (kg * 8 + k) * 512:
                                                   (kg * 8 + k + 1) * 512],
                                              start=(k == 0), stop=(k == 7))
                                          for k in range(8)][-1])
                            if kg == 0:
                                acc = step(block.vector, sDV, [(sPE.h, mm)],
                                           lambda v, c=c, pd=pd: v.tensor_copy(
                                               s_acc[:, c * 512:(c + 1) * 512],
                                               pd[:]))
                            elif kg < 7:
                                acc = step(block.vector, sDV, [(sPE.h, mm)],
                                           lambda v, c=c, pd=pd: v.tensor_add(
                                               s_acc[:, c * 512:(c + 1) * 512],
                                               s_acc[:, c * 512:(c + 1) * 512],
                                               pd[:]))
                            else:
                                acc = stepseq(block.vector, sDV, [(sPE.h, mm)], [
                                    lambda v, c=c, pd=pd: v.tensor_add(
                                        s_acc[:, c * 512:(c + 1) * 512],
                                        s_acc[:, c * 512:(c + 1) * 512], pd[:]),
                                    lambda v, c=c: v.tensor_add(
                                        s_acc[:, c * 512:(c + 1) * 512],
                                        s_acc[:, c * 512:(c + 1) * 512],
                                        s_x1b[:, c * 512:(c + 1) * 512])])
                            evg[pdn] = (sDV.h, acc)
                        w_guard[0] = sPE.n if kg > 0 else w_guard[0]

                    @block.sync
                    def _(s, t=t, n=sDV.n):
                        s.wait_ge(sDV.h, n)
                        s.dma_start(
                            out=fmload(o_att[:, t * 512:(t + 1) * 512], 512),
                            in_=sb3(s_acc[:], 512)).then_inc(sST.h, 16)
                        sST.n += 16

                    pb_guard[0] = sST.n

            @block.sync
            def _(s):
                s.wait_ge(sST.h, sST.n)

    return nc



NMG = DFF // 512


def build_l4(fz=None):
    pre = "" if fz is None else "z4_"
    if fz is None:
        nc = bass.Bass("TRN2", target_bir_lowering=False,
                       detect_race_conditions=False)
        xn_in = nc.dram_tensor("xn_bf", [D, TOK], BF16, kind="ExternalInput").ap()
        at_in = nc.dram_tensor("attn_in", [D, TOK], F32,
                               kind="ExternalInput").ap()
        x_in = nc.dram_tensor("x_fm", [D, TOK], F32, kind="ExternalInput").ap()
        o_x2 = nc.dram_tensor("x2", [D, TOK], F32, kind="ExternalOutput").ap()
        o_xn1 = nc.dram_tensor("xn1", [D, TOK], BF16, kind="ExternalOutput").ap()
    else:
        nc = fz.nc
        xn_in = fz.dram["xn_bf"]
        at_in = fz.dram["attn_out"]
        x_in = fz.dram["x_fm"]
        o_x2 = fz.dram["x2"]
        o_xn1 = fz.dram["xn1"]
    wgT = nc.dram_tensor("wgT", [D, DFF], F32, kind="ExternalInput").ap()
    wuT = nc.dram_tensor("wuT", [D, DFF], F32, kind="ExternalInput").ap()
    wdT = nc.dram_tensor("wdT", [DFF, D], F32, kind="ExternalInput").ap()
    wrT = nc.dram_tensor("wrT", [D, 8], F32, kind="ExternalInput").ap()
    agT = nc.dram_tensor("agT", [D, 128], F32, kind="ExternalInput").ap()
    auT = nc.dram_tensor("auT", [D, 128], F32, kind="ExternalInput").ap()
    bgT = nc.dram_tensor("bgT", [128, DFF], F32, kind="ExternalInput").ap()
    buT = nc.dram_tensor("buT", [128, DFF], F32, kind="ExternalInput").ap()
    adT = nc.dram_tensor("adT", [DFF, 128], F32, kind="ExternalInput").ap()
    bdT = nc.dram_tensor("bdT", [128, D], F32, kind="ExternalInput").ap()
    csel = nc.dram_tensor("csel", [16, 8 * 128], F32, kind="ExternalInput").ap()
    crep = nc.dram_tensor("crep", [8, 128], F32, kind="ExternalInput").ap()
    cmask = nc.dram_tensor("cmask", [128, 16], F32, kind="ExternalInput").ap()
    hb_dram = nc.dram_tensor("hb_scr", [DFF, TOK], BF16, kind="Internal").ap()

    ones_col = nc.inline_tensor(np.ones((128, 1), np.float32),
                                pre + "l4_oc").ap()
    ones_row = nc.inline_tensor(np.ones((1, 128), np.float32),
                                pre + "l4_or").ap()
    ident_c = nc.inline_tensor(np.eye(128, dtype=np.float32),
                               pre + "l4_id").ap()

    with ExitStack() as ctx:
        def sb(name, shape, dtype=F32, c=None):
            return (c or ctx).enter_context(nc.sbuf_tensor(pre + name, shape, dtype))

        def ps(name, shape, dtype=F32):
            return ctx.enter_context(nc.psum_tensor(pre + name, shape, dtype))

        s_wg = sb("s_wg", [128, NCH * 512], BF16)
        s_wu = sb("s_wu", [128, NCH * 512], BF16)
        s_wd = sb("s_wd", [128, 8 * D], BF16)
        s_xn = sb("s_xn", [128, NCH * 512], BF16)
        s_hin = sb("s_hin", [128, NCH * 512], BF16)
        s_n1 = sb("s_n1", [128, NCH])
        s_ag = sb("s_ag", [128, NCH * 128], BF16)
        s_au = sb("s_au", [128, NCH * 128], BF16)
        s_bg = sb("s_bg", [128, 512], BF16)
        s_bu = sb("s_bu", [128, 512], BF16)
        s_ad = sb("s_ad", [128, 512], BF16)
        s_bd = sb("s_bd", [128, D], BF16)
        s_wr = sb("s_wr", [128, NCH * 8], BF16)
        s_sel = sb("s_sel", [16, 8 * 128])
        s_rep = sb("s_rep", [8, 128])
        s_msk = sb("s_msk", [128, 16])
        s_id = sb("s_id", [128, 128])
        s_rg = sb("s_rg", [128, 512], BF16)
        s_ru = sb("s_ru", [128, 512], BF16)
        s_sg2 = s_rg
        s_hb2 = s_ru
        s_rgd = sb("s_rgd", [128, 8 * 512], BF16)
        s_rud = sb("s_rud", [128, 8 * 512], BF16)
        s_rt = sb("s_rt", [128, 8])
        s_rt2 = sb("s_rt2", [128, 8])
        s_ew = sb("s_ew", [128, 8])
        s_ewr = sb("s_ewr", [8, 512])
        s_rd = [sb(f"s_rd{e}", [16, 512]) for e in range(8)]
        s_rdb = sb("s_rdb", [128, 512], BF16)
        s_sg = sb("s_sg", [128, 512], BF16)
        s_hb = sb("s_hb", [128, 512], BF16)
        s_h3 = sb("s_h3", [128, 8 * 512], BF16)
        s_x2 = sb("s_x2", [128, NCH * 512])
        s_at = sb("s_at", [128, 512])
        s_xf = sb("s_xf", [128, 512])
        s_row = sb("s_row", [1, 2 * 512])
        s_sq = sb("s_sq", [128, 512])
        ones_sb = sb("ones_sb", [128, 1])
        ones_rsb = sb("ones_rsb", [1, 128])

        ps_g = ps("ps_g", [128, 512])
        ps_u = ps("ps_u", [128, 512])
        ps_r = [ps("ps_r0", [16, 512]), ps("ps_r1", [16, 512])]
        ps_m = ps("ps_m", [128, 512])
        ps_m2 = ps("ps_m2", [128, 512])
        ps_s1 = ps("ps_s1", [1, 512])
        ps_s2 = ps("ps_s2", [1, 512])

        if fz is None:
            sW = Ctr(ctx.enter_context(nc.semaphore("sW")))
            sX = Ctr(ctx.enter_context(nc.semaphore("sX")))
            sPE = Ctr(ctx.enter_context(nc.semaphore("sPE")))
            sDV = Ctr(ctx.enter_context(nc.semaphore("sDV")))
            sAC = Ctr(ctx.enter_context(nc.semaphore("sAC")))
            sST = Ctr(ctx.enter_context(nc.semaphore("sST")))
        else:
            sW, sX, sPE, sDV, sAC, sST = (fz.sW, fz.sX, fz.sPE, fz.sDV,
                                          fz.sAC, fz.sST)

        with (nc.Block() if fz is None else nullcontext(fz.block)) as block:
            step, stepseq = mk_steps(block)

            @block.gpsimd
            def _(g):
                for src_, dst, cols in ((agT, s_ag, 128), (auT, s_au, 128),
                                        (bdT, s_bd, None), (wrT, s_wr, 8)):
                    if cols is None:
                        g.dma_start(out=dst[:], in_=src_).then_inc(sW.h, 16)
                    else:
                        g.dma_start(out=sb3(dst[:], cols),
                                    in_=fmload(src_, cols)).then_inc(sW.h, 16)
                    sW.n += 16

            @block.sync
            def _(s):
                s.dma_start(out=s_sel[:], in_=csel).then_inc(sX.h, 16)
                s.dma_start(out=s_rep[:], in_=crep).then_inc(sX.h, 16)
                s.dma_start(out=s_msk[:], in_=cmask).then_inc(sX.h, 16)
                s.dma_start(out=s_id[:], in_=ident_c).then_inc(sX.h, 16)
                s.dma_start(out=ones_sb[:], in_=ones_col).then_inc(sX.h, 16)
                s.dma_start(out=ones_rsb[:], in_=ones_row).then_inc(sX.h, 16)
                sX.n += 96

            const_ld = sX.n
            wA_ld = sW.n
            evg = {}
            guard = {"xn_st": 0, "h3": 0, "stats": 0, "xn_pe": 0}
            wgu_guard = [0]
            wd_guard = [0]

            for t in range(NT):
                @block.sync
                def _(s, t=t):
                    if guard["xn_pe"]:
                        s.wait_ge(sPE.h, guard["xn_pe"])
                    if guard["xn_st"]:
                        s.wait_ge(sST.h, guard["xn_st"])
                    s.dma_start(out=sb3(s_xn[:], 512),
                                in_=fmload(xn_in[:, t * 512:(t + 1) * 512], 512)
                                ).then_inc(sX.h, 16)
                    sX.n += 16

                xld = sX.n

                for tc in range(4):
                    mm = step(block.tensor, sPE,
                              [(sW.h, wA_ld), (sX.h, xld), (sX.h, const_ld),
                               evg.get("m", (None, 0))],
                              lambda te, tc=tc: [
                                  te.matmul(ps_m[0:128, 0:8],
                                            s_xn[:, k * 512 + tc * 128:
                                                 k * 512 + (tc + 1) * 128],
                                            s_wr[:, k * 8:(k + 1) * 8],
                                            start=(k == 0), stop=(k == NCH - 1))
                                  for k in range(NCH)][-1])
                    sm = stepseq(block.vector, sDV, [(sPE.h, mm)], [
                        lambda v: v.tensor_copy(s_rt[:], ps_m[0:128, 0:8]),
                        lambda v: v.reduce_max(s_ew[:, 0:1], s_rt[:], axis=AX.X),
                        lambda v: v.tensor_scalar(s_rt[:], s_rt[:],
                                                  s_ew[:, 0:1], None,
                                                  ALU.subtract)])
                    evg["m"] = (sDV.h, sm)
                    ex = step(block.scalar, sAC,
                              [(sDV.h, sm), evg.get("rt_ac", (None, 0))],
                              lambda sc: sc.activation(s_rt[:], s_rt[:], AF.Exp,
                                                       accum_out=s_ew[:, 1:2]))
                    evg["rt_ac"] = (sAC.h, ex)
                    pr = stepseq(block.vector, sDV, [(sAC.h, ex)], [
                        lambda v: v.reciprocal(s_ew[:, 2:3], s_ew[:, 1:2]),
                        lambda v: v.tensor_scalar(s_rt[:], s_rt[:],
                                                  s_ew[:, 2:3], None, ALU.mult),
                        lambda v: v.reduce_max(s_ew[:, 3:4], s_rt[:], axis=AX.X),
                        lambda v: v.tensor_scalar(s_rt2[:], s_rt[:],
                                                  s_ew[:, 3:4], None, ALU.is_ge),
                        lambda v: v.scalar_tensor_tensor(s_rt2[:], s_rt2[:],
                                                         -1e30, s_rt[:],
                                                         ALU.mult, ALU.add),
                        lambda v: v.reduce_max(s_ew[:, 5:6], s_rt2[:], axis=AX.X),
                        lambda v: v.tensor_scalar(s_rt2[:], s_rt[:],
                                                  s_ew[:, 5:6], None, ALU.is_ge),
                        lambda v: v.tensor_mul(s_rt2[:], s_rt2[:], s_rt[:])])
                    tr = step(block.tensor, sPE,
                              [(sDV.h, pr), evg.get("m2", (None, 0))],
                              lambda te: te.matmul(ps_m2[0:8, 0:128], s_rt2[:],
                                                   s_id[:], start=True,
                                                   stop=True))
                    ev = step(block.vector, sDV, [(sPE.h, tr)],
                              lambda v, tc=tc: v.tensor_copy(
                                  s_ewr[:, tc * 128:(tc + 1) * 128],
                                  ps_m2[0:8, 0:128]))
                    evg["m2"] = (sDV.h, ev)
                ew_done = sDV.n

                rmm = step(block.tensor, sPE, [evg.get("m", (None, 0))],
                           lambda te: [
                               te.matmul(ps_m[:],
                                         s_ag[:, k * 128:(k + 1) * 128],
                                         s_xn[:, k * 512:(k + 1) * 512],
                                         start=(k == 0), stop=(k == NCH - 1))
                               for k in range(NCH)][-1])
                rg = step(block.vector, sDV, [(sPE.h, rmm)],
                          lambda v: v.tensor_scalar(s_rg[:], ps_m[:],
                                                    MOE_SCALING, None, ALU.mult))
                evg["m"] = (sDV.h, rg)
                rmm2 = step(block.tensor, sPE, [evg["m"]],
                            lambda te: [
                                te.matmul(ps_m[:],
                                          s_au[:, k * 128:(k + 1) * 128],
                                          s_xn[:, k * 512:(k + 1) * 512],
                                          start=(k == 0), stop=(k == NCH - 1))
                                for k in range(NCH)][-1])
                ru = step(block.vector, sDV, [(sPE.h, rmm2)],
                          lambda v: v.tensor_scalar(s_ru[:], ps_m[:],
                                                    MOE_SCALING, None, ALU.mult))
                evg["m"] = (sDV.h, ru)
                rdl = ru
                for e in range(8):
                    rdl = step(block.vector, sDV, [(sDV.h, rdl)],
                               lambda v, e=e: v.tensor_scalar(
                                   s_rgd[:, e * 512:(e + 1) * 512], s_rg[:],
                                   s_msk[:, 8 + e:9 + e], None, ALU.mult))
                    rdl = step(block.vector, sDV, [(sDV.h, rdl)],
                               lambda v, e=e: v.tensor_scalar(
                                   s_rud[:, e * 512:(e + 1) * 512], s_ru[:],
                                   s_msk[:, 8 + e:9 + e], None, ALU.mult))
                r_done = rdl

                rd_init = [False] * 8
                for mg in range(NMG):
                    @block.gpsimd
                    def _(g, mg=mg):
                        g.wait_ge(sPE.h, wgu_guard[0])
                        g.dma_start(out=sb3(s_wg[:], 512),
                                    in_=fmload(wgT[:, mg * 512:(mg + 1) * 512],
                                               512)).then_inc(sW.h, 16)
                        g.dma_start(out=sb3(s_wu[:], 512),
                                    in_=fmload(wuT[:, mg * 512:(mg + 1) * 512],
                                               512)).then_inc(sW.h, 16)
                        g.dma_start(out=s_bg[:],
                                    in_=bgT[:, mg * 512:(mg + 1) * 512]
                                    ).then_inc(sW.h, 16)
                        g.dma_start(out=s_bu[:],
                                    in_=buT[:, mg * 512:(mg + 1) * 512]
                                    ).then_inc(sW.h, 16)
                        g.dma_start(out=s_ad[:].rearrange("p (c m) -> p c m", m=128),
                                    in_=adT[mg * 512:(mg + 1) * 512, :].rearrange(
                                        "(c p) m -> p c m", p=128)
                                    ).then_inc(sW.h, 16)
                        sW.n += 80

                    wld = sW.n
                    for mc0 in (0, 2):
                        SS = []
                        for si in (0, 1):
                            mc = mc0 + si
                            SS.append({
                                "si": si, "mc": mc, "mgc": mg * 4 + mc,
                                "pg": ps_g if si == 0 else ps_m,
                                "pu": ps_u if si == 0 else ps_m2,
                                "sg": s_sg if si == 0 else s_sg2,
                                "hb": s_hb if si == 0 else s_hb2,
                                "kg": "gps" if si == 0 else "m",
                                "ku": "ups" if si == 0 else "m2",
                                "ksg": "sg" if si == 0 else "sgB",
                                "kh3": "h3" if si == 0 else "h3B",
                            })
                        for S in SS:
                            S["gmm"] = step(
                                block.tensor, sPE,
                                [(sW.h, wld), (sDV.h, r_done),
                                 evg.get(S["kg"], (None, 0))],
                                lambda te, S=S: [
                                    te.matmul(S["pg"][:],
                                              s_wg[:, k * 512 + S["mc"] * 128:
                                                   k * 512 + (S["mc"] + 1) * 128],
                                              s_xn[:, k * 512:(k + 1) * 512],
                                              start=(k == 0),
                                              stop=(k == NCH - 1))
                                    for k in range(NCH)][-1])
                            S["umm"] = step(
                                block.tensor, sPE,
                                [evg.get(S["ku"], (None, 0))],
                                lambda te, S=S: [
                                    te.matmul(S["pu"][:],
                                              s_wu[:, k * 512 + S["mc"] * 128:
                                                   k * 512 + (S["mc"] + 1) * 128],
                                              s_xn[:, k * 512:(k + 1) * 512],
                                              start=(k == 0),
                                              stop=(k == NCH - 1))
                                    for k in range(NCH)][-1])
                        for S in SS:
                            S["sgb"] = step(
                                block.scalar, sAC,
                                [(sPE.h, S["gmm"]),
                                 evg.get(S["ksg"], (None, 0))],
                                lambda sc, S=S: sc.activation(
                                    S["sg"][:], S["pg"][:], AF.Silu))
                            S["hbb"] = step(
                                block.vector, sDV,
                                [(sAC.h, S["sgb"]), (sPE.h, S["umm"]),
                                 (sST.h, guard.get(S["kh3"], 0))],
                                lambda v, S=S: v.tensor_mul(
                                    S["hb"][:], S["pu"][:], S["sg"][:]))
                            evg[S["ksg"]] = (sDV.h, S["hbb"])

                            @block.sync
                            def _(s, t=t, S=S, n=None):
                                s.wait_ge(sDV.h, S["hbb"])
                                s.dma_start(
                                    out=hb_dram[S["mgc"] * 128:
                                                (S["mgc"] + 1) * 128,
                                                t * 512:(t + 1) * 512],
                                    in_=S["hb"][:]).then_inc(sST.h, 16)
                                sST.n += 16

                            guard[S["kh3"]] = sST.n
                            S["g_reader"] = (sAC.h, S["sgb"])
                            S["u_reader"] = (sDV.h, S["hbb"])
                        for e in range(8):
                            for S in SS:
                                S["gdm"] = step(
                                    block.tensor, sPE, [S["g_reader"]],
                                    lambda te, S=S, e=e: te.matmul(
                                        S["pg"][:],
                                        s_bg[:, S["mc"] * 128:
                                             (S["mc"] + 1) * 128],
                                        s_rgd[:, e * 512:(e + 1) * 512],
                                        start=False, stop=True,
                                        skip_group_check=True))
                                S["udm"] = step(
                                    block.tensor, sPE, [S["u_reader"]],
                                    lambda te, S=S, e=e: te.matmul(
                                        S["pu"][:],
                                        s_bu[:, S["mc"] * 128:
                                             (S["mc"] + 1) * 128],
                                        s_rud[:, e * 512:(e + 1) * 512],
                                        start=False, stop=True,
                                        skip_group_check=True))
                            for S in SS:
                                S["sge"] = step(
                                    block.scalar, sAC,
                                    [(sPE.h, S["gdm"]),
                                     evg.get(S["ksg"], (None, 0))],
                                    lambda sc, S=S: sc.activation(
                                        S["sg"][:], S["pg"][:], AF.Silu))
                                S["hbe"] = step(
                                    block.vector, sDV,
                                    [(sAC.h, S["sge"]), (sPE.h, S["udm"]),
                                     (sST.h, guard.get(S["kh3"], 0)) if e == 0
                                     else (None, 0)],
                                    lambda v, S=S: v.tensor_mul(
                                        S["hb"][:], S["pu"][:], S["sg"][:]))
                                evg[S["ksg"]] = (sDV.h, S["hbe"])
                                S["g_reader"] = (sAC.h, S["sge"])
                                S["u_reader"] = (sDV.h, S["hbe"])
                            for S in SS:
                                prd = ps_r[S["si"]]
                                S["amm"] = step(
                                    block.tensor, sPE,
                                    [(sDV.h, S["hbe"]),
                                     evg.get(("rd", S["si"]), (None, 0))],
                                    lambda te, S=S, e=e, prd=prd: te.matmul(
                                        prd[:],
                                        s_ad[:, S["mc"] * 128 + e * 16:
                                             S["mc"] * 128 + e * 16 + 16],
                                        S["hb"][:], start=True, stop=True))
                            for S in SS:
                                prd = ps_r[S["si"]]
                                waits = [(sPE.h, S["amm"]),
                                         evg.get(("rdw", e), (None, 0))]
                                if not rd_init[e]:
                                    rdev = step(
                                        block.vector, sDV, waits,
                                        lambda v, e=e, prd=prd:
                                        v.tensor_copy(s_rd[e][:], prd[:]))
                                    rd_init[e] = True
                                else:
                                    rdev = step(
                                        block.vector, sDV, waits,
                                        lambda v, e=e, prd=prd:
                                        v.tensor_add(s_rd[e][:],
                                                     s_rd[e][:], prd[:]))
                                evg[("rd", S["si"])] = (sDV.h, rdev)
                                evg[("rdw", e)] = (sDV.h, rdev)
                        evg["gps"] = SS[0]["g_reader"]
                        evg["ups"] = SS[0]["u_reader"]
                        evg["m"] = SS[1]["g_reader"]
                        evg["m2"] = SS[1]["u_reader"]
                    wgu_guard[0] = sPE.n

                stk = step(block.tensor, sPE,
                           [evg.get(("rd", 0), (None, 0)),
                            evg.get(("rd", 1), (None, 0)),
                            evg.get("m", (None, 0))],
                           lambda te: [
                               te.matmul(ps_m[:],
                                         s_sel[:, e * 128:(e + 1) * 128],
                                         s_rd[e][:], start=(e == 0),
                                         stop=(e == 7))
                               for e in range(8)][-1])
                evg[("rd", 0)] = (sPE.h, stk)
                evg[("rd", 1)] = (sPE.h, stk)
                rda = step(block.vector, sDV, [(sPE.h, stk)],
                           lambda v: v.tensor_copy(s_sq[:], ps_m[:]))
                evg["m"] = (sDV.h, rda)
                web = step(block.tensor, sPE,
                           [(sDV.h, ew_done), evg.get("m2", (None, 0))],
                           lambda te: te.matmul(ps_m2[:], s_rep[:], s_ewr[:],
                                                start=True, stop=True))
                rdb = step(block.vector, sDV, [(sPE.h, web), (sDV.h, rda)],
                           lambda v: v.tensor_mul(s_rdb[:], ps_m2[:], s_sq[:]))
                evg["m2"] = (sDV.h, rdb)

                at_guard = {}
                kg_first = [1]
                for kg in range(6):
                    lo = kg * 1024
                    nch_k = 8 if kg < 5 else 4

                    @block.gpsimd
                    def _(g, lo=lo, nch_k=nch_k):
                        g.wait_ge(sPE.h, wd_guard[0])
                        g.dma_start(
                            out=sb3(s_wd[:, 0:nch_k * D], D),
                            in_=fmload(wdT[lo:lo + nch_k * 128, :], D)
                            ).then_inc(sW.h, 16)
                        sW.n += 16

                    wdld = sW.n

                    @block.sync
                    def _(s, t=t, lo=lo, nch_k=nch_k):
                        s.wait_ge(sPE.h, wd_guard[0])
                        if kg_first[0]:
                            s.wait_ge(sST.h, guard["h3"])
                            kg_first[0] = 0
                        s.dma_start(
                            out=sb3(s_h3[:, 0:nch_k * 512], 512),
                            in_=fmload(hb_dram[lo:lo + nch_k * 128,
                                               t * 512:(t + 1) * 512], 512)
                            ).then_inc(sX.h, 16)
                        sX.n += 16

                    h3ld = sX.n
                    for c in range(NCH):
                        pd = ps_m if c % 2 == 0 else ps_m2
                        pdn = "m" if c % 2 == 0 else "m2"

                        def emit_down(te, c=c, pd=pd, kg=kg, nch_k=nch_k):
                            inst = None
                            for k in range(nch_k):
                                inst = te.matmul(
                                    pd[:],
                                    s_wd[:, k * D + c * 128:k * D + (c + 1) * 128],
                                    s_h3[:, k * 512:(k + 1) * 512],
                                    start=(k == 0),
                                    stop=(kg != 5 and k == nch_k - 1))
                            if kg == 5:
                                inst = te.matmul(pd[:],
                                                 s_bd[:, c * 128:(c + 1) * 128],
                                                 s_rdb[:], start=False, stop=True)
                            return inst

                        waits = [(sW.h, wdld), (sX.h, h3ld),
                                 evg.get(pdn, (None, 0))]
                        if kg == 5:
                            waits.append((sDV.h, rdb))
                        mm = step(block.tensor, sPE, waits, emit_down)
                        if kg == 0:
                            ac = step(block.vector, sDV, [(sPE.h, mm)],
                                      lambda v, c=c, pd=pd: v.tensor_copy(
                                          s_x2[:, c * 512:(c + 1) * 512], pd[:]))
                            evg[pdn] = (sDV.h, ac)
                        elif kg < 5:
                            ac = step(block.vector, sDV, [(sPE.h, mm)],
                                      lambda v, c=c, pd=pd: v.tensor_add(
                                          s_x2[:, c * 512:(c + 1) * 512],
                                          s_x2[:, c * 512:(c + 1) * 512], pd[:]))
                            evg[pdn] = (sDV.h, ac)
                        else:
                            @block.sync
                            def _(s, t=t, c=c):
                                sem, val = at_guard.get(0, (None, 0))
                                if val:
                                    s.wait_ge(sem, val)
                                s.dma_start(
                                    out=s_at[:],
                                    in_=at_in[c * 128:(c + 1) * 128,
                                              t * 512:(t + 1) * 512]
                                    ).then_inc(sX.h, 16)
                                s.dma_start(
                                    out=s_xf[:],
                                    in_=x_in[c * 128:(c + 1) * 128,
                                             t * 512:(t + 1) * 512]
                                    ).then_inc(sX.h, 16)
                                sX.n += 32

                            cld = sX.n
                            ac = stepseq(block.vector, sDV,
                                         [(sPE.h, mm), (sX.h, cld)], [
                                lambda v, c=c, pd=pd: v.tensor_add(
                                    s_x2[:, c * 512:(c + 1) * 512],
                                    s_x2[:, c * 512:(c + 1) * 512], pd[:]),
                                lambda v, c=c: v.tensor_add(
                                    s_x2[:, c * 512:(c + 1) * 512],
                                    s_x2[:, c * 512:(c + 1) * 512], s_at[:]),
                                lambda v, c=c: v.tensor_add(
                                    s_x2[:, c * 512:(c + 1) * 512],
                                    s_x2[:, c * 512:(c + 1) * 512], s_xf[:])])
                            at_guard[0] = (sDV.h, ac)
                            evg[pdn] = (sDV.h, ac)
                            sq = step(block.scalar, sAC,
                                      [(sDV.h, ac), evg.get("sq", (None, 0))],
                                      lambda sc, c=c: sc.activation(
                                          s_sq[:],
                                          s_x2[:, c * 512:(c + 1) * 512],
                                          AF.Square))
                            cs = step(block.tensor, sPE,
                                      [(sAC.h, sq),
                                       (sDV.h, guard["stats"]) if c == 0
                                       else (None, 0)],
                                      lambda te, c=c: (
                                          te.matmul(ps_s1[:], ones_sb[:],
                                                    s_x2[:, c * 512:(c + 1) * 512],
                                                    start=(c == 0),
                                                    stop=(c == NCH - 1)),
                                          te.matmul(ps_s2[:], ones_sb[:], s_sq[:],
                                                    start=(c == 0),
                                                    stop=(c == NCH - 1)))[-1])
                            evg["sq"] = (sPE.h, cs)
                    wd_guard[0] = sPE.n
                stats_mm = sPE.n
                x2_done = sDV.n

                @block.sync
                def _(s, t=t, n=x2_done):
                    s.wait_ge(sDV.h, n)
                    s.dma_start(out=fmload(o_x2[:, t * 512:(t + 1) * 512], 512),
                                in_=sb3(s_x2[:], 512)).then_inc(sST.h, 16)
                    sST.n += 16

                st1 = stepseq(block.vector, sDV, [(sPE.h, stats_mm)], [
                    lambda v: v.tensor_scalar(s_row[0:1, 0:512], ps_s1[:],
                                              1.0 / D, None, ALU.mult),
                    lambda v: v.tensor_scalar(s_row[0:1, 512:1024], ps_s2[:],
                                              1.0 / D, None, ALU.mult),
                    lambda v: v.tensor_mul(s_sq[0:1, 0:512], s_row[0:1, 0:512],
                                           s_row[0:1, 0:512]),
                    lambda v: v.scalar_tensor_tensor(
                        s_row[0:1, 512:1024], s_row[0:1, 512:1024], EPS_LN,
                        s_sq[0:1, 0:512], ALU.add, ALU.subtract)])
                st2 = step(block.scalar, sAC, [(sDV.h, st1)],
                           lambda sc: sc.activation(s_row[0:1, 512:1024],
                                                    s_row[0:1, 512:1024],
                                                    AF.Sqrt))
                st3 = stepseq(block.vector, sDV, [(sAC.h, st2)], [
                    lambda v: v.reciprocal(s_row[0:1, 512:1024],
                                           s_row[0:1, 512:1024]),
                    lambda v: v.scalar_tensor_tensor(
                        s_row[0:1, 0:512], s_row[0:1, 0:512], -1.0,
                        s_row[0:1, 512:1024], ALU.mult, ALU.mult)])
                guard["stats"] = st3
                bca = step(block.tensor, sPE,
                           [(sDV.h, st3), evg.get("m", (None, 0))],
                           lambda te: te.matmul(ps_m[:], ones_rsb[:],
                                                s_row[0:1, 512:1024],
                                                start=True, stop=True))
                bcb = step(block.tensor, sPE, [evg.get("m2", (None, 0))],
                           lambda te: te.matmul(ps_m2[:], ones_rsb[:],
                                                s_row[0:1, 0:512],
                                                start=True, stop=True))
                xnl = st3
                for c in range(NCH):
                    xnl = stepseq(block.vector, sDV, [(sPE.h, bcb)], [
                        lambda v, c=c: v.tensor_mul(
                            s_sq[:], s_x2[:, c * 512:(c + 1) * 512], ps_m[:]),
                        lambda v, c=c: v.tensor_add(
                            s_xn[:, c * 512:(c + 1) * 512], s_sq[:], ps_m2[:])])
                evg["m"] = (sDV.h, xnl)
                evg["m2"] = (sDV.h, xnl)

                @block.sync
                def _(s, t=t, n=xnl):
                    s.wait_ge(sDV.h, n)
                    s.dma_start(out=fmload(o_xn1[:, t * 512:(t + 1) * 512], 512),
                                in_=sb3(s_xn[:], 512)).then_inc(sST.h, 16)
                    sST.n += 16

                guard["xn_st"] = sST.n
                guard["xn_pe"] = sPE.n

            @block.sync
            def _(s):
                s.wait_ge(sST.h, sST.n)

    return nc


def build_l5(fz=None):
    pre = "" if fz is None else "z5_"
    if fz is None:
        nc = bass.Bass("TRN2", target_bir_lowering=False,
                       detect_race_conditions=False)
        xn_in = nc.dram_tensor("xn1_bf", [D, TOK], BF16,
                               kind="ExternalInput").ap()
        o_at = nc.dram_tensor("at1", [D, TOK], F32, kind="ExternalOutput").ap()
    else:
        nc = fz.nc
        xn_in = fz.dram["xn1"]
        o_at = fz.dram["at1"]
    wqkvT = nc.dram_tensor("wqkvT", [D, 3 * D], F32, kind="ExternalInput").ap()
    wo1T = nc.dram_tensor("wo1T", [D, D], F32, kind="ExternalInput").ap()
    csel5 = nc.dram_tensor("csel5", [128, 16 * 128], F32, kind="ExternalInput").ap()
    cbias5 = nc.dram_tensor("cbias5", [16, 4], F32, kind="ExternalInput").ap()
    ao_dram = nc.dram_tensor("ao_scr", [D, TOK], BF16, kind="Internal").ap()

    ident_c = nc.inline_tensor(np.eye(128, dtype=np.float32),
                               pre + "l5_id").ap()
    ones_row = nc.inline_tensor(np.ones((1, 128), np.float32),
                                pre + "l5_or").ap()

    with ExitStack() as ctx:
        def sb(name, shape, dtype=F32, c=None):
            return (c or ctx).enter_context(nc.sbuf_tensor(pre + name, shape, dtype))

        def ps(name, shape, dtype=F32):
            return ctx.enter_context(nc.psum_tensor(pre + name, shape, dtype))

        s_w = sb("s_w", [128, NCH * 1536], BF16)
        s_xn = sb("s_xn", [128, NCH * 512], BF16)
        s_hin = sb("s_hin", [128, NCH * 512], BF16)
        s_n1 = sb("s_n1", [128, NCH])
        s_q = sb("s_q", [128, 16 * 512], BF16)
        s_k = sb("s_k", [128, 16 * 512], BF16)
        s_v = sb("s_v", [128, 16 * 512], BF16)
        s_p = sb("s_p", [128, 512], BF16)
        s_sm = sb("s_sm", [16, 4 * 512])
        s_mx = sb("s_mx", [16, 2 * 512])
        s_sel = sb("s_sel", [128, 16 * 128], BF16)
        s_bia = sb("s_bia", [16, 4])
        s_id = sb("s_id", [128, 128])
        s_or = sb("s_or", [1, 128])
        s_wr = sb("s_wr", [1, 512])
        s_acm = sb("s_acm", [128, 512])
        s_tmp = sb("s_tmp", [128, 512])
        s_ao = sb("s_ao", [128, 512], BF16)
        s_aot = sb("s_aot", [128, NCH * 512], BF16)
        s_o = sb("s_o", [128, NCH * 512])

        ps_q = [ps("ps_q0", [128, 512]), ps("ps_q1", [128, 512])]
        ps_s = [ps(f"ps_s{m}", [128, 512]) for m in range(4)]
        ps_w = ps("ps_w", [1, 512])
        ps_b = ps("ps_b", [128, 512])

        if fz is None:
            sW = Ctr(ctx.enter_context(nc.semaphore("sW")))
            sX = Ctr(ctx.enter_context(nc.semaphore("sX")))
            sPE = Ctr(ctx.enter_context(nc.semaphore("sPE")))
            sDV = Ctr(ctx.enter_context(nc.semaphore("sDV")))
            sAC = Ctr(ctx.enter_context(nc.semaphore("sAC")))
            sST = Ctr(ctx.enter_context(nc.semaphore("sST")))
        else:
            sW, sX, sPE, sDV, sAC, sST = (fz.sW, fz.sX, fz.sPE, fz.sDV,
                                          fz.sAC, fz.sST)

        with (nc.Block() if fz is None else nullcontext(fz.block)) as block:
            step, stepseq = mk_steps(block)

            @block.gpsimd
            def _(g):
                g.dma_start(out=s_sel[:], in_=csel5).then_inc(sW.h, 16)
                sW.n += 16

            @block.sync
            def _(s):
                s.dma_start(out=s_bia[:], in_=cbias5).then_inc(sX.h, 16)
                s.dma_start(out=s_id[:], in_=ident_c).then_inc(sX.h, 16)
                s.dma_start(out=s_or[:], in_=ones_row).then_inc(sX.h, 16)
                sX.n += 48

            const_ld = sX.n
            sel_ld = sW.n
            evg = {}
            guard = {"w": 0, "qkv": 0, "xn": 0}

            for hg in range(4):
                @block.gpsimd
                def _(g, hg=hg):
                    g.wait_ge(sPE.h, guard["w"])
                    for j, base in enumerate((0, D, 2 * D)):
                        g.dma_start(
                            out=sb3(s_w[:, j * NCH * 512:(j + 1) * NCH * 512], 512),
                            in_=fmload(wqkvT[:, base + hg * 512:
                                             base + (hg + 1) * 512], 512)
                            ).then_inc(sW.h, 16)
                    sW.n += 48

                wld = sW.n
                for l in range(4):
                    @block.sync
                    def _(s, l=l, hg=hg):
                        if guard["xn"]:
                            s.wait_ge(sPE.h, guard["xn"])
                        s.dma_start(out=sb3(s_xn[:], 512),
                                    in_=fmload(xn_in[:, l * 512:(l + 1) * 512],
                                               512)).then_inc(sX.h, 16)
                        sX.n += 16

                    xld = sX.n
                    for j, dst in enumerate((s_q, s_k, s_v)):
                        for hl in range(4):
                            pq = ps_q[hl % 2]
                            mm = step(block.tensor, sPE,
                                      [(sW.h, wld), (sX.h, xld),
                                       evg.get(("q", hl % 2), (None, 0))],
                                      lambda te, j=j, hl=hl, pq=pq: [
                                          te.matmul(
                                              pq[:],
                                              s_w[:, (j * NCH + k) * 512 +
                                                  hl * 128:(j * NCH + k) * 512 +
                                                  (hl + 1) * 128],
                                              s_xn[:, k * 512:(k + 1) * 512],
                                              start=(k == 0), stop=(k == NCH - 1))
                                          for k in range(NCH)][-1])
                            ev = step(block.vector, sDV, [(sPE.h, mm)],
                                      lambda v, dst=dst, hl=hl, l=l, pq=pq:
                                      v.tensor_copy(
                                          dst[:, (hl * 4 + l) * 512:
                                              (hl * 4 + l + 1) * 512], pq[:]))
                            evg[("q", hl % 2)] = (sDV.h, ev)
                    guard["xn"] = sPE.n
                qkv_done = sDV.n

                for m in range(4):
                    first = [True]
                    for l in range(m, 4):
                        for hl in range(4):
                            r = hl * 4 + l
                            pt = step(block.vector, sDV,
                                      [(sDV.h, qkv_done),
                                       evg.get("p", (None, 0))],
                                      lambda v, hl=hl, l=l, m=m: v.tensor_mul(
                                          s_p[:],
                                          s_q[:, (hl * 4 + l) * 512:
                                              (hl * 4 + l + 1) * 512],
                                          s_k[:, (hl * 4 + m) * 512:
                                              (hl * 4 + m + 1) * 512]))
                            sm = step(block.tensor, sPE,
                                      [(sDV.h, pt), (sW.h, sel_ld),
                                       evg.get(("s", m), (None, 0))
                                       if first[0] else (None, 0)],
                                      lambda te, m=m, r=r, first=first[0]:
                                      te.matmul(ps_s[m][:],
                                                s_sel[:, r * 128:(r + 1) * 128],
                                                s_p[:], start=first,
                                                stop=(r == 15)))
                            first[0] = False
                            evg["p"] = (sPE.h, sm)
                sc_done = sPE.n

                sm1 = stepseq(block.vector, sDV, [(sPE.h, sc_done)], [
                    lambda v: v.tensor_copy(
                        sb3(s_sm[:], 512),
                        ps_stack[:] if False else _smview(ps_s)),
                ]) if False else None
                ev0 = 0
                for m in range(4):
                    ev0 = step(block.vector, sDV,
                               [(sPE.h, sc_done), (sX.h, const_ld)],
                               lambda v, m=m: v.tensor_scalar(
                                   s_sm[:, m * 512:(m + 1) * 512],
                                   ps_s[m][0:16, :], s_bia[:, m:m + 1],
                                   None, ALU.add))
                    evg[("s", m)] = (sDV.h, ev0)
                smax = stepseq(block.vector, sDV, [(sDV.h, ev0)], [
                    lambda v: v.tensor_max(s_mx[:, 0:512], s_sm[:, 0:512],
                                           s_sm[:, 512:1024]),
                    lambda v: v.tensor_max(s_mx[:, 512:1024],
                                           s_sm[:, 1024:1536],
                                           s_sm[:, 1536:2048]),
                    lambda v: v.tensor_max(s_mx[:, 0:512], s_mx[:, 0:512],
                                           s_mx[:, 512:1024])])
                sub = smax
                for m in range(4):
                    sub = step(block.vector, sDV, [(sDV.h, sub)],
                               lambda v, m=m: v.tensor_sub(
                                   s_sm[:, m * 512:(m + 1) * 512],
                                   s_sm[:, m * 512:(m + 1) * 512],
                                   s_mx[:, 0:512]))
                ex = step(block.scalar, sAC, [(sDV.h, sub)],
                          lambda sc: sc.activation(s_sm[:], s_sm[:], AF.Exp))
                sden = stepseq(block.vector, sDV, [(sAC.h, ex)], [
                    lambda v: v.tensor_add(s_mx[:, 0:512], s_sm[:, 0:512],
                                           s_sm[:, 512:1024]),
                    lambda v: v.tensor_add(s_mx[:, 512:1024],
                                           s_sm[:, 1024:1536],
                                           s_sm[:, 1536:2048]),
                    lambda v: v.tensor_add(s_mx[:, 0:512], s_mx[:, 0:512],
                                           s_mx[:, 512:1024]),
                    lambda v: v.reciprocal(s_mx[:, 512:1024], s_mx[:, 0:512])])
                wnorm = sden
                for m in range(4):
                    wnorm = step(block.vector, sDV, [(sDV.h, wnorm)],
                                 lambda v, m=m: v.tensor_mul(
                                     s_sm[:, m * 512:(m + 1) * 512],
                                     s_sm[:, m * 512:(m + 1) * 512],
                                     s_mx[:, 512:1024]))

                for l in range(4):
                    for hl in range(4):
                        nterm = 0
                        for m in range(l + 1):
                            r = hl * 4 + l
                            wx = step(block.tensor, sPE,
                                      [(sDV.h, wnorm), evg.get("w", (None, 0))],
                                      lambda te, r=r, m=m: te.matmul(
                                          ps_w[:], s_id[0:16, r:r + 1],
                                          s_sm[0:16, m * 512:(m + 1) * 512],
                                          start=True, stop=True))
                            we = step(block.vector, sDV, [(sPE.h, wx)],
                                      lambda v: v.tensor_copy(s_wr[:], ps_w[:]))
                            evg["w"] = (sDV.h, we)
                            bc = step(block.tensor, sPE,
                                      [(sDV.h, we), evg.get("b", (None, 0))],
                                      lambda te: te.matmul(ps_b[:], s_or[:],
                                                           s_wr[:], start=True,
                                                           stop=True))
                            if m == 0:
                                mac = step(block.vector, sDV, [(sPE.h, bc)],
                                           lambda v, hl=hl, m=m: v.tensor_mul(
                                               s_acm[:], ps_b[:],
                                               s_v[:, (hl * 4 + m) * 512:
                                                   (hl * 4 + m + 1) * 512]))
                            else:
                                mac = stepseq(block.vector, sDV, [(sPE.h, bc)], [
                                    lambda v, hl=hl, m=m: v.tensor_mul(
                                        s_tmp[:], ps_b[:],
                                        s_v[:, (hl * 4 + m) * 512:
                                            (hl * 4 + m + 1) * 512]),
                                    lambda v: v.tensor_add(s_acm[:], s_acm[:],
                                                           s_tmp[:])])
                            evg["b"] = (sDV.h, mac)
                        fin = step(block.vector, sDV, [(sDV.h, mac)],
                                   lambda v: v.tensor_copy(s_ao[:], s_acm[:]))

                        @block.sync
                        def _(s, hg=hg, hl=hl, l=l, n=fin):
                            s.wait_ge(sDV.h, n)
                            s.dma_start(
                                out=ao_dram[(hg * 4 + hl) * 128:
                                            (hg * 4 + hl + 1) * 128,
                                            l * 512:(l + 1) * 512],
                                in_=s_ao[:]).then_inc(sST.h, 16)
                            sST.n += 16

                        evg["ao"] = (sST.h, sST.n)
                guard["w"] = sPE.n

            ao_done = sST.n

            @block.gpsimd
            def _(g):
                g.wait_ge(sPE.h, guard["w"])
                g.dma_start(out=sb3(s_w[:, 0:NCH * 512], 512),
                            in_=fmload(wo1T[:, 0:512], 512)).then_inc(sW.h, 16)
                g.dma_start(out=sb3(s_w[:, NCH * 512:2 * NCH * 512], 512),
                            in_=fmload(wo1T[:, 512:1024], 512)).then_inc(sW.h, 16)
                g.dma_start(out=sb3(s_w[:, 2 * NCH * 512:3 * NCH * 512], 512),
                            in_=fmload(wo1T[:, 1024:1536], 512)).then_inc(sW.h, 16)
                sW.n += 48

            _dv_now = sDV.n

            @block.gpsimd
            def _(g):
                g.wait_ge(sDV.h, _dv_now)
                g.dma_start(out=sb3(s_q[:], 512),
                            in_=fmload(wo1T[:, 1536:2048], 512)).then_inc(sW.h, 16)
                sW.n += 16

            wld = sW.n
            for l in range(4):
                @block.sync
                def _(s, l=l):
                    s.wait_ge(sST.h, ao_done)
                    if guard.get("aot", 0):
                        s.wait_ge(sPE.h, guard["aot"])
                    s.dma_start(out=sb3(s_aot[:], 512),
                                in_=fmload(ao_dram[:, l * 512:(l + 1) * 512],
                                           512)).then_inc(sX.h, 16)
                    sX.n += 16

                ald = sX.n
                for c in range(NCH):
                    pq = ps_q[c % 2]
                    wsrc = (lambda k, c: s_w[:, (c // 4) * NCH * 512 + k * 512 +
                                             (c % 4) * 128:
                                             (c // 4) * NCH * 512 + k * 512 +
                                             (c % 4) * 128 + 128])                         if c < 12 else                         (lambda k, c: s_q[:, k * 512 + (c % 4) * 128:
                                          k * 512 + (c % 4) * 128 + 128])
                    mm = step(block.tensor, sPE,
                              [(sW.h, wld), (sX.h, ald),
                               evg.get(("q", c % 2), (None, 0))],
                              lambda te, c=c, pq=pq, wsrc=wsrc: [
                                  te.matmul(pq[:], wsrc(k, c),
                                            s_aot[:, k * 512:(k + 1) * 512],
                                            start=(k == 0), stop=(k == NCH - 1))
                                  for k in range(NCH)][-1])
                    ev = step(block.vector, sDV, [(sPE.h, mm)],
                              lambda v, c=c, pq=pq: v.tensor_copy(
                                  s_o[:, c * 512:(c + 1) * 512], pq[:]))
                    evg[("q", c % 2)] = (sDV.h, ev)
                guard["aot"] = sPE.n

                @block.sync
                def _(s, l=l, n=sDV.n):
                    s.wait_ge(sDV.h, n)
                    s.dma_start(out=fmload(o_at[:, l * 512:(l + 1) * 512], 512),
                                in_=sb3(s_o[:], 512)).then_inc(sST.h, 16)
                    sST.n += 16

            @block.sync
            def _(s):
                s.wait_ge(sST.h, sST.n)

    return nc


def build_l6(fz=None):
    pre = "" if fz is None else "z6_"
    if fz is None:
        nc = bass.Bass("TRN2", target_bir_lowering=False,
                       detect_race_conditions=False)
        xn_in = nc.dram_tensor("xn1_bf", [D, TOK], BF16,
                               kind="ExternalInput").ap()
        x2_in = nc.dram_tensor("x2", [D, TOK], F32, kind="ExternalInput").ap()
        at_in = nc.dram_tensor("at1", [D, TOK], F32, kind="ExternalInput").ap()
        o_out = nc.dram_tensor("out_fm", [D, TOK], F32,
                               kind="ExternalOutput").ap()
    else:
        nc = fz.nc
        xn_in = fz.dram["xn1"]
        x2_in = fz.dram["x2"]
        at_in = fz.dram["at1"]
        o_out = fz.dram["out_fm"]
    w1g = nc.dram_tensor("w1g", [D, FF1], F32, kind="ExternalInput").ap()
    w1u = nc.dram_tensor("w1u", [D, FF1], F32, kind="ExternalInput").ap()
    w1d = nc.dram_tensor("w1d", [FF1, D], F32, kind="ExternalInput").ap()
    h_dram = nc.dram_tensor("h6_scr", [FF1, TOK], BF16, kind="Internal").ap()

    with ExitStack() as ctx:
        def sb(name, shape, dtype=F32, c=None):
            return (c or ctx).enter_context(nc.sbuf_tensor(pre + name, shape, dtype))

        def ps(name, shape, dtype=F32):
            return ctx.enter_context(nc.psum_tensor(pre + name, shape, dtype))

        s_g = sb("s_g", [128, 512], BF16)
        s_hb = sb("s_hb", [128, 512], BF16)
        cA = ExitStack()
        s_w0 = sb("s_w0", [128, NCH * 512], BF16, cA)
        s_w1 = sb("s_w1", [128, NCH * 512], BF16, cA)
        s_xn = sb("s_xn", [128, NCH * 512], BF16, cA)

        ps_a = ps("ps_a", [128, 512])
        ps_b = ps("ps_b", [128, 512])

        if fz is None:
            sW = Ctr(ctx.enter_context(nc.semaphore("sW")))
            sX = Ctr(ctx.enter_context(nc.semaphore("sX")))
            sPE = Ctr(ctx.enter_context(nc.semaphore("sPE")))
            sDV = Ctr(ctx.enter_context(nc.semaphore("sDV")))
            sAC = Ctr(ctx.enter_context(nc.semaphore("sAC")))
            sPO = Ctr(ctx.enter_context(nc.semaphore("sPO")))
            sST = Ctr(ctx.enter_context(nc.semaphore("sST")))
        else:
            sW, sX, sPE, sDV, sAC, sPO, sST = (fz.sW, fz.sX, fz.sPE, fz.sDV,
                                               fz.sAC, fz.sPO, fz.sST)

        with (nc.Block() if fz is None else nullcontext(fz.block)) as block:
            step, stepseq = mk_steps(block)

            evg = {}
            guard = {"w": 0, "xn": 0, "h": 0}

            for mg in range(16):
                @block.gpsimd
                def _(g, mg=mg):
                    g.wait_ge(sPE.h, guard["w"])
                    g.dma_start(out=sb3(s_w0[:], 512),
                                in_=fmload(w1g[:, mg * 512:(mg + 1) * 512], 512)
                                ).then_inc(sW.h, 16)
                    g.dma_start(out=sb3(s_w1[:], 512),
                                in_=fmload(w1u[:, mg * 512:(mg + 1) * 512], 512)
                                ).then_inc(sW.h, 16)
                    sW.n += 32

                wld = sW.n
                for t in range(NT):
                    @block.sync
                    def _(s, t=t, mg=mg):
                        if guard["xn"]:
                            s.wait_ge(sPE.h, guard["xn"])
                        s.dma_start(out=sb3(s_xn[:], 512),
                                    in_=fmload(xn_in[:, t * 512:(t + 1) * 512],
                                               512)).then_inc(sX.h, 16)
                        sX.n += 16

                    xld = sX.n
                    for mc in range(4):
                        mmg = step(block.tensor, sPE,
                                   [(sW.h, wld), (sX.h, xld),
                                    evg.get("a", (None, 0))],
                                   lambda te, mc=mc: [
                                       te.matmul(ps_a[:],
                                                 s_w0[:, k * 512 + mc * 128:
                                                      k * 512 + (mc + 1) * 128],
                                                 s_xn[:, k * 512:(k + 1) * 512],
                                                 start=(k == 0),
                                                 stop=(k == NCH - 1))
                                       for k in range(NCH)][-1])
                        gs = step(block.scalar, sAC,
                                  [(sPE.h, mmg), evg.get("g", (None, 0))],
                                  lambda sc: sc.activation(s_g[:], ps_a[:],
                                                           AF.Silu))
                        evg["a"] = (sAC.h, gs)
                        mmu = step(block.tensor, sPE,
                                   [evg.get("b", (None, 0))],
                                   lambda te, mc=mc: [
                                       te.matmul(ps_b[:],
                                                 s_w1[:, k * 512 + mc * 128:
                                                      k * 512 + (mc + 1) * 128],
                                                 s_xn[:, k * 512:(k + 1) * 512],
                                                 start=(k == 0),
                                                 stop=(k == NCH - 1))
                                       for k in range(NCH)][-1])
                        hc = step(block.vector, sDV,
                                  [(sPE.h, mmu), (sAC.h, gs),
                                   (sST.h, guard["h"])],
                                  lambda v: v.tensor_mul(s_hb[:], ps_b[:],
                                                         s_g[:]))
                        evg["b"] = (sDV.h, hc)
                        evg["g"] = (sDV.h, hc)

                        @block.sync
                        def _(s, t=t, mg=mg, mc=mc, n=hc):
                            s.wait_ge(sDV.h, n)
                            s.dma_start(
                                out=h_dram[mg * 512 + mc * 128:
                                           mg * 512 + (mc + 1) * 128,
                                           t * 512:(t + 1) * 512],
                                in_=s_hb[:]).then_inc(sST.h, 16)
                            sST.n += 16

                        guard["h"] = sST.n
                    guard["xn"] = sPE.n
                guard["w"] = sPE.n

            cA.close()
            s_wd = sb("s_wd", [128, 8 * D], BF16)
            s_h = sb("s_h", [128, 64 * 512], BF16)
            s_sq = sb("s_sq", [128, 4 * 512])
            s_t2 = sb("s_t2", [128, 3 * 512])
            s_x2 = sb("s_x2", [128, 512])
            s_at = sb("s_at", [128, 512])
            s_o = sb("s_o", [128, NCH * 512])
            hA_done = sST.n
            at_guard = {}
            wd_guard = [0]
            hB_guard = [0]
            for t in range(NT):
                @block.sync
                def _(s, t=t):
                    s.wait_ge(sST.h, hA_done)
                    if hB_guard[0]:
                        s.wait_ge(sPE.h, hB_guard[0])
                    if guard.get("hdv", 0):
                        s.wait_ge(sPO.h, guard["hdv"])
                    s.dma_start(out=sb3(s_h[:], 512),
                                in_=fmload(h_dram[:, t * 512:(t + 1) * 512],
                                           512)).then_inc(sX.h, 16)
                    sX.n += 16

                hld = sX.n
                last_po = 0
                for c in range(16):
                    mem = [c, c + 16, c + 32, c + 48]
                    sq4 = step(block.scalar, sAC,
                               [(sX.h, hld), evg.get("sq", (None, 0))],
                               lambda sc, mem=mem: [
                                   sc.activation(s_sq[:, i * 512:(i + 1) * 512],
                                                 s_h[:, mem[i] * 512:
                                                     (mem[i] + 1) * 512],
                                                 AF.Square)
                                   for i in range(4)][-1])
                    thr = stepseq(block.vector, sDV, [(sAC.h, sq4)], [
                        lambda v: v.tensor_max(s_t2[:, 0:512], s_sq[:, 0:512],
                                               s_sq[:, 512:1024]),
                        lambda v: v.tensor_tensor(s_t2[:, 512:1024],
                                                  s_sq[:, 0:512],
                                                  s_sq[:, 512:1024], ALU.min),
                        lambda v: v.tensor_max(s_t2[:, 1024:1536],
                                               s_sq[:, 1024:1536],
                                               s_sq[:, 1536:2048])])
                    thr2 = stepseq(block.vector, sDV, [(sDV.h, thr)], [
                        lambda v: v.tensor_tensor(s_sq[:, 0:512],
                                                  s_sq[:, 1024:1536],
                                                  s_sq[:, 1536:2048], ALU.min),
                        lambda v: v.tensor_max(s_t2[:, 512:1024],
                                               s_t2[:, 512:1024],
                                               s_sq[:, 0:512]),
                        lambda v: v.tensor_tensor(s_t2[:, 0:512],
                                                  s_t2[:, 0:512],
                                                  s_t2[:, 1024:1536], ALU.min),
                        lambda v: v.tensor_max(s_t2[:, 0:512], s_t2[:, 0:512],
                                               s_t2[:, 512:1024])])
                    rq0 = step(block.scalar, sAC, [(sDV.h, thr2)],
                               lambda sc, mem=mem: sc.activation(
                                   s_sq[:, 0:512],
                                   s_h[:, mem[0] * 512:(mem[0] + 1) * 512],
                                   AF.Square))
                    msk = stepseq(block.vector, sDV, [(sAC.h, rq0)], [
                        lambda v, i=i: v.tensor_tensor(
                            s_sq[:, i * 512:(i + 1) * 512],
                            s_sq[:, i * 512:(i + 1) * 512],
                            s_t2[:, 0:512], ALU.is_ge)
                        for i in range(4)])
                    po = step(block.gpsimd, sPO, [(sDV.h, msk)],
                              lambda g, mem=mem: [
                                  g.tensor_mul(
                                      s_h[:, mem[i] * 512:(mem[i] + 1) * 512],
                                      s_h[:, mem[i] * 512:(mem[i] + 1) * 512],
                                      s_sq[:, i * 512:(i + 1) * 512])
                                  for i in range(4)][-1])
                    evg["sq"] = (sPO.h, po)
                    last_po = po
                guard["hdv"] = last_po

                for kg in range(8):
                    @block.gpsimd
                    def _(g, kg=kg):
                        g.wait_ge(sPE.h, wd_guard[0])
                        g.dma_start(
                            out=sb3(s_wd[:], D),
                            in_=fmload(w1d[kg * 1024:(kg + 1) * 1024, :], D)
                            ).then_inc(sW.h, 16)
                        sW.n += 16

                    wld = sW.n
                    for c in range(NCH):
                        pd = ps_a if c % 2 == 0 else ps_b
                        pdn = "a" if c % 2 == 0 else "b"
                        mm = step(block.tensor, sPE,
                                  [(sW.h, wld), (sPO.h, last_po),
                                   evg.get(pdn, (None, 0))],
                                  lambda te, c=c, pd=pd, kg=kg: [
                                      te.matmul(
                                          pd[:],
                                          s_wd[:, k * D + c * 128:
                                               k * D + (c + 1) * 128],
                                          s_h[:, (kg * 8 + k) * 512:
                                              (kg * 8 + k + 1) * 512],
                                          start=(k == 0), stop=(k == 7))
                                      for k in range(8)][-1])
                        if kg == 0:
                            ac = step(block.vector, sDV, [(sPE.h, mm)],
                                      lambda v, c=c, pd=pd: v.tensor_copy(
                                          s_o[:, c * 512:(c + 1) * 512], pd[:]))
                        elif kg < 7:
                            ac = step(block.vector, sDV, [(sPE.h, mm)],
                                      lambda v, c=c, pd=pd: v.tensor_add(
                                          s_o[:, c * 512:(c + 1) * 512],
                                          s_o[:, c * 512:(c + 1) * 512], pd[:]))
                        else:
                            @block.sync
                            def _(s, t=t, c=c):
                                sem, val = at_guard.get(0, (None, 0))
                                if val:
                                    s.wait_ge(sem, val)
                                s.dma_start(
                                    out=s_x2[:],
                                    in_=x2_in[c * 128:(c + 1) * 128,
                                              t * 512:(t + 1) * 512]
                                    ).then_inc(sX.h, 16)
                                s.dma_start(
                                    out=s_at[:],
                                    in_=at_in[c * 128:(c + 1) * 128,
                                              t * 512:(t + 1) * 512]
                                    ).then_inc(sX.h, 16)
                                sX.n += 32

                            cld = sX.n
                            ac = stepseq(block.vector, sDV,
                                         [(sPE.h, mm), (sX.h, cld)], [
                                lambda v, c=c, pd=pd: v.tensor_add(
                                    s_o[:, c * 512:(c + 1) * 512],
                                    s_o[:, c * 512:(c + 1) * 512], pd[:]),
                                lambda v, c=c: v.tensor_add(
                                    s_o[:, c * 512:(c + 1) * 512],
                                    s_o[:, c * 512:(c + 1) * 512], s_x2[:]),
                                lambda v, c=c: v.tensor_add(
                                    s_o[:, c * 512:(c + 1) * 512],
                                    s_o[:, c * 512:(c + 1) * 512], s_at[:])])
                            at_guard[0] = (sDV.h, ac)
                        evg[pdn] = (sDV.h, ac)
                    wd_guard[0] = sPE.n
                hB_guard[0] = sPE.n

                @block.sync
                def _(s, t=t, n=sDV.n):
                    s.wait_ge(sDV.h, n)
                    s.dma_start(out=fmload(o_out[:, t * 512:(t + 1) * 512], 512),
                                in_=sb3(s_o[:], 512)).then_inc(sST.h, 16)
                    sST.n += 16

            @block.sync
            def _(s):
                s.wait_ge(sST.h, sST.n)

    return nc


class _FZ:
    pass


def build_fused():
    nc = bass.Bass("TRN2", target_bir_lowering=False,
                   detect_race_conditions=False)
    fz = _FZ()
    fz.nc = nc
    dram = {}
    dram["x_fm"] = nc.dram_tensor("x_fm", [D, TOK], F32,
                                  kind="ExternalInput").ap()
    dram["xn_bf"] = nc.dram_tensor("xn_bf", [D, TOK], BF16, kind="Internal").ap()
    dram["hin_bf"] = nc.dram_tensor("hin_bf", [D, TOK], BF16,
                                    kind="Internal").ap()
    dram["attn_out"] = nc.dram_tensor("attn_out", [D, TOK], F32,
                                      kind="Internal").ap()
    dram["x2"] = nc.dram_tensor("x2", [D, TOK], F32, kind="Internal").ap()
    dram["xn1"] = nc.dram_tensor("xn1", [D, TOK], BF16, kind="Internal").ap()
    dram["at1"] = nc.dram_tensor("at1", [D, TOK], F32, kind="Internal").ap()
    dram["out_fm"] = nc.dram_tensor("out_fm", [D, TOK], F32,
                                    kind="ExternalOutput").ap()
    fz.dram = dram
    a2a1_in_t = nc.dram_tensor("a2a1_in", [8 * 1536, 1024], BF16,
                               kind="Internal")
    a2a1_out_t = nc.dram_tensor("a2a1_out", [8 * 1536, 1024], BF16,
                                kind="Internal")
    g1a_in_t = nc.dram_tensor("g1a_in", [8 * 512, 8], F32, kind="Internal")
    g1a_out_t = nc.dram_tensor("g1a_out", [8 * 512, 8], F32, kind="Internal")
    a2a2_in_t = nc.dram_tensor("a2a2_in", [8 * 1024, 512], BF16,
                               kind="Internal")
    a2a2_out_t = nc.dram_tensor("a2a2_out", [8 * 1024, 512], BF16,
                                kind="Internal")
    fz.a2a1_in = a2a1_in_t.ap()
    fz.a2a1_out = a2a1_out_t.ap()
    fz.g1a_in = g1a_in_t.ap()
    fz.g1a_out = g1a_out_t.ap()
    fz.a2a2_in = a2a2_in_t.ap()
    fz.a2a2_out = a2a2_out_t.ap()

    with ExitStack() as ctx:
        ctx.enter_context(allow_ici_all_to_all())
        for nm in ("sW", "sX", "sPE", "sDV", "sAC", "sST", "sPO", "sCC"):
            setattr(fz, nm, Ctr(ctx.enter_context(nc.semaphore(nm))))
        with nc.Block() as block:
            fz.block = block
            ctrs = [fz.sW, fz.sX, fz.sPE, fz.sDV, fz.sAC, fz.sST, fz.sPO,
                    fz.sCC]

            def barrier():
                for eng in (block.sync, block.tensor, block.vector,
                            block.scalar, block.gpsimd):
                    @eng
                    def _(e):
                        for ct in ctrs:
                            if ct.n:
                                e.wait_ge(ct.h, ct.n)

            build_l1(fz)
            barrier()

            @block.gpsimd
            def _(gq):
                gq.collective_compute(
                    "AllToAll", ALU.bypass,
                    replica_groups=[list(range(NCORE))],
                    ins=[a2a1_in_t.ap().opt()],
                    outs=[a2a1_out_t.ap().opt()]).then_inc(fz.sCC.h, 1)
                gq.collective_compute(
                    "AllToAll", ALU.bypass,
                    replica_groups=[list(range(NCORE))],
                    ins=[g1a_in_t.ap().opt()],
                    outs=[g1a_out_t.ap().opt()]).then_inc(fz.sCC.h, 1)

            fz.sCC.n += 2
            barrier()
            build_l2(fz)
            barrier()

            @block.gpsimd
            def _(gq):
                gq.collective_compute(
                    "AllToAll", ALU.bypass,
                    replica_groups=[list(range(NCORE))],
                    ins=[a2a2_in_t.ap().opt()],
                    outs=[a2a2_out_t.ap().opt()]).then_inc(fz.sCC.h, 1)

            fz.sCC.n += 1
            barrier()
            build_l3(fz)
            barrier()
            build_l4(fz)
            barrier()
            build_l5(fz)
            barrier()
            build_l6(fz)
    return nc


def _shard_seq(x):
    out = []
    for c in range(NCORE):
        xc = x[:, c * SC:(c + 1) * SC, :]
        out.append(np.ascontiguousarray(xc.transpose(2, 0, 1).reshape(D, TOK)))
    return out


def _unshard_seq(parts):
    out = np.empty((B, S, D), dtype=parts[0].dtype)
    for c in range(NCORE):
        out[:, c * SC:(c + 1) * SC, :] = (
            np.asarray(parts[c]).reshape(D, B, SC).transpose(1, 2, 0))
    return out


def _run(nc, in_maps):
    res = run_bass_kernel_spmd(nc, in_maps, core_ids=list(range(NCORE)))
    return res.results


def kernel_unfused(**inputs):
    f32 = np.float32
    x = np.asarray(inputs["L0_attn_x"], dtype=f32)
    g = lambda k: np.ascontiguousarray(np.asarray(inputs[k], dtype=f32))

    xs = _shard_seq(x)
    wg16 = np.ascontiguousarray(
        np.concatenate([g("L0_attn_w_ig"), g("L0_attn_w_fg")], axis=1))
    n1c = np.ascontiguousarray(g("L0_attn_norm1_w").reshape(NCH, 128).T)
    r1 = _run(build_l1(), [
        {"x_fm": xs[c], "wq": g("L0_attn_wq"), "wk": g("L0_attn_wk"),
         "wv": g("L0_attn_wv"), "wg16": wg16, "n1w": n1c}
        for c in range(NCORE)])

    QG = np.empty((1024, B, S), dtype=BF16NP)
    KG = np.empty((1024, B, S), dtype=BF16NP)
    VG = np.empty((B, S, D), dtype=BF16NP)
    GG = np.empty((B, S, 16), dtype=f32)
    XN = np.empty((D, B, S), dtype=BF16NP)
    HIN = [None] * NCORE
    for c in range(NCORE):
        sl = slice(c * SC, (c + 1) * SC)
        QG[:, :, sl] = np.asarray(r1[c]["q_fm"]).reshape(1024, B, SC)
        KG[:, :, sl] = np.asarray(r1[c]["k_fm"]).reshape(1024, B, SC)
        VG[:, sl, :] = np.asarray(r1[c]["v_tm"]).reshape(B, SC, D)
        GG[:, sl, :] = np.asarray(r1[c]["gates"]).reshape(B, SC, 16)
        XN[:, :, sl] = np.asarray(r1[c]["xn_bf"]).reshape(D, B, SC)
        HIN[c] = np.asarray(r1[c]["hin_bf"])


    mh = g("L0_attn_mh_w").reshape(H, DV)
    in2 = []
    for c in range(NCORE):
        b = c // 2
        h0 = 4 * (c % 2)
        q_bh = np.ascontiguousarray(
            QG[h0 * 128:(h0 + 4) * 128, b, :])
        k_bh = np.ascontiguousarray(
            KG[h0 * 128:(h0 + 4) * 128, b, :])
        v_bh = np.ascontiguousarray(VG[b, :, h0 * 256:(h0 + 4) * 256])
        gates_px = np.ascontiguousarray(np.concatenate(
            [GG[b, :, h0:h0 + 4], GG[b, :, 8 + h0:8 + h0 + 4]], axis=1))
        mhc = np.ascontiguousarray(
            mh[h0:h0 + 4].reshape(4, 2, 128).transpose(2, 0, 1).reshape(128, 8))
        bi = g("L0_attn_b_ig")[h0:h0 + 4]
        bf = g("L0_attn_b_fg")[h0:h0 + 4]
        gbias = np.ascontiguousarray(np.broadcast_to(
            np.concatenate([bi, bf])[None, :], (128, 8)))
        in2.append({"q_bh": q_bh, "k_bh": k_bh, "v_bh": v_bh,
                    "gates_px": gates_px, "mh": mhc, "gbias": gbias})
    r2 = _run(build_l2(), in2)

    HG = np.empty((D, B, S), dtype=BF16NP)
    for c in range(NCORE):
        b = c // 2
        h0 = 4 * (c % 2)
        HG[h0 * 256:(h0 + 4) * 256, b, :] = np.asarray(r2[c]["h_fm"])

    n2c = np.ascontiguousarray(
        g("L0_attn_norm2_w").reshape(NCH, 128).T)
    in3 = []
    for c in range(NCORE):
        sl = slice(c * SC, (c + 1) * SC)
        in3.append({
            "xn_bf": np.ascontiguousarray(XN[:, :, sl].reshape(D, TOK)),
            "hin_bf": HIN[c],
            "h_fm": np.ascontiguousarray(HG[:, :, sl].reshape(D, TOK)),
            "w_og": g("L0_attn_w_og"), "w_out": g("L0_attn_w_out"),
            "n2w": n2c, "w_gate": g("L0_attn_w_gate"),
            "w_up": g("L0_attn_w_up"), "w_down": g("L0_attn_w_down")})
    r3 = _run(build_l3(), in3)

    E, R = NEXP, RLORA
    Ag, Au = g("L0_ffn_Ag"), g("L0_ffn_Au")
    Bg, Bu = g("L0_ffn_Bg"), g("L0_ffn_Bu")
    Ad, Bd = g("L0_ffn_Ad"), g("L0_ffn_Bd")
    sel = np.zeros((16, 8 * 128), np.float32)
    for e in range(8):
        for i in range(16):
            sel[i, e * 128 + e * 16 + i] = 1.0
    rep = np.zeros((8, 128), np.float32)
    for e in range(8):
        rep[e, e * 16:(e + 1) * 16] = MOE_SCALING
    cm = np.zeros((128, 16), np.float32)
    for e in range(8):
        cm[e * 16:(e + 1) * 16, e] = 1.0
    cm[:, 8] = cm[:, 0]
    for e in range(1, 8):
        cm[:, 8 + e] = cm[:, e] - cm[:, e - 1]
    w4 = {"wgT": np.ascontiguousarray(g("L0_ffn_Wg").T),
          "wuT": np.ascontiguousarray(g("L0_ffn_Wu").T),
          "wdT": np.ascontiguousarray(g("L0_ffn_Wd").T),
          "wrT": np.ascontiguousarray(g("L0_ffn_Wr").T),
          "agT": np.ascontiguousarray(Ag.reshape(E * R, D).T),
          "auT": np.ascontiguousarray(Au.reshape(E * R, D).T),
          "bgT": np.ascontiguousarray(Bg.transpose(0, 2, 1).reshape(E * R, DFF)),
          "buT": np.ascontiguousarray(Bu.transpose(0, 2, 1).reshape(E * R, DFF)),
          "adT": np.ascontiguousarray(Ad.reshape(E * R, DFF).T),
          "bdT": np.ascontiguousarray(Bd.transpose(0, 2, 1).reshape(E * R, D)),
          "csel": sel, "crep": rep, "cmask": cm}
    in4 = []
    for c in range(NCORE):
        sl = slice(c * SC, (c + 1) * SC)
        d = {"xn_bf": in3[c]["xn_bf"],
             "attn_in": np.asarray(r3[c]["attn_out"]), "x_fm": xs[c]}
        d.update(w4)
        in4.append(d)
    r4 = _run(build_l4(), in4)

    sel5 = np.zeros((128, 16 * 128), np.float32)
    for r in range(16):
        sel5[:, r * 128 + r] = 128.0 ** (-0.5)
    bias5 = np.zeros((16, 4), np.float32)
    for hl in range(4):
        for l in range(4):
            for m in range(4):
                if l < m:
                    bias5[hl * 4 + l, m] = -1e30
    in5 = []
    for c in range(NCORE):
        in5.append({"xn1_bf": np.asarray(r4[c]["xn1"]),
                    "wqkvT": np.ascontiguousarray(g("L1_attn_weight_qkv").T),
                    "wo1T": np.ascontiguousarray(g("L1_attn_weight_out").T),
                    "csel5": sel5, "cbias5": bias5})
    r5 = _run(build_l5(), in5)

    perm = np.arange(FF1).reshape(FF1 // 4, 4).T.reshape(-1)
    w6 = {"w1g": np.ascontiguousarray(g("L1_ffn_w_gate")[:, perm]),
          "w1u": np.ascontiguousarray(g("L1_ffn_w_up")[:, perm]),
          "w1d": np.ascontiguousarray(g("L1_ffn_w_down")[perm, :])}
    in6 = []
    for c in range(NCORE):
        d = {"xn1_bf": np.asarray(r4[c]["xn1"]),
             "x2": np.asarray(r4[c]["x2"]), "at1": np.asarray(r5[c]["at1"])}
        d.update(w6)
        in6.append(d)
    r6 = _run(build_l6(), in6)

    return _unshard_seq([r6[c]["out_fm"] for c in range(NCORE)])


def kernel(**inputs):
    f32 = np.float32
    x = np.asarray(inputs["L0_attn_x"], dtype=f32)
    g = lambda k: np.ascontiguousarray(np.asarray(inputs[k], dtype=f32))

    xs = _shard_seq(x)
    wg16 = np.ascontiguousarray(
        np.concatenate([g("L0_attn_w_ig"), g("L0_attn_w_fg")], axis=1))
    n1c = np.ascontiguousarray(g("L0_attn_norm1_w").reshape(NCH, 128).T)
    n2c = np.ascontiguousarray(g("L0_attn_norm2_w").reshape(NCH, 128).T)
    mh = g("L0_attn_mh_w").reshape(H, DV)

    w3 = {"w_og": g("L0_attn_w_og"), "w_out": g("L0_attn_w_out"), "n2w": n2c,
          "w_gate": g("L0_attn_w_gate"), "w_up": g("L0_attn_w_up"),
          "w_down": g("L0_attn_w_down")}

    E, R = NEXP, RLORA
    Ag, Au = g("L0_ffn_Ag"), g("L0_ffn_Au")
    Bg, Bu = g("L0_ffn_Bg"), g("L0_ffn_Bu")
    Ad, Bd = g("L0_ffn_Ad"), g("L0_ffn_Bd")
    sel = np.zeros((16, 8 * 128), np.float32)
    for e in range(8):
        for i in range(16):
            sel[i, e * 128 + e * 16 + i] = 1.0
    rep = np.zeros((8, 128), np.float32)
    for e in range(8):
        rep[e, e * 16:(e + 1) * 16] = MOE_SCALING
    cm = np.zeros((128, 16), np.float32)
    for e in range(8):
        cm[e * 16:(e + 1) * 16, e] = 1.0
    cm[:, 8] = cm[:, 0]
    for e in range(1, 8):
        cm[:, 8 + e] = cm[:, e] - cm[:, e - 1]
    w4 = {"wgT": np.ascontiguousarray(g("L0_ffn_Wg").T),
          "wuT": np.ascontiguousarray(g("L0_ffn_Wu").T),
          "wdT": np.ascontiguousarray(g("L0_ffn_Wd").T),
          "wrT": np.ascontiguousarray(g("L0_ffn_Wr").T),
          "agT": np.ascontiguousarray(Ag.reshape(E * R, D).T),
          "auT": np.ascontiguousarray(Au.reshape(E * R, D).T),
          "bgT": np.ascontiguousarray(Bg.transpose(0, 2, 1).reshape(E * R, DFF)),
          "buT": np.ascontiguousarray(Bu.transpose(0, 2, 1).reshape(E * R, DFF)),
          "adT": np.ascontiguousarray(Ad.reshape(E * R, DFF).T),
          "bdT": np.ascontiguousarray(Bd.transpose(0, 2, 1).reshape(E * R, D)),
          "csel": sel, "crep": rep, "cmask": cm}

    sel5 = np.zeros((128, 16 * 128), np.float32)
    for r in range(16):
        sel5[:, r * 128 + r] = 128.0 ** (-0.5)
    bias5 = np.zeros((16, 4), np.float32)
    for hl in range(4):
        for l in range(4):
            for m in range(4):
                if l < m:
                    bias5[hl * 4 + l, m] = -1e30
    w5 = {"wqkvT": np.ascontiguousarray(g("L1_attn_weight_qkv").T),
          "wo1T": np.ascontiguousarray(g("L1_attn_weight_out").T),
          "csel5": sel5, "cbias5": bias5}

    perm = np.arange(FF1).reshape(FF1 // 4, 4).T.reshape(-1)
    w6 = {"w1g": np.ascontiguousarray(g("L1_ffn_w_gate")[:, perm]),
          "w1u": np.ascontiguousarray(g("L1_ffn_w_up")[:, perm]),
          "w1d": np.ascontiguousarray(g("L1_ffn_w_down")[perm, :])}

    in_maps = []
    for c in range(NCORE):
        h0 = 4 * (c % 2)
        bi = g("L0_attn_b_ig")[h0:h0 + 4]
        bf = g("L0_attn_b_fg")[h0:h0 + 4]
        gbias = np.ascontiguousarray(np.broadcast_to(
            np.concatenate([bi, bf])[None, :], (128, 8)))
        mhc = np.ascontiguousarray(
            mh[h0:h0 + 4].reshape(4, 2, 128).transpose(2, 0, 1).reshape(128, 8))
        d = {"x_fm": xs[c], "wq": g("L0_attn_wq"), "wk": g("L0_attn_wk"),
             "wv": g("L0_attn_wv"), "wg16": wg16, "n1w": n1c,
             "gbias": gbias, "mh": mhc}
        d.update(w3)
        d.update(w4)
        d.update(w5)
        d.update(w6)
        in_maps.append(d)
    r = _run(build_fused(), in_maps)
    return _unshard_seq([r[c]["out_fm"] for c in range(NCORE)])
